# version 13
# baseline (speedup 1.0000x reference)
import math
from contextlib import ExitStack

import numpy as np
import ml_dtypes

import concourse.bass as bass
import concourse.mybir as mybir
from concourse.bass_utils import run_bass_kernel_spmd
import concourse.bass_utils as _bass_utils

_orig_run_command = _bass_utils.run_command

def _run_command_ldwopt(cmd, *a, **kw):
    cmd = ["--enable-ldw-opt=true" if c == "--enable-ldw-opt=false" else c
           for c in cmd]
    return _orig_run_command(cmd, *a, **kw)

_bass_utils.run_command = _run_command_ldwopt

F32 = mybir.dt.float32
BF16 = mybir.dt.bfloat16
AF = mybir.ActivationFunctionType
ALU = mybir.AluOpType
AX = mybir.AxisListType
BF = ml_dtypes.bfloat16

NCORES = 8
T, E, H, G, D = 8192, 2048, 16, 4, 128
HID, LAT, NEXP = 1024, 512, 16
TOWN = T // NCORES
NTILE = T // 128
JT = 8
ECH = E // 128
SCALE_ATT = 1.0 / math.sqrt(D)
NEG = -1.0e30
EPS = 1e-5
RSCALE = 2.5

DEBUG = 0
PHASES = 99
NEXP_RUN = NEXP
PIPE_P1 = 1


class Ctr:
    def __init__(self, name, step=1):
        self.name, self.step, self.val, self.sem = name, step, 0, None

    def bump(self):
        self.val += self.step
        return self.val

    def snap(self):
        return (self, self.val)


class Prog:
    ORDER = ("sp", "gp", "te", "act", "dve")
    QNAME = {"sp": "sync", "gp": "gpsimd", "te": "tensor",
             "act": "scalar", "dve": "vector"}

    def __init__(self):
        self.ops = []

    def op(self, eng, fn, waits=(), inc=()):
        ws = []
        for w in waits:
            if w is None:
                continue
            ws.append((w, w.val) if isinstance(w, Ctr) else w)
        incs = [(c, c.bump()) for c in inc]
        self.ops.append((eng, fn, ws, incs))

    def barrier(self, ctrs):
        snap = [c.snap() for c in ctrs]
        for eng in self.ORDER:
            self.ops.append((eng, None, snap, []))

    def replay(self, block):
        for name in self.ORDER:
            mine = [(fn, ws, incs) for (e, fn, ws, incs) in self.ops
                    if e == name]
            if not mine:
                continue

            def section(eng, mine=mine, name=name):
                dedupe = name in ()
                hwm = {}
                for fn, ws, incs in mine:
                    for c, v in ws:
                        if v > 0 and (not dedupe
                                      or hwm.get(id(c.sem), 0) < v):
                            eng.wait_ge(c.sem, v)
                            hwm[id(c.sem)] = v
                    if fn is None:
                        continue
                    inst = fn(eng)
                    for c, _ in incs:
                        inst = inst.then_inc(c.sem, c.step)

            getattr(block, self.QNAME[name])(section)


def _stage_inputs(ins):
    x = np.ascontiguousarray(np.asarray(ins["L0_attn_x"]).reshape(T, E))
    asnp = lambda a: np.ascontiguousarray(np.asarray(a))
    base = {
        "x_full": x,
        "xT_full": asnp(x.T),
        "identf": np.eye(128, dtype=np.float32),
        "sel8": np.ascontiguousarray(
            np.eye(8, dtype=np.float32)[:, :, None]
            * np.ones((1, 1, 128), np.float32)),
        "wq": asnp(ins["L0_attn_Wq"]), "wk": asnp(ins["L0_attn_Wk"]),
        "wv": asnp(ins["L0_attn_Wv"]),
        "bqT": asnp(np.asarray(ins["L0_attn_bq"]).T),
        "bkT": asnp(np.asarray(ins["L0_attn_bk"]).T),
        "bvT": asnp(np.asarray(ins["L0_attn_bv"]).T),
        "sw1T": asnp(np.asarray(ins["L0_ffn_sw1"]).T),
        "swgT": asnp(np.asarray(ins["L0_ffn_swg"]).T),
        "sw2T": asnp(np.asarray(ins["L0_ffn_sw2"]).T),
        "wdT": asnp(np.asarray(ins["L0_ffn_Wdown"]).T),
        "wgT": asnp(np.asarray(ins["L0_ffn_Wg"]).T),
        "wupT": asnp(np.asarray(ins["L0_ffn_Wup"]).T),
        "rw1T": asnp(np.asarray(ins["L0_ffn_rw1"]).transpose(0, 2, 1)),
        "rwgT": asnp(np.asarray(ins["L0_ffn_rwg"]).transpose(0, 2, 1)),
        "rw2T": asnp(np.asarray(ins["L0_ffn_rw2"]).transpose(0, 2, 1)),
        "btrow": asnp(np.tile(np.asarray(ins["L0_ffn_biases"]), JT)[None, :]),
        "ident": np.eye(128, dtype=BF),
        "tri": np.where(np.arange(128)[:, None] > np.arange(128)[None, :],
                        np.float32(NEG), np.float32(0.0)).astype(np.float32),
        "eps": np.full((128, 1), EPS, np.float32),
        "onesb": np.ones((1, 128), BF),
        "onesf": np.ones((1, 128), np.float32),
    }
    xt = x.reshape(NTILE, 128, E)
    maps = []
    for c in range(NCORES):
        m = dict(base)
        m["x_own"] = np.ascontiguousarray(xt[c::8].reshape(TOWN, E))
        db = np.zeros((128, JT), np.float32)
        db[:, c:] = NEG
        m["dbias"] = db
        maps.append(m)
    return maps


def _assemble(results):
    out = np.empty((T, E), np.float32)
    ot = out.reshape(NTILE, 128, E)
    for c in range(NCORES):
        ot[c::8] = np.asarray(results[c]["out"]).reshape(JT, 128, E)
    return out.reshape(1, T, E)


class Builder:
    def __init__(self):
        self.nc = bass.Bass("TRN2", target_bir_lowering=False)
        self.P = Prog()
        self.c_sp = Ctr("c_sp", 16)
        self.c_gp = Ctr("c_gp", 16)
        self.c_te = Ctr("c_te", 1)
        self.c_act = Ctr("c_act", 1)
        self.c_dve = Ctr("c_dve", 1)
        self.ctrs = [self.c_sp, self.c_gp, self.c_te, self.c_act, self.c_dve]

    def dram_in(self, name, shape, dtype=F32):
        return self.nc.dram_tensor(name, shape, dtype, kind="ExternalInput").ap()

    def dram_out(self, name, shape, dtype=F32):
        return self.nc.dram_tensor(name, shape, dtype, kind="ExternalOutput").ap()

    def declare(self):
        di = self.dram_in
        self.x_full = di("x_full", [T, E])
        self.x_own = di("x_own", [TOWN, E])
        self.xT_full = di("xT_full", [E, T])
        self.identf_in = di("identf", [128, 128])
        self.sel8_in = di("sel8", [8, 8, 128])
        self.wq_in = di("wq", [H, E, D])
        self.wk_in = di("wk", [G, E, D])
        self.wv_in = di("wv", [G, E, D])
        self.bqT_in = di("bqT", [D, H])
        self.bkT_in = di("bkT", [D, G])
        self.bvT_in = di("bvT", [D, G])
        self.sw1T_in = di("sw1T", [E, HID])
        self.swgT_in = di("swgT", [E, HID])
        self.sw2T_in = di("sw2T", [HID, E])
        self.wdT_in = di("wdT", [E, LAT])
        self.wgT_in = di("wgT", [E, NEXP])
        self.wupT_in = di("wupT", [LAT, E])
        self.rw1T_in = di("rw1T", [NEXP, LAT, HID])
        self.rwgT_in = di("rwgT", [NEXP, LAT, HID])
        self.rw2T_in = di("rw2T", [NEXP, HID, LAT])
        self.btrow_in = di("btrow", [1, JT * NEXP])
        self.ident_in = di("ident", [128, 128], BF16)
        self.tri_in = di("tri", [128, 128])
        self.eps_in = di("eps", [128, 1])
        self.onesb_in = di("onesb", [1, 128], BF16)
        self.onesf_in = di("onesf", [1, 128])
        self.dbias_in = di("dbias", [128, JT])
        self.out = self.dram_out("out", [TOWN, E])
        self.k_hbm = self.nc.dram_tensor("k_hbm", [G, D, T], BF16).ap()
        self.v_hbm = self.nc.dram_tensor("v_hbm", [G, D, T], BF16).ap()
        self.s_hbm = self.nc.dram_tensor("s_hbm", [TOWN, E], F32).ap()

    def sb(self, es, name, shape, dtype):
        return es.enter_context(self.nc.sbuf_tensor(name, shape, dtype))

    def ps(self, es, name, shape, dtype=F32):
        return es.enter_context(self.nc.psum_tensor(name, shape, dtype))

    def bar(self):
        self.P.barrier(self.ctrs)

    def ln_stats(self, bufs, src_dma_fn, st):
        P = self.P
        c_sp, c_act, c_dve = self.c_sp, self.c_act, self.c_dve
        xt, sqs, stats = bufs["xt"], bufs["sqs"], bufs["stats"]
        msq, var4 = bufs["msq"], bufs["var4"]
        P.op("sp", src_dma_fn, waits=[st.get("xt_free")], inc=[c_sp])
        xin = c_sp.snap()
        for t in range(4):
            P.op("dve", lambda e, t=t, xt=xt, stats=stats: e.reduce_sum(
                stats[:, t:t + 1], xt[:, t * E:(t + 1) * E], axis=AX.X),
                waits=[xin, st.get("stats_free")], inc=[c_dve])
            P.op("act", lambda e, t=t, xt=xt, sqs=sqs, stats=stats:
                 e.activation(
                     sqs[:, 0:E], xt[:, t * E:(t + 1) * E], AF.Square,
                     accum_out=stats[:, 4 + t:5 + t]),
                 waits=[xin, st.get("stats_free")], inc=[c_act])
        sums_done = c_dve.snap()
        sqs_done = c_act.snap()
        P.op("dve", lambda e, stats=stats: e.tensor_scalar(
            stats[:, 8:12], stats[:, 0:4], 1.0 / E, None, ALU.mult),
            waits=[sums_done], inc=[c_dve])
        P.op("dve", lambda e, stats=stats, msq=msq: e.tensor_tensor(
            msq[:, :], stats[:, 8:12], stats[:, 8:12], ALU.mult),
            waits=[c_dve], inc=[c_dve])
        P.op("dve", lambda e, stats=stats, msq=msq, var4=var4:
             e.scalar_tensor_tensor(
                 var4[:, :], stats[:, 4:8], 1.0 / E, msq[:, :],
                 ALU.mult, ALU.subtract),
             waits=[sqs_done, c_dve], inc=[c_dve])
        P.op("act", lambda e, var4=var4: e.activation(
            var4[:, :], var4[:, :], AF.Ln, bias=self.epsb[:, 0:1]),
            waits=[c_dve], inc=[c_act])
        P.op("act", lambda e, var4=var4, stats=stats: e.activation(
            stats[:, 12:16], var4[:, :], AF.Exp, scale=-0.5),
            waits=[c_act], inc=[c_act])
        rstd_done = c_act.snap()
        P.op("dve", lambda e, stats=stats: e.scalar_tensor_tensor(
            stats[:, 16:20], stats[:, 8:12], -1.0, stats[:, 12:16],
            ALU.mult, ALU.mult),
            waits=[rstd_done], inc=[c_dve])
        st["xt_free"] = self.c_dve.snap()
        return self.c_dve.snap()

    def ln_chunk(self, bufs, src_dma_fn, xnT_write_fn, st):
        P = self.P
        c_sp, c_te, c_act, c_dve = self.c_sp, self.c_te, self.c_act, self.c_dve
        xt, sqs, stats = bufs["xt"], bufs["sqs"], bufs["stats"]
        msq, var4, xnb, tp_ps = (bufs["msq"], bufs["var4"], bufs["xnb"],
                                 bufs["tp_ps"])
        P.op("sp", src_dma_fn, waits=[st.get("xt_free")], inc=[c_sp])
        xin = c_sp.snap()
        for t in range(4):
            P.op("dve", lambda e, t=t: e.reduce_sum(
                stats[:, t:t + 1], xt[:, t * E:(t + 1) * E], axis=AX.X),
                waits=[xin, st.get("stats_free")], inc=[c_dve])
            P.op("act", lambda e, t=t: e.activation(
                sqs[:, 0:E],
                xt[:, t * E:(t + 1) * E], AF.Square,
                accum_out=stats[:, 4 + t:5 + t]),
                waits=[xin, st.get("stats_free")],
                inc=[c_act])
        sums_done = c_dve.snap()
        sqs_done = c_act.snap()
        P.op("dve", lambda e: e.tensor_scalar(
            stats[:, 8:12], stats[:, 0:4], 1.0 / E, None, ALU.mult),
            waits=[sums_done], inc=[c_dve])
        P.op("dve", lambda e: e.tensor_tensor(
            msq[:, :], stats[:, 8:12], stats[:, 8:12], ALU.mult),
            waits=[c_dve], inc=[c_dve])
        P.op("dve", lambda e: e.scalar_tensor_tensor(
            var4[:, :], stats[:, 4:8], 1.0 / E, msq[:, :],
            ALU.mult, ALU.subtract),
            waits=[sqs_done, c_dve], inc=[c_dve])
        P.op("act", lambda e: e.activation(
            var4[:, :], var4[:, :], AF.Ln, bias=self.epsb[:, 0:1]),
            waits=[c_dve], inc=[c_act])
        P.op("act", lambda e: e.activation(
            stats[:, 12:16], var4[:, :], AF.Exp, scale=-0.5),
            waits=[c_act], inc=[c_act])
        rstd_done = c_act.snap()
        for t in range(4):
            P.op("dve", lambda e, t=t: e.tensor_scalar(
                xnb[:, t * E:(t + 1) * E], xt[:, t * E:(t + 1) * E],
                stats[:, 8 + t:9 + t], stats[:, 12 + t:13 + t],
                ALU.subtract, ALU.mult),
                waits=[rstd_done], inc=[c_dve])
        xnb_done = c_dve.snap()
        st["xt_free"] = xnb_done
        st["stats_free"] = xnb_done
        tp_ps2 = bufs["tp_ps2"]
        for b8 in range(8):
            tp = tp_ps if b8 % 2 == 0 else tp_ps2
            for k in range(8):
                idx = b8 * 8 + k
                t, ech = idx // ECH, idx % ECH
                P.op("te", lambda e, t=t, ech=ech, k=k, tp=tp: e.transpose(
                    tp[:, k * 128:(k + 1) * 128],
                    xnb[:, t * E + ech * 128: t * E + (ech + 1) * 128],
                    self.identb[:, :]),
                    waits=[xnb_done, st.get(f"tp_free{b8 % 2}")],
                    inc=([c_te] if k == 7 else []))
            tp_done = c_te.snap()
            t0, ech0 = (b8 * 8) // ECH, (b8 * 8) % ECH
            P.op("dve", lambda e, b8=b8, t0=t0, ech0=ech0:
                 xnT_write_fn(e, b8, t0, ech0),
                 waits=[tp_done, st.get("xnT_free")], inc=[c_dve])
            st[f"tp_free{b8 % 2}"] = c_dve.snap()
        return c_dve.snap()

    def build(self):
        self.declare()
        with ExitStack() as outer:
            with nc_sem_stack(self.nc, self.ctrs):
                s = self.sb
                self.identb = s(outer, "identb", [128, 128], BF16)
                self.identf = s(outer, "identf_b", [128, 128], F32)
                self.sel8b = s(outer, "sel8b", [8, 8 * 128], F32)
                self.trib = s(outer, "trib", [128, 128], F32)
                self.epsb = s(outer, "epsb", [128, 1], F32)
                self.onesbb = s(outer, "onesbb", [1, 128], BF16)
                self.onesfb = s(outer, "onesfb", [1, 128], F32)
                self.onescol = s(outer, "onescol", [128, 1], BF16)
                self.dbiasb = s(outer, "dbiasb", [128, JT], F32)
                self.bqTb = s(outer, "bqTb", [128, H], F32)
                self.bkTb = s(outer, "bkTb", [128, G], F32)
                self.bvTb = s(outer, "bvTb", [128, G], F32)
                self.brep8 = s(outer, "brep8", [128, JT * NEXP], F32)
                self.wdense = s(outer, "wdense", [128, JT * NEXP], F32)
                self.kmax64 = s(outer, "kmax64", [1, G * 16], F32)

                with self.nc.Block() as block:
                    with ExitStack() as p0:
                        self.phase0(p0)
                    self.bar()
                    if PHASES >= 1:
                        with ExitStack() as p1:
                            self.phase1(p1)
                        self.bar()
                    if PHASES >= 2:
                        self.hhT = s(outer, "hhT", [128, 8 * TOWN], BF16)
                        self.xlT = s(outer, "xlT", [128, 4 * TOWN], BF16)
                        with ExitStack() as mid:
                            self.xnT_own = s(mid, "xnT_own",
                                             [128, ECH * TOWN], BF16)
                            self.q_sb = s(mid, "q_sb", [128, H * TOWN], BF16)
                            self.kown = s(mid, "kown", [128, G * TOWN], BF16)
                            self.vown = s(mid, "vown", [128, G * JT * 129],
                                          BF16)
                            with ExitStack() as p1b:
                                self.phase1b(p1b)
                            self.bar()
                            if PHASES >= 3:
                                with ExitStack() as p15:
                                    self.phase15(p15)
                                self.bar()
                            if PHASES >= 4:
                                with ExitStack() as p2:
                                    self.phase2(p2)
                                self.bar()
                        if PHASES >= 5:
                            with ExitStack() as p3:
                                self.phase3(p3)
                            self.bar()
                    self.finale()
                    self.P.replay(block)
        return self.nc

    def phase0(self, es):
        P = self.P
        c_sp, c_te, c_dve = self.c_sp, self.c_te, self.c_dve
        for dst, src in [
            (self.identb, self.ident_in), (self.trib, self.tri_in),
            (self.identf, self.identf_in),
            (self.epsb, self.eps_in), (self.onesbb, self.onesb_in), (self.onesfb, self.onesf_in),
            (self.dbiasb, self.dbias_in), (self.bqTb, self.bqT_in),
            (self.bkTb, self.bkT_in), (self.bvTb, self.bvT_in),
        ]:
            P.op("sp", lambda e, d=dst, s=src: e.dma_start(out=d[:, :], in_=s),
                 inc=[c_sp])
        btrow = self.sb(es, "btrow_s", [1, JT * NEXP], F32)
        P.op("sp", lambda e: e.dma_start(out=btrow[:, :], in_=self.btrow_in),
             inc=[c_sp])
        P.op("sp", lambda e: e.dma_start(
            out=self.sel8b[:, :].rearrange("p (c n) -> p c n", c=8),
            in_=self.sel8_in), inc=[c_sp])
        ld = c_sp.snap()
        P.op("dve", lambda e: e.memset(self.onescol[:, :], 1.0), inc=[c_dve])
        rep_ps = self.ps(es, "rep_ps", [128, JT * NEXP])
        P.op("te", lambda e: e.matmul(rep_ps[:, 0:JT * NEXP],
                                      self.onesfb[0:1, :], btrow[0:1, :],
                                      start=True, stop=True),
             waits=[ld], inc=[c_te])
        mm2 = self.c_te.snap()
        P.op("dve", lambda e: e.tensor_copy(self.brep8[:, :],
                                            rep_ps[:, 0:JT * NEXP]),
             waits=[mm2], inc=[c_dve])

    def phase1(self, es):
        P = self.P
        c_sp, c_gp, c_te, c_act, c_dve = (self.c_sp, self.c_gp, self.c_te,
                                          self.c_act, self.c_dve)
        s = self.sb
        wkb = s(es, "wkb", [128, G * ECH * 128], BF16)
        wvb = s(es, "wvb", [128, G * ECH * 128], BF16)
        bufs2 = []
        for i in (0, 1):
            bufs2.append({
                "xt": s(es, f"xt{i}", [128, 4 * E], F32),
                "sqs": s(es, f"sqs{i}", [128, E], F32),
                "stats": s(es, f"stats{i}", [128, 24], F32),
                "msq": s(es, f"msq{i}", [128, 4], F32),
                "var4": s(es, f"var4{i}", [128, 4], F32),
            })
        xTb = [s(es, f"xTb{i}", [128, ECH * 512], BF16) for i in (0, 1)]
        statT = s(es, "statT", [8, 128], F32)
        reps = [s(es, f"reps{i}", [128, 1024], F32) for i in (0, 1)]
        t1b = [s(es, f"t1b{i}", [128, 512], F32) for i in (0, 1)]
        kst = [s(es, f"kst{i}", [128, G * 512], BF16) for i in (0, 1)]
        vst = [s(es, f"vst{i}", [128, G * 512], BF16) for i in (0, 1)]
        ukv = s(es, "ukv", [128, 2 * G], F32)
        ksq1 = s(es, "ksq1", [128, 512], BF16)
        kv_ps = [self.ps(es, f"kv_ps{i}", [128, 512]) for i in range(4)]
        st_ps = self.ps(es, "st_ps", [128, 128])
        rep_ps = self.ps(es, "rep_ps1", [128, 1024])

        for gi in range(G):
            P.op("gp", lambda e, gi=gi: e.dma_start(
                out=wkb[:, gi * ECH * 128:(gi + 1) * ECH * 128]
                .rearrange("p (c d) -> p c d", c=ECH),
                in_=self.wk_in[gi].rearrange("(c p) d -> p c d", p=128)),
                inc=[c_gp])
            P.op("gp", lambda e, gi=gi: e.dma_start(
                out=wvb[:, gi * ECH * 128:(gi + 1) * ECH * 128]
                .rearrange("p (c d) -> p c d", c=ECH),
                in_=self.wv_in[gi].rearrange("(c p) d -> p c d", p=128)),
                inc=[c_gp])
        wload = c_gp.snap()

        for gi in range(G):
            for which, wb in ((0, wkb), (1, wvb)):
                for ech in range(ECH):
                    P.op("te", lambda e, gi=gi, which=which, wb=wb, ech=ech:
                         e.matmul(
                             st_ps[:, which:which + 1],
                             wb[:, (gi * ECH + ech) * 128:
                                (gi * ECH + ech + 1) * 128],
                             self.onescol[:, 0:1],
                             start=(ech == 0), stop=(ech == ECH - 1)),
                         waits=[wload],
                         inc=([c_te] if ech == ECH - 1 else []))
                mm = self.c_te.snap()
                P.op("dve", lambda e, gi=gi, which=which: e.tensor_copy(
                    ukv[:, which * G + gi:which * G + gi + 1],
                    st_ps[:, which:which + 1]),
                    waits=[mm], inc=[c_dve])
        u_done = c_dve.snap()

        st2 = [{}, {}]
        kst_free = [None, None]
        vst_free = [None, None]
        rep_free = [None, None]
        t1_free = [None, None]
        xT_free = [None, None]
        prev_vevac = [None]
        statT_free = [None]

        stt_prev = [None]
        rep_ps_free = [None]

        def rep_build(bi, stats_done):
            stats = bufs2[bi]["stats"]
            P.op("te", lambda e, stats=stats: e.transpose(
                st_ps[0:8, 0:128], stats[:, 12:20], self.identf[:, :]),
                waits=[stats_done, stt_prev[0], u_done, c_dve], inc=[c_te])
            P.op("dve", lambda e: e.tensor_copy(statT[:, :],
                                                st_ps[0:8, 0:128]),
                 waits=[c_te], inc=[c_dve])
            stt = c_dve.snap()
            stt_prev[0] = stt
            st2[bi]["stats_free"] = stt
            for c in range(8):
                dst = (c % 4) * 128 + (0 if c < 4 else 512)
                P.op("te", lambda e, c=c, dst=dst: e.matmul(
                    rep_ps[:, dst:dst + 128],
                    self.sel8b[:, c * 128:(c + 1) * 128],
                    statT[:, :], start=True, stop=True),
                    waits=[stt, rep_ps_free[0], rep_free[bi], c_dve],
                    inc=([c_te] if c == 7 else []))
            selmm = self.c_te.snap()
            P.op("dve", lambda e, bi=bi: e.tensor_copy(
                reps[bi][:, :], rep_ps[:, :]),
                waits=[selmm], inc=[c_dve])
            rep_ps_free[0] = self.c_dve.snap()
            return rep_ps_free[0]

        def kv_proj(ch, bi, rep_done):
            bank_guard = prev_vevac[0]
            for which, wb, ust, bT, dst_h, dst_s in (
                (0, wkb, 0, self.bkTb, self.k_hbm, kst),
                (1, wvb, G, self.bvTb, self.v_hbm, vst),
            ):
                for gi in range(G):
                    for ech in range(ECH):
                        P.op("te", lambda e, gi=gi, ech=ech, bi=bi, wb=wb:
                             e.matmul(
                                 kv_ps[gi][:, :],
                                 wb[:, (gi * ECH + ech) * 128:
                                    (gi * ECH + ech + 1) * 128],
                                 xTb[bi][:, ech * 512:(ech + 1) * 512],
                                 start=(ech == 0), stop=(ech == ECH - 1)),
                             waits=[st2[bi]["xT_ld"], wload, kst_free[bi],
                                    vst_free[bi], bank_guard],
                             inc=([c_te] if ech == ECH - 1 else []))
                mm_done = self.c_te.snap()
                for gi in range(G):
                    tf = t1_free[gi % 2]
                    P.op("dve", lambda e, gi=gi, bi=bi: e.tensor_tensor(
                        t1b[gi % 2][:, :], kv_ps[gi][:, :],
                        reps[bi][:, 0:512], ALU.mult),
                        waits=[mm_done, rep_done, u_done, tf], inc=[c_dve])
                    P.op("dve", lambda e, gi=gi, bi=bi, ust=ust:
                         e.scalar_tensor_tensor(
                             t1b[gi % 2][:, :], reps[bi][:, 512:1024],
                             ukv[:, ust + gi:ust + gi + 1],
                             t1b[gi % 2][:, :], ALU.mult, ALU.add),
                         waits=[c_dve], inc=[c_dve])
                    fx = c_dve.snap()
                    P.op("act", lambda e, gi=gi, bi=bi, bT=bT, dst_s=dst_s:
                         e.activation(
                             dst_s[bi][:, gi * 512:(gi + 1) * 512],
                             t1b[gi % 2][:, :], AF.Identity,
                             bias=bT[:, gi:gi + 1]),
                         waits=[fx], inc=[c_act])
                    t1_free[gi % 2] = self.c_act.snap()
                ev_done = self.c_act.snap()
                bank_guard = self.c_dve.snap()
                if which == 0:
                    for gi in range(G):
                        P.op("act", lambda e, gi=gi, bi=bi: e.activation(
                            ksq1[:, :], kst[bi][:, gi * 512:(gi + 1) * 512],
                            AF.Square),
                            waits=[ev_done, c_te], inc=[c_act])
                        P.op("te", lambda e: e.matmul(
                            rep_ps[0:1, 0:512], self.onescol[:, 0:1],
                            ksq1[:, :], start=True, stop=True),
                            waits=[c_act, c_dve], inc=[c_te])
                        P.op("dve", lambda e, gi=gi, ch=ch: e.reduce_max(
                            self.kmax64[0:1, gi * 16 + ch:gi * 16 + ch + 1],
                            rep_ps[0:1, 0:512], axis=AX.X),
                            waits=[c_te], inc=[c_dve])
                if which == 1:
                    prev_vevac[0] = bank_guard
                    xT_free[bi] = mm_done
                for gi in range(G):
                    P.op("sp", lambda e, ch=ch, gi=gi, bi=bi, dst_h=dst_h,
                         dst_s=dst_s: e.dma_start(
                             out=dst_h[gi, :, ch * 512:(ch + 1) * 512],
                             in_=dst_s[bi][:, gi * 512:(gi + 1) * 512]),
                         waits=[ev_done], inc=[c_sp])
                if which == 0:
                    kst_free[bi] = self.c_sp.snap()
                else:
                    vst_free[bi] = self.c_sp.snap()

        prev = None
        for ch in range(T // 512):
            bi = ch % 2
            bufs = bufs2[bi]
            st = st2[bi]

            def src_dma(e, ch=ch, bufs=bufs):
                return e.dma_start(
                    out=bufs["xt"][:, :].rearrange("p (t e) -> p t e", t=4),
                    in_=self.x_full[512 * ch:512 * (ch + 1), :]
                    .rearrange("(t p) e -> p t e", p=128))

            P.op("gp", lambda e, ch=ch, bi=bi: e.dma_start(
                out=xTb[bi][:, :].rearrange("p (c n) -> p c n", c=ECH),
                in_=self.xT_full[:, 512 * ch:512 * (ch + 1)]
                .rearrange("(c p) n -> p c n", p=128)),
                waits=[xT_free[bi]], inc=[c_gp])
            st2[bi]["xT_ld"] = self.c_gp.snap()

            stats_done = self.ln_stats(bufs, src_dma, st)
            if prev is not None:
                pch, pbi = prev
                kv_proj(pch, pbi, st2[pbi]["rep_done"])
            st2[bi]["rep_done"] = rep_build(bi, stats_done)
            prev = (ch, bi)
        pch, pbi = prev
        kv_proj(pch, pbi, st2[pbi]["rep_done"])

        if DEBUG:
            fin = self.c_sp.snap()
            self.dbg_khbm = self.dram_out("dbg_khbm", [G, D, T], BF16)
            self.dbg_vhbm = self.dram_out("dbg_vhbm", [G, D, T], BF16)
            P.op("sp", lambda e: e.dma_start(out=self.dbg_khbm,
                                             in_=self.k_hbm), waits=[fin],
                 inc=[c_sp])
            P.op("sp", lambda e: e.dma_start(out=self.dbg_vhbm,
                                             in_=self.v_hbm), waits=[fin],
                 inc=[c_sp])

    def phase1b(self, es):
        P = self.P
        c_sp, c_gp, c_te, c_act, c_dve = (self.c_sp, self.c_gp, self.c_te,
                                          self.c_act, self.c_dve)
        s = self.sb
        wkb = [s(es, f"wkb2{i}", [128, ECH * 128], BF16) for i in (0, 1)]
        wvb = [s(es, f"wvb2{i}", [128, ECH * 128], BF16) for i in (0, 1)]
        wqb = [s(es, f"wqb{i}", [128, ECH * 128], BF16) for i in (0, 1)]
        bufs = {
            "xt": s(es, "xtb", [128, 4 * E], F32),
            "sqs": s(es, "sqsb", [128, E], F32),
            "stats": s(es, "statsb", [128, 16], F32),
            "msq": s(es, "msqb", [128, 4], F32),
            "var4": s(es, "var4b", [128, 4], F32),
            "xnb": s(es, "xnbb", [128, 4 * E], BF16),
            "tp_ps": self.ps(es, "tp_psb", [128, 1024], BF16),
            "tp_ps2": self.ps(es, "tp_psb2", [128, 1024], BF16),
        }
        mm_ps = [self.ps(es, f"mm_psb{i}", [128, 512]) for i in range(4)]
        vtb = s(es, "vtb", [128, 1024], BF16)
        vt_free = None
        vtp_free = [None, None]

        xnT_view = self.xnT_own[:, :].rearrange("p (c n) -> p c n", c=ECH)
        st = {}
        for ch in range(TOWN // 512):
            def src_dma(e, ch=ch):
                return e.dma_start(
                    out=bufs["xt"][:, :].rearrange("p (t e) -> p t e", t=4),
                    in_=self.x_own[512 * ch:512 * (ch + 1), :]
                    .rearrange("(t p) e -> p t e", p=128))

            def xnT_wr(e, b8, t0, ech0, ch=ch):
                tp = bufs["tp_ps"] if b8 % 2 == 0 else bufs["tp_ps2"]
                return e.tensor_copy(
                    xnT_view[:, ech0:ech0 + 8,
                             ch * 512 + t0 * 128: ch * 512 + (t0 + 1) * 128],
                    tp[:, :].rearrange("p (k n) -> p k n", k=8))

            self.ln_chunk(bufs, src_dma, xnT_wr, st)
        xnT_done = self.c_dve.snap()

        wq_free = [None, None]
        ps_free = [None, None, None, None]
        for h in range(H):
            wi = h % 2
            P.op("gp", lambda e, h=h, wi=wi: e.dma_start(
                out=wqb[wi][:, :].rearrange("p (c d) -> p c d", c=ECH),
                in_=self.wq_in[h].rearrange("(c p) d -> p c d", p=128)),
                waits=[wq_free[wi]], inc=[c_gp])
            wq_ld = c_gp.snap()
            for tc in range(2):
                pi = tc
                for ech in range(ECH):
                    P.op("te", lambda e, wi=wi, ech=ech, tc=tc, pi=pi: e.matmul(
                        mm_ps[pi][:, :],
                        wqb[wi][:, ech * 128:(ech + 1) * 128],
                        xnT_view[:, ech, tc * 512:(tc + 1) * 512],
                        start=(ech == 0), stop=(ech == ECH - 1)),
                        waits=[xnT_done, wq_ld, ps_free[pi]],
                        inc=([c_te] if ech == ECH - 1 else []))
                qmm = self.c_te.snap()
                P.op("act", lambda e, h=h, tc=tc, pi=pi: e.activation(
                    self.q_sb[:, h * TOWN + tc * 512:
                              h * TOWN + (tc + 1) * 512],
                    mm_ps[pi][:, :], AF.Identity, bias=self.bqTb[:, h:h + 1]),
                    waits=[qmm], inc=[c_act])
                ps_free[pi] = self.c_act.snap()
            wq_free[wi] = self.c_act.snap()

        P.op("dve", lambda e: e.memset(
            self.vown[:, :].rearrange("p (q s) -> p q s", s=129)
            [:, :, 128:129], 1.0), inc=[c_dve])
        ones_set = self.c_dve.snap()

        kvw_free = [None, None]
        for gi in range(G):
            wi = gi % 2
            P.op("gp", lambda e, gi=gi, wi=wi: e.dma_start(
                out=wkb[wi][:, :].rearrange("p (c d) -> p c d", c=ECH),
                in_=self.wk_in[gi].rearrange("(c p) d -> p c d", p=128)),
                waits=[kvw_free[wi]], inc=[c_gp])
            P.op("gp", lambda e, gi=gi, wi=wi: e.dma_start(
                out=wvb[wi][:, :].rearrange("p (c d) -> p c d", c=ECH),
                in_=self.wv_in[gi].rearrange("(c p) d -> p c d", p=128)),
                waits=[kvw_free[wi]], inc=[c_gp])
            kv_wload = c_gp.snap()
            for tc in range(2):
                pi = 2 + tc
                for ech in range(ECH):
                    P.op("te", lambda e, wi=wi, tc=tc, ech=ech, pi=pi: e.matmul(
                        mm_ps[pi][:, :],
                        wkb[wi][:, ech * 128:(ech + 1) * 128],
                        xnT_view[:, ech, tc * 512:(tc + 1) * 512],
                        start=(ech == 0), stop=(ech == ECH - 1)),
                        waits=[xnT_done, kv_wload, ps_free[pi]],
                        inc=([c_te] if ech == ECH - 1 else []))
                kmm = self.c_te.snap()
                P.op("act", lambda e, gi=gi, tc=tc, pi=pi: e.activation(
                    self.kown[:, gi * TOWN + tc * 512:
                              gi * TOWN + (tc + 1) * 512],
                    mm_ps[pi][:, :], AF.Identity, bias=self.bkTb[:, gi:gi + 1]),
                    waits=[kmm], inc=[c_act])
                ps_free[pi] = self.c_act.snap()
            for tc in range(2):
                pi = 2 + tc
                for ech in range(ECH):
                    P.op("te", lambda e, wi=wi, tc=tc, ech=ech, pi=pi: e.matmul(
                        mm_ps[pi][:, :],
                        wvb[wi][:, ech * 128:(ech + 1) * 128],
                        xnT_view[:, ech, tc * 512:(tc + 1) * 512],
                        start=(ech == 0), stop=(ech == ECH - 1)),
                        waits=[xnT_done, kv_wload, ps_free[pi]],
                        inc=([c_te] if ech == ECH - 1 else []))
                vmm = self.c_te.snap()
                P.op("act", lambda e, gi=gi, tc=tc, pi=pi: e.activation(
                    vtb[:, tc * 512:(tc + 1) * 512], mm_ps[pi][:, :],
                    AF.Identity, bias=self.bvTb[:, gi:gi + 1]),
                    waits=[vmm, vt_free], inc=[c_act])
                ps_free[pi] = self.c_act.snap()
            vt_done = self.c_act.snap()
            for half in range(2):
                tp = bufs["tp_ps"] if half == 0 else bufs["tp_ps2"]
                for k in range(4):
                    t = half * 4 + k
                    P.op("te", lambda e, t=t, k=k, tp=tp: e.transpose(
                        tp[:, k * 128:(k + 1) * 128],
                        vtb[:, t * 128:(t + 1) * 128], self.identb[:, :]),
                        waits=[vt_done, vtp_free[half]],
                        inc=([c_te] if k == 3 else []))
                tdone = self.c_te.snap()
                P.op("dve", lambda e, gi=gi, half=half, tp=tp: e.tensor_copy(
                    self.vown[:, :].rearrange("p (q s) -> p q s", s=129)
                    [:, gi * JT + half * 4: gi * JT + (half + 1) * 4, 0:128],
                    tp[:, 0:512].rearrange("p (k n) -> p k n", k=4)),
                    waits=[tdone, ones_set], inc=[c_dve])
                vtp_free[half] = self.c_dve.snap()
            vt_free = self.c_dve.snap()
            kvw_free[wi] = self.c_dve.snap()

        if DEBUG:
            fin = [self.c_act.snap(), self.c_dve.snap()]
            self.dbg_xnt = self.dram_out("dbg_xnt", [128, ECH * TOWN], BF16)
            self.dbg_q = self.dram_out("dbg_q", [128, H * TOWN], BF16)
            self.dbg_kown = self.dram_out("dbg_kown", [128, G * TOWN], BF16)
            self.dbg_vown = self.dram_out("dbg_vown", [128, G * JT * 129], BF16)
            for dst, src in [(self.dbg_xnt, self.xnT_own),
                             (self.dbg_q, self.q_sb),
                             (self.dbg_kown, self.kown),
                             (self.dbg_vown, self.vown)]:
                P.op("sp", lambda e, d=dst, s=src: e.dma_start(out=d,
                                                               in_=s[:, :]),
                     waits=fin, inc=[c_sp])

    def phase15(self, es):
        P = self.P
        c_sp, c_gp, c_te, c_act, c_dve = (self.c_sp, self.c_gp, self.c_te,
                                          self.c_act, self.c_dve)
        s = self.sb
        sw1b = s(es, "sw1b", [128, ECH * HID], BF16)
        swgb = s(es, "swgb", [128, ECH * HID], BF16)
        wdb = s(es, "wdb", [128, ECH * LAT], BF16)
        wgb = s(es, "wgb", [128, ECH * NEXP], BF16)
        prob = s(es, "prob", [128, JT * NEXP], F32)
        vbuf = s(es, "vbuf_g", [128, JT * NEXP], F32)
        cnt = [s(es, f"cnt{i}", [128, JT * NEXP], F32) for i in (0, 1)]
        rbuf = s(es, "rbuf", [128, HID], BF16)
        rbuf2 = s(es, "rbuf2", [128, HID], BF16)
        rsum = s(es, "rsum", [128, JT], F32)
        z_ps = self.ps(es, "z_ps", [128, 512])
        h_ps = self.ps(es, "h_ps", [128, 1024])
        g_ps = self.ps(es, "g_ps", [128, 1024])

        xnT_view = self.xnT_own[:, :].rearrange("p (c n) -> p c n", c=ECH)

        P.op("gp", lambda e: e.dma_start(
            out=sw1b[:, :].rearrange("p (c n) -> p c n", c=ECH),
            in_=self.sw1T_in.rearrange("(c p) n -> p c n", p=128)),
            inc=[c_gp])
        P.op("gp", lambda e: e.dma_start(
            out=swgb[:, :].rearrange("p (c n) -> p c n", c=ECH),
            in_=self.swgT_in.rearrange("(c p) n -> p c n", p=128)),
            inc=[c_gp])
        P.op("gp", lambda e: e.dma_start(
            out=wdb[:, :].rearrange("p (c n) -> p c n", c=ECH),
            in_=self.wdT_in.rearrange("(c p) n -> p c n", p=128)),
            inc=[c_gp])
        P.op("gp", lambda e: e.dma_start(
            out=wgb[:, :].rearrange("p (c n) -> p c n", c=ECH),
            in_=self.wgT_in.rearrange("(c p) n -> p c n", p=128)),
            inc=[c_gp])
        wload = c_gp.snap()

        for t in range(JT):
            for ech in range(ECH):
                P.op("te", lambda e, t=t, ech=ech: e.matmul(
                    z_ps[:, t * NEXP:(t + 1) * NEXP],
                    xnT_view[:, ech, t * 128:(t + 1) * 128],
                    wgb[:, ech * NEXP:(ech + 1) * NEXP],
                    start=(ech == 0), stop=(ech == ECH - 1)),
                    waits=[wload],
                    inc=([c_te] if ech == ECH - 1 and t == JT - 1 else []))
        zmm = self.c_te.snap()
        P.op("act", lambda e: e.activation(
            prob[:, :], z_ps[:, 0:JT * NEXP], AF.Exp, scale=-1.0),
            waits=[zmm], inc=[c_act])
        P.op("dve", lambda e: e.tensor_scalar(
            prob[:, :], prob[:, :], 1.0, None, ALU.add),
            waits=[c_act], inc=[c_dve])
        P.op("dve", lambda e: e.reciprocal(prob[:, :], prob[:, :]),
            waits=[c_dve], inc=[c_dve])
        P.op("dve", lambda e: e.tensor_tensor(
            vbuf[:, :], prob[:, :], self.brep8[:, :], ALU.add),
            waits=[c_dve], inc=[c_dve])
        P.op("dve", lambda e: e.memset(cnt[0][:, :], 0.0),
            waits=[c_dve], inc=[c_dve])
        cur = 0
        for ep in range(NEXP):
            for t in range(JT):
                P.op("dve", lambda e, ep=ep, t=t, cur=cur: e.scalar_tensor_tensor(
                    cnt[1 - cur][:, t * NEXP:(t + 1) * NEXP],
                    vbuf[:, t * NEXP:(t + 1) * NEXP],
                    vbuf[:, t * NEXP + ep:t * NEXP + ep + 1],
                    cnt[cur][:, t * NEXP:(t + 1) * NEXP],
                    ALU.is_lt, ALU.add),
                    waits=[c_dve], inc=[c_dve])
            cur = 1 - cur
        P.op("dve", lambda e, cur=cur: e.tensor_scalar(
            cnt[1 - cur][:, :], cnt[cur][:, :], 7.5, None, ALU.is_le),
            waits=[c_dve], inc=[c_dve])
        P.op("dve", lambda e, cur=cur: e.tensor_tensor(
            self.wdense[:, :], prob[:, :], cnt[1 - cur][:, :], ALU.mult),
            waits=[c_dve], inc=[c_dve])
        P.op("dve", lambda e: e.reduce_sum(
            rsum[:, :].rearrange("p (j o) -> p j o", o=1),
            self.wdense[:, :].rearrange("p (j n) -> p j n", j=JT), axis=AX.X),
            waits=[c_dve], inc=[c_dve])
        P.op("dve", lambda e: e.reciprocal(rsum[:, :], rsum[:, :]),
            waits=[c_dve], inc=[c_dve])
        for t in range(JT):
            P.op("dve", lambda e, t=t: e.tensor_scalar(
                self.wdense[:, t * NEXP:(t + 1) * NEXP],
                self.wdense[:, t * NEXP:(t + 1) * NEXP],
                rsum[:, t:t + 1], RSCALE, ALU.mult, ALU.mult),
                waits=[c_dve], inc=[c_dve])

        r_free = None
        for hch in range(8):
            for tc in range(2):
                for ech in range(ECH):
                    P.op("te", lambda e, hch=hch, tc=tc, ech=ech: e.matmul(
                        h_ps[:, tc * 512:(tc + 1) * 512],
                        sw1b[:, ech * HID + hch * 128:
                             ech * HID + (hch + 1) * 128],
                        xnT_view[:, ech, tc * 512:(tc + 1) * 512],
                        start=(ech == 0), stop=(ech == ECH - 1)),
                        waits=[wload, c_dve], inc=[])
                    P.op("te", lambda e, hch=hch, tc=tc, ech=ech: e.matmul(
                        g_ps[:, tc * 512:(tc + 1) * 512],
                        swgb[:, ech * HID + hch * 128:
                             ech * HID + (hch + 1) * 128],
                        xnT_view[:, ech, tc * 512:(tc + 1) * 512],
                        start=(ech == 0), stop=(ech == ECH - 1)),
                        waits=[wload, c_dve],
                        inc=([c_te] if ech == ECH - 1 and tc == 1 else []))
            hmm = self.c_te.snap()
            P.op("act", lambda e: e.activation(
                rbuf[:, :], g_ps[:, :], AF.Relu),
                waits=[hmm, r_free], inc=[c_act])
            P.op("dve", lambda e: e.tensor_tensor(
                rbuf2[:, :], rbuf[:, :], rbuf[:, :], ALU.mult),
                waits=[c_act], inc=[c_dve])
            P.op("dve", lambda e, hch=hch: e.tensor_tensor(
                self.hhT[:, hch * TOWN:(hch + 1) * TOWN],
                h_ps[:, :], rbuf2[:, :], ALU.mult),
                waits=[c_dve], inc=[c_dve])
            r_free = self.c_dve.snap()

        for lch in range(4):
            for tc in range(2):
                for ech in range(ECH):
                    P.op("te", lambda e, lch=lch, tc=tc, ech=ech: e.matmul(
                        h_ps[:, tc * 512:(tc + 1) * 512],
                        wdb[:, ech * LAT + lch * 128:
                            ech * LAT + (lch + 1) * 128],
                        xnT_view[:, ech, tc * 512:(tc + 1) * 512],
                        start=(ech == 0), stop=(ech == ECH - 1)),
                        waits=[wload, c_act, c_dve],
                        inc=([c_te] if ech == ECH - 1 and tc == 1 else []))
            xmm = self.c_te.snap()
            P.op("act", lambda e, lch=lch: e.activation(
                self.xlT[:, lch * TOWN:(lch + 1) * TOWN], h_ps[:, :],
                AF.Identity),
                waits=[xmm], inc=[c_act])

        if DEBUG:
            fin = [self.c_act.snap(), self.c_dve.snap()]
            self.dbg_w = self.dram_out("dbg_w", [128, JT * NEXP], F32)
            self.dbg_hh = self.dram_out("dbg_hh", [128, 8 * TOWN], BF16)
            self.dbg_xl = self.dram_out("dbg_xl", [128, 4 * TOWN], BF16)
            for dst, src in [(self.dbg_w, self.wdense), (self.dbg_hh, self.hhT),
                             (self.dbg_xl, self.xlT)]:
                P.op("sp", lambda e, d=dst, s=src: e.dma_start(out=d,
                                                               in_=s[:, :]),
                     waits=fin, inc=[c_sp])

    def phase2(self, es):
        P = self.P
        c_sp, c_gp, c_te, c_act, c_dve = (self.c_sp, self.c_gp, self.c_te,
                                          self.c_act, self.c_dve)
        s = self.sb
        NS = 2
        kbuf = [s(es, f"kbuf{i}", [128, T], BF16) for i in (0, 1)]
        vbuf = [s(es, f"vbuf{i}", [128, 64 * 144], BF16) for i in (0, 1)]
        pt = [s(es, f"pt{i}", [128, 1024], BF16) for i in range(NS)]
        sst = s(es, "sst", [128, JT * 128], F32)
        qsq = s(es, "qsqb", [128, 4 * TOWN], BF16)
        ksq = s(es, "ksqb", [128, 512], BF16)
        kmaxb = s(es, "kmaxb", [1, 20], F32)
        cbuf = s(es, "cbuf", [1, 16], F32)
        crep4 = s(es, "crep4", [128, 4], F32)
        dbadj4 = s(es, "dbadj4", [128, 4 * JT], F32)
        rcpb = s(es, "rcpb", [128, JT], F32)
        zbuf = s(es, "zbuf", [128, 512], BF16)
        s_ps = [self.ps(es, f"s_ps{i}", [128, 1024]) for i in range(NS)]
        o_ps = [self.ps(es, f"o_ps{i}", [128, 512]) for i in range(3)]
        cp_ps = self.ps(es, "cp_ps", [128, 512])

        def o_tile(j):
            return o_ps[j // 3][:, (j % 3) * 129:(j % 3) * 129 + 129]

        for i in (0, 1):
            P.op("dve", lambda e, i=i: e.memset(
                vbuf[i][:, :].rearrange("p (q s) -> p q s", s=144)
                [:, :, 128:129], 1.0), inc=[c_dve])
        P.op("dve", lambda e: e.memset(zbuf[:, :], 0.0), inc=[c_dve])
        ones_set = c_dve.snap()

        exp_done = [None] * NS
        av_done = [None] * NS
        head_eplg = None
        sst_free = None
        group_done = [None, None]
        slot = 0

        for gi in range(G):
            bi = gi % 2
            P.op("gp", lambda e, gi=gi, bi=bi: e.dma_start(
                out=kbuf[bi][:, :], in_=self.k_hbm[gi]),
                waits=[group_done[bi]], inc=[c_gp])
            for qt in range(64):
                P.op("sp", lambda e, gi=gi, bi=bi, qt=qt: e.dma_start_transpose(
                    vbuf[bi][:, qt * 144:qt * 144 + 128],
                    self.v_hbm[gi, :, qt * 128:(qt + 1) * 128]),
                    waits=([group_done[bi], ones_set] if qt == 0 else []),
                    inc=[c_sp])
            ready_sp = self.c_sp.snap()
            ready_gp = self.c_gp.snap()
            ready = ready_sp

            guard = [w for w in exp_done if w] + [ready_sp, ready_gp]
            P.op("dve", lambda e, gi=gi: e.reduce_max(
                kmaxb[0:1, 0:1], self.kmax64[0:1, gi * 16:(gi + 1) * 16],
                axis=AX.X), waits=[c_dve], inc=[c_dve])
            P.op("act", lambda e: e.activation(
                kmaxb[0:1, 1:2], kmaxb[0:1, 0:1], AF.Ln),
                waits=[c_dve], inc=[c_act])
            lnk_done = c_act.snap()

            P.op("act", lambda e, gi=gi: e.activation(
                qsq[:, :], self.q_sb[:, gi * 4 * TOWN:(gi + 1) * 4 * TOWN],
                AF.Square), waits=[self.c_dve.snap()], inc=[c_act])
            qsq_done = c_act.snap()
            qprev = [None, None]
            for i in range(8):
                sl = i % 2
                P.op("te", lambda e, i=i, sl=sl: e.matmul(
                    s_ps[1][0:1, sl * 512:(sl + 1) * 512],
                    self.onescol[:, 0:1],
                    qsq[:, i * 512:(i + 1) * 512],
                    start=True, stop=True),
                    waits=guard + [qsq_done, qprev[sl]], inc=[c_te])
                mm = self.c_te.snap()
                P.op("dve", lambda e, i=i, sl=sl: e.reduce_max(
                    cbuf[0:1, i:i + 1],
                    s_ps[1][0:1, sl * 512:(sl + 1) * 512], axis=AX.X),
                    waits=[mm], inc=[c_dve])
                qprev[sl] = c_dve.snap()
            P.op("dve", lambda e: e.reduce_max(
                cbuf[0:1, 8:12].rearrange("o (h x) -> o h x", h=4),
                cbuf[0:1, 0:8].rearrange("o (h x) -> o h x", h=4),
                axis=AX.X),
                waits=[qprev[0], qprev[1]], inc=[c_dve])
            P.op("act", lambda e: e.activation(
                cbuf[0:1, 12:16], cbuf[0:1, 8:12], AF.Ln),
                waits=[c_dve], inc=[c_act])
            P.op("dve", lambda e: e.tensor_scalar(
                cbuf[0:1, 8:12], cbuf[0:1, 12:16], kmaxb[0:1, 1:2], None,
                ALU.add), waits=[c_act, lnk_done], inc=[c_dve])
            P.op("act", lambda e: e.activation(
                cbuf[0:1, 12:16], cbuf[0:1, 8:12], AF.Exp, scale=0.5),
                waits=[c_dve], inc=[c_act])
            P.op("dve", lambda e: e.tensor_scalar(
                cbuf[0:1, 8:12], cbuf[0:1, 12:16], -SCALE_ATT, None,
                ALU.mult), waits=[c_act], inc=[c_dve])
            P.op("te", lambda e: e.matmul(
                cp_ps[:, 0:4], self.onesfb[0:1, :], cbuf[0:1, 8:12],
                start=True, stop=True), waits=[c_dve], inc=[c_te])
            P.op("dve", lambda e: e.tensor_copy(crep4[:, :], cp_ps[:, 0:4]),
                 waits=[c_te], inc=[c_dve])
            for hh in range(4):
                P.op("dve", lambda e, hh=hh: e.tensor_scalar(
                    dbadj4[:, hh * JT:(hh + 1) * JT], self.dbiasb[:, :],
                    crep4[:, hh:hh + 1], None, ALU.add),
                    waits=[c_dve], inc=[c_dve])
            cprep = c_dve.snap()

            for hh in range(4):
                h = gi * 4 + hh
                o_free = head_eplg
                qoff = h * TOWN
                for b in range(3):
                    P.op("te", lambda e, b=b: e.matmul(
                        o_ps[b][:, :], self.identb[:, :], zbuf[:, :],
                        start=True, stop=False),
                        waits=[ones_set, o_free, cprep],
                        inc=([c_te] if b == 2 else []))

                def emit_S(jobs, sl, cprep):
                    alln = sum(len(j["s_mms"]) for j in jobs)
                    i = 0
                    for base, jb in zip((0, 512), jobs):
                        for dst_off, w, lf, rf in jb["s_mms"]:
                            i += 1
                            P.op("te", lambda e, dst_off=base + dst_off, w=w,
                                 lf=lf, rf=rf, sl=sl: e.matmul(
                                     s_ps[sl][:, dst_off:dst_off + w],
                                     lf(), rf(), start=True, stop=True),
                                 waits=[ready, cprep, exp_done[sl]],
                                 inc=([c_te] if i == alln else []))
                    return self.c_te.snap()

                def emit_EA(jobs, sl, smm):
                    extra = []
                    ntri = sum(4 for jb in jobs if jb["tri"])
                    it = 0
                    for base, jb in zip((0, 512), jobs):
                        if not jb["tri"]:
                            continue
                        for i4 in range(4):
                            it += 1
                            P.op("dve", lambda e, o=base + i4 * 128, sl=sl:
                                 e.tensor_tensor(
                                     s_ps[sl][:, o:o + 128],
                                     s_ps[sl][:, o:o + 128],
                                     self.trib[:, :], ALU.add),
                                 waits=([smm] if it == 1 else []),
                                 inc=([c_dve] if it == ntri else []))
                    if ntri:
                        extra = [self.c_dve.snap()]
                    width = (512 + jobs[1]["width"]) if len(jobs) == 2 \
                        else jobs[0]["width"]
                    P.op("act", lambda e, sl=sl, width=width,
                         bias_ap=jobs[0]["bias"]: e.activation(
                             pt[sl][:, 0:width], s_ps[sl][:, 0:width],
                             AF.Exp, bias=bias_ap, scale=SCALE_ATT),
                         waits=[smm, av_done[sl]] + extra, inc=[c_act])
                    exp_done[sl] = c_act.snap()
                    na = sum(len(jb["avs"]) for jb in jobs)
                    i = 0
                    for base, jb in zip((0, 512), jobs):
                        for j, pt_off, vf in jb["avs"]:
                            i += 1
                            P.op("te", lambda e, j=j, pt_off=base + pt_off,
                                 vf=vf, sl=sl: e.matmul(
                                     o_tile(j), pt[sl][:, pt_off:pt_off + 128],
                                     vf(), start=False, stop=False),
                                 waits=[exp_done[sl]],
                                 inc=([c_te] if i == na else []))
                    av_done[sl] = self.c_te.snap()

                jl = []
                for kt in range(56):
                    jmin = kt // 8 + 1
                    w_all = (8 - jmin) * 128
                    off = 0
                    while off < w_all:
                        w = min(512, w_all - off)
                        q0 = qoff + jmin * 128 + off
                        jlo = jmin + off // 128
                        lf = (lambda bi=bi, kt=kt:
                              kbuf[bi][:, kt * 128:(kt + 1) * 128])
                        rf = (lambda q0=q0, w=w: self.q_sb[:, q0:q0 + w])
                        avs = [(j, (j - jlo) * 128,
                                (lambda bi=bi, kt=kt:
                                 vbuf[bi][:, kt * 144:kt * 144 + 129]))
                               for j in range(jlo, jlo + w // 128)]
                        jl.append(dict(s_mms=[(0, w, lf, rf)], width=w,
                                       bias=crep4[:, hh:hh + 1], avs=avs,
                                       tri=False, bkey=("c", hh)))
                        off += w
                for dd in range(8):
                    for half in range(2):
                        smms = []
                        avs = []
                        for i in range(4):
                            j = half * 4 + i
                            kk = 8 * j + dd
                            lf = (lambda bi=bi, kk=kk:
                                  kbuf[bi][:, kk * 128:(kk + 1) * 128])
                            q0 = qoff + j * 128
                            rf = (lambda q0=q0: self.q_sb[:, q0:q0 + 128])
                            smms.append((i * 128, 128, lf, rf))
                            avs.append((j, i * 128,
                                        (lambda bi=bi, kk=kk:
                                         vbuf[bi][:, kk * 144:kk * 144 + 129])))
                        jl.append(dict(
                            s_mms=smms, width=512,
                            bias=dbadj4[:, hh * JT + dd:hh * JT + dd + 1],
                            avs=avs, tri=False, bkey=("d", hh, dd)))
                for half in range(2):
                    smms = []
                    avs = []
                    for i in range(4):
                        j = half * 4 + i
                        k0 = gi * TOWN + j * 128
                        lf = (lambda k0=k0: self.kown[:, k0:k0 + 128])
                        q0 = qoff + j * 128
                        rf = (lambda q0=q0: self.q_sb[:, q0:q0 + 128])
                        smms.append((i * 128, 128, lf, rf))
                        avs.append((j, i * 128,
                                    (lambda q=gi * JT + j:
                                     self.vown[:, q * 129:(q + 1) * 129])))
                    jl.append(dict(s_mms=smms, width=512,
                                   bias=crep4[:, hh:hh + 1], avs=avs,
                                   tri=True, bkey=("t", hh)))
                plist = []
                p = 0
                while p < len(jl):
                    if (p + 1 < len(jl)
                            and jl[p]["bkey"] == jl[p + 1]["bkey"]):
                        plist.append([jl[p], jl[p + 1]])
                        p += 2
                    else:
                        plist.append([jl[p]])
                        p += 1
                pend = None
                for pr in plist:
                    sl = slot
                    slot = (slot + 1) % NS
                    smm = emit_S(pr, sl, cprep)
                    if pend is not None:
                        emit_EA(*pend)
                    pend = (pr, sl, smm)
                emit_EA(*pend)
                all_av = self.c_te.snap()

                for j in range(JT):
                    P.op("dve", lambda e, j=j: e.reciprocal(
                        rcpb[:, j:j + 1], o_tile(j)[:, 128:129]),
                        waits=[all_av, c_dve], inc=[c_dve])
                    P.op("dve", lambda e, j=j: e.tensor_scalar(
                        sst[:, j * 128:(j + 1) * 128], o_tile(j)[:, 0:128],
                        rcpb[:, j:j + 1], None, ALU.mult),
                        waits=[c_dve, sst_free], inc=[c_dve])
                head_eplg = self.c_dve.snap()
                P.op("sp", lambda e, h=h: e.dma_start(
                    out=self.s_hbm.rearrange("(j p) e -> p j e", p=128)
                    [:, :, h * 128:(h + 1) * 128],
                    in_=sst[:, :].rearrange("p (j d) -> p j d", j=JT)),
                    waits=[head_eplg], inc=[c_sp])
                sst_free = self.c_sp.snap()
            group_done[bi] = self.c_te.snap()

        if DEBUG:
            fin = self.c_sp.snap()
            self.dbg_s = self.dram_out("dbg_s", [TOWN, E], F32)
            P.op("sp", lambda e: e.dma_start(out=self.dbg_s, in_=self.s_hbm),
                 waits=[fin], inc=[c_sp])

    def phase3(self, es):
        P = self.P
        c_sp, c_gp, c_te, c_act, c_dve = (self.c_sp, self.c_gp, self.c_te,
                                          self.c_act, self.c_dve)
        s = self.sb
        acc = [s(es, f"acc{i}", [128, JT * 512], F32) for i in (0, 1)]

        with ExitStack() as pa:
            rw1b = [self.sb(pa, f"rw1b{i}", [128, 4 * HID], BF16)
                    for i in (0, 1)]
            rwgb = [self.sb(pa, f"rwgb{i}", [128, 4 * HID], BF16)
                    for i in (0, 1)]
            rw2b = [self.sb(pa, f"rw2b{i}", [128, 8 * LAT], BF16)
                    for i in (0, 1)]
            abT = self.sb(pa, "abT", [128, 8 * TOWN], BF16)
            rbu = [self.sb(pa, f"rb3{i}", [128, 512], BF16) for i in (0, 1)]
            rb2u = [self.sb(pa, f"rb23{i}", [128, 512], BF16) for i in (0, 1)]
            ab_ps = [self.ps(pa, f"ab_ps{i}", [128, 1024]) for i in (0, 1)]
            y_ps = [self.ps(pa, f"y_ps{i}", [128, 512]) for i in (0, 1)]
            unit_free = [None, None]
            rb_free = [None, None]

            xlT_view = self.xlT[:, :].rearrange("p (l n) -> p l n", l=4)
            P.op("dve", lambda e: e.memset(acc[0][:, :], 0.0), inc=[c_dve])
            acc_zero = c_dve.snap()

            w_free = [None, None]
            y_free = [None, None]
            for ex in range(NEXP_RUN):
                wi = ex % 2
                P.op("gp", lambda e, ex=ex, wi=wi: e.dma_start(
                    out=rw1b[wi][:, :].rearrange("p (l n) -> p l n", l=4),
                    in_=self.rw1T_in[ex].rearrange("(l p) n -> p l n", p=128)),
                    waits=[w_free[wi]], inc=[c_gp])
                P.op("gp", lambda e, ex=ex, wi=wi: e.dma_start(
                    out=rwgb[wi][:, :].rearrange("p (l n) -> p l n", l=4),
                    in_=self.rwgT_in[ex].rearrange("(l p) n -> p l n", p=128)),
                    waits=[c_gp], inc=[c_gp])
                P.op("gp", lambda e, ex=ex, wi=wi: e.dma_start(
                    out=rw2b[wi][:, :].rearrange("p (hh n) -> p hh n", hh=8),
                    in_=self.rw2T_in[ex].rearrange("(hh p) n -> p hh n", p=128)),
                    waits=[c_gp], inc=[c_gp])
                wld = self.c_gp.snap()
                if DEBUG and ex == 0:
                    self.dbg_rwge = self.dram_out("dbg_rwge", [128, 4 * HID], BF16)
                    P.op("sp", lambda e: e.dma_start(out=self.dbg_rwge,
                                                     in_=rwgb[0][:, :]),
                         waits=[wld], inc=[c_sp])

                for hch in range(8):
                    for tc in range(2):
                        pu = (hch * 2 + tc) % 2
                        for lch in range(4):
                            P.op("te", lambda e, wi=wi, hch=hch, tc=tc,
                                 lch=lch, pu=pu: e.matmul(
                                     ab_ps[pu][:, 0:512],
                                     rw1b[wi][:, lch * HID + hch * 128:
                                              lch * HID + (hch + 1) * 128],
                                     xlT_view[:, lch, tc * 512:(tc + 1) * 512],
                                     start=(lch == 0), stop=(lch == 3)),
                                 waits=[wld, unit_free[pu]], inc=[])
                            P.op("te", lambda e, wi=wi, hch=hch, tc=tc,
                                 lch=lch, pu=pu: e.matmul(
                                     ab_ps[pu][:, 512:1024],
                                     rwgb[wi][:, lch * HID + hch * 128:
                                              lch * HID + (hch + 1) * 128],
                                     xlT_view[:, lch, tc * 512:(tc + 1) * 512],
                                     start=(lch == 0), stop=(lch == 3)),
                                 waits=[wld, unit_free[pu]],
                                 inc=([c_te] if lch == 3 else []))
                        abmm = self.c_te.snap()
                        P.op("act", lambda e, pu=pu: e.activation(
                            rbu[pu][:, :], ab_ps[pu][:, 512:1024], AF.Relu),
                            waits=[abmm, rb_free[pu]], inc=[c_act])
                        P.op("dve", lambda e, pu=pu: e.tensor_tensor(
                            rb2u[pu][:, :], rbu[pu][:, :], rbu[pu][:, :],
                            ALU.mult),
                            waits=[c_act], inc=[c_dve])
                        P.op("dve", lambda e, hch=hch, tc=tc, pu=pu:
                             e.tensor_tensor(
                                 abT[:, hch * TOWN + tc * 512:
                                     hch * TOWN + (tc + 1) * 512],
                                 ab_ps[pu][:, 0:512], rb2u[pu][:, :],
                                 ALU.mult),
                             waits=[c_dve], inc=[c_dve])
                        unit_free[pu] = self.c_dve.snap()
                        rb_free[pu] = unit_free[pu]
                ab_done = self.c_dve.snap()

                prev_acc = acc[ex % 2]
                next_acc = acc[(ex + 1) % 2]
                for t in range(JT):
                    yi = t % 2
                    for hch in range(8):
                        P.op("te", lambda e, wi=wi, t=t, hch=hch, yi=yi:
                             e.matmul(
                                 y_ps[yi][:, :],
                                 abT[:, hch * TOWN + t * 128:
                                     hch * TOWN + (t + 1) * 128],
                                 rw2b[wi][:, hch * LAT:(hch + 1) * LAT],
                                 start=(hch == 0), stop=(hch == 7)),
                             waits=[ab_done, y_free[yi]],
                             inc=([c_te] if hch == 7 else []))
                    ymm = self.c_te.snap()
                    P.op("dve", lambda e, ex=ex, t=t, yi=yi,
                         pa_=prev_acc, na=next_acc: e.scalar_tensor_tensor(
                             na[:, t * 512:(t + 1) * 512],
                             y_ps[yi][:, :],
                             self.wdense[:, t * NEXP + ex:t * NEXP + ex + 1],
                             pa_[:, t * 512:(t + 1) * 512],
                             ALU.mult, ALU.add),
                         waits=[ymm, acc_zero], inc=[c_dve])
                    y_free[yi] = self.c_dve.snap()
                w_free[wi] = self.c_dve.snap()
            acc_done = self.c_dve.snap()
            acc_fin = acc[NEXP_RUN % 2]
            if DEBUG:
                self.dbg_acc = self.dram_out("dbg_acc", [128, JT * 512], F32)
                P.op("sp", lambda e: e.dma_start(out=self.dbg_acc,
                                                 in_=acc_fin[:, :]),
                     waits=[acc_done], inc=[c_sp])
                self.dbg_ab = self.dram_out("dbg_ab", [128, 8 * TOWN], BF16)
                P.op("sp", lambda e: e.dma_start(out=self.dbg_ab,
                                                 in_=abT[:, :]),
                     waits=[acc_done], inc=[c_sp])
                self.dbg_xl2 = self.dram_out("dbg_xl2", [128, 4 * TOWN], BF16)
                P.op("sp", lambda e: e.dma_start(out=self.dbg_xl2,
                                                 in_=self.xlT[:, :]),
                     waits=[acc_done], inc=[c_sp])


        with ExitStack() as pb:
            sw2b = self.sb(pb, "sw2b", [128, 8 * E], BF16)
            wupb = self.sb(pb, "wupb", [128, 4 * E], BF16)
            accb = self.sb(pb, "accb", [128, JT * 512], BF16)
            accT = self.sb(pb, "accT", [128, 4 * TOWN], BF16)
            gbuf = self.sb(pb, "gbuf", [128, E], F32)
            stile = self.sb(pb, "stile", [128, E], F32)
            xtile = self.sb(pb, "xtile", [128, E], F32)
            outst = self.sb(pb, "outst", [128, E], F32)
            o4_ps = self.ps(pb, "o4_ps", [128, 2048])
            tp2_ps = self.ps(pb, "tp2_ps", [128, 512], BF16)

            P.op("gp", lambda e: e.dma_start(
                out=sw2b[:, :].rearrange("p (hh n) -> p hh n", hh=8),
                in_=self.sw2T_in.rearrange("(hh p) n -> p hh n", p=128)),
                waits=[acc_done], inc=[c_gp])
            P.op("gp", lambda e: e.dma_start(
                out=wupb[:, :].rearrange("p (l n) -> p l n", l=4),
                in_=self.wupT_in.rearrange("(l p) n -> p l n", p=128)),
                waits=[acc_done], inc=[c_gp])
            wld2 = self.c_gp.snap()

            P.op("dve", lambda e: e.tensor_copy(accb[:, :], acc_fin[:, :]),
                 waits=[acc_done], inc=[c_dve])
            accb_done = self.c_dve.snap()
            tp_free = None
            for t in range(JT):
                for lch in range(4):
                    P.op("te", lambda e, t=t, lch=lch: e.transpose(
                        tp2_ps[:, lch * 128:(lch + 1) * 128],
                        accb[:, t * 512 + lch * 128: t * 512 + (lch + 1) * 128],
                        self.identb[:, :]),
                        waits=[accb_done, tp_free],
                        inc=([c_te] if lch == 3 else []))
                tmm = self.c_te.snap()
                P.op("dve", lambda e, t=t: e.tensor_copy(
                    accT[:, :].rearrange("p (l n) -> p l n", l=4)
                    [:, :, t * 128:(t + 1) * 128],
                    tp2_ps[:, :].rearrange("p (l n) -> p l n", l=4)),
                    waits=[tmm], inc=[c_dve])
                tp_free = self.c_dve.snap()
            accT_done = self.c_dve.snap()

            o4_free = None
            st_free = acc_done
            for t in range(JT):
                P.op("sp", lambda e, t=t: e.dma_start(
                    out=stile[:, :],
                    in_=self.s_hbm[t * 128:(t + 1) * 128, :]),
                    waits=[st_free], inc=[c_sp])
                P.op("sp", lambda e, t=t: e.dma_start(
                    out=xtile[:, :],
                    in_=self.x_own[t * 128:(t + 1) * 128, :]),
                    waits=[st_free], inc=[c_sp])
                sx_in = self.c_sp.snap()
                for nch in range(4):
                    for hch in range(8):
                        P.op("te", lambda e, t=t, nch=nch, hch=hch: e.matmul(
                            o4_ps[:, nch * 512:(nch + 1) * 512],
                            self.hhT[:, hch * TOWN + t * 128:
                                     hch * TOWN + (t + 1) * 128],
                            sw2b[:, hch * E + nch * 512:
                                 hch * E + (nch + 1) * 512],
                            start=(hch == 0), stop=False),
                            waits=[wld2, o4_free], inc=[])
                    for lch in range(4):
                        P.op("te", lambda e, t=t, nch=nch, lch=lch: e.matmul(
                            o4_ps[:, nch * 512:(nch + 1) * 512],
                            accT[:, lch * TOWN + t * 128:
                                 lch * TOWN + (t + 1) * 128],
                            wupb[:, lch * E + nch * 512:
                                 lch * E + (nch + 1) * 512],
                            start=False, stop=(lch == 3)),
                            waits=[accT_done, wld2, o4_free],
                            inc=([c_te] if lch == 3 and nch == 3 else []))
                omm = self.c_te.snap()
                P.op("act", lambda e: e.activation(
                    gbuf[:, :], o4_ps[:, :], AF.Silu),
                    waits=[omm], inc=[c_act])
                o4_free = self.c_act.snap()
                if DEBUG and t == 0:
                    self.dbg_g = self.dram_out("dbg_g", [128, E], F32)
                    P.op("sp", lambda e: e.dma_start(out=self.dbg_g,
                                                     in_=gbuf[:, :]),
                         waits=[o4_free], inc=[c_sp])
                P.op("dve", lambda e: e.tensor_tensor(
                    outst[:, :], stile[:, :], gbuf[:, :], ALU.mult),
                    waits=[c_act, sx_in, st_free], inc=[c_dve])
                P.op("dve", lambda e: e.tensor_tensor(
                    outst[:, :], outst[:, :], xtile[:, :], ALU.add),
                    waits=[c_dve], inc=[c_dve])
                comb = self.c_dve.snap()
                P.op("sp", lambda e, t=t: e.dma_start(
                    out=self.out[t * 128:(t + 1) * 128, :], in_=outst[:, :]),
                    waits=[comb], inc=[c_sp])
                st_free = self.c_sp.snap()

    def finale(self):
        self.P.op("sp", None, waits=[self.c_sp.snap()])



def nc_sem_stack(nc, ctrs):
    class _S:
        def __enter__(self):
            self.es = ExitStack()
            for c in ctrs:
                c.sem = self.es.enter_context(nc.semaphore(c.name))
            return self

        def __exit__(self, *a):
            self.es.close()
            return False
    return _S()


def kernel(**inputs):
    maps = _stage_inputs(inputs)
    b = Builder()
    nc = b.build()
    res = run_bass_kernel_spmd(nc, maps, core_ids=list(range(NCORES)))
    if DEBUG:
        return res
    return _assemble(res.results)


# version 15
# speedup vs baseline: 1.1030x; 1.1030x over previous
import math
from contextlib import ExitStack

import numpy as np
import ml_dtypes

import concourse.bass as bass
import concourse.mybir as mybir
from concourse.bass_utils import run_bass_kernel_spmd
import concourse.bass_utils as _bass_utils

_orig_run_command = _bass_utils.run_command

def _run_command_ldwopt(cmd, *a, **kw):
    cmd = ["--enable-ldw-opt=true" if c == "--enable-ldw-opt=false" else c
           for c in cmd]
    return _orig_run_command(cmd, *a, **kw)

_bass_utils.run_command = _run_command_ldwopt

F32 = mybir.dt.float32
BF16 = mybir.dt.bfloat16
AF = mybir.ActivationFunctionType
ALU = mybir.AluOpType
AX = mybir.AxisListType
BF = ml_dtypes.bfloat16

NCORES = 8
T, E, H, G, D = 8192, 2048, 16, 4, 128
HID, LAT, NEXP = 1024, 512, 16
TOWN = T // NCORES
NTILE = T // 128
JT = 8
ECH = E // 128
SCALE_ATT = 1.0 / math.sqrt(D)
NEG = -1.0e30
EPS = 1e-5
RSCALE = 2.5

DEBUG = 0
PHASES = 99
NEXP_RUN = NEXP
PIPE_P1 = 1


class Ctr:
    def __init__(self, name, step=1):
        self.name, self.step, self.val, self.sem = name, step, 0, None

    def bump(self):
        self.val += self.step
        return self.val

    def snap(self):
        return (self, self.val)


class Prog:
    ORDER = ("sp", "gp", "te", "act", "dve")
    QNAME = {"sp": "sync", "gp": "gpsimd", "te": "tensor",
             "act": "scalar", "dve": "vector"}

    def __init__(self):
        self.ops = []

    def op(self, eng, fn, waits=(), inc=(), dd=False):
        ws = []
        for w in waits:
            if w is None:
                continue
            ws.append((w, w.val) if isinstance(w, Ctr) else w)
        incs = [(c, c.bump()) for c in inc]
        self.ops.append((eng, fn, ws, incs, dd))

    def barrier(self, ctrs):
        snap = [c.snap() for c in ctrs]
        for eng in self.ORDER:
            self.ops.append((eng, None, snap, [], False))

    def replay(self, block):
        for name in self.ORDER:
            mine = [(fn, ws, incs, dd) for (e, fn, ws, incs, dd) in self.ops
                    if e == name]
            if not mine:
                continue

            def section(eng, mine=mine, name=name):
                hwm = {}
                for fn, ws, incs, dd in mine:
                    for c, v in ws:
                        if v > 0 and (not dd or hwm.get(id(c.sem), 0) < v):
                            eng.wait_ge(c.sem, v)
                            if v > hwm.get(id(c.sem), 0):
                                hwm[id(c.sem)] = v
                    if fn is None:
                        continue
                    inst = fn(eng)
                    for c, _ in incs:
                        inst = inst.then_inc(c.sem, c.step)

            getattr(block, self.QNAME[name])(section)


def _stage_inputs(ins):
    x = np.ascontiguousarray(np.asarray(ins["L0_attn_x"]).reshape(T, E))
    asnp = lambda a: np.ascontiguousarray(np.asarray(a))
    base = {
        "x_full": x,
        "xT_full": asnp(x.T),
        "identf": np.eye(128, dtype=np.float32),
        "sel8": np.ascontiguousarray(
            np.eye(8, dtype=np.float32)[:, :, None]
            * np.ones((1, 1, 128), np.float32)),
        "wq": asnp(ins["L0_attn_Wq"]), "wk": asnp(ins["L0_attn_Wk"]),
        "wv": asnp(ins["L0_attn_Wv"]),
        "bqT": asnp(np.asarray(ins["L0_attn_bq"]).T),
        "bkT": asnp(np.asarray(ins["L0_attn_bk"]).T),
        "bvT": asnp(np.asarray(ins["L0_attn_bv"]).T),
        "sw1T": asnp(np.asarray(ins["L0_ffn_sw1"]).T),
        "swgT": asnp(np.asarray(ins["L0_ffn_swg"]).T),
        "sw2T": asnp(np.asarray(ins["L0_ffn_sw2"]).T),
        "wdT": asnp(np.asarray(ins["L0_ffn_Wdown"]).T),
        "wgT": asnp(np.asarray(ins["L0_ffn_Wg"]).T),
        "wupT": asnp(np.asarray(ins["L0_ffn_Wup"]).T),
        "rw1T": asnp(np.asarray(ins["L0_ffn_rw1"]).transpose(0, 2, 1)),
        "rwgT": asnp(np.asarray(ins["L0_ffn_rwg"]).transpose(0, 2, 1)),
        "rw2T": asnp(np.asarray(ins["L0_ffn_rw2"]).transpose(0, 2, 1)),
        "btrow": asnp(np.tile(np.asarray(ins["L0_ffn_biases"]), JT)[None, :]),
        "ident": np.eye(128, dtype=BF),
        "tri": np.where(np.arange(128)[:, None] > np.arange(128)[None, :],
                        np.float32(NEG), np.float32(0.0)).astype(np.float32),
        "eps": np.full((128, 1), EPS, np.float32),
        "onesb": np.ones((1, 128), BF),
        "onesf": np.ones((1, 128), np.float32),
    }
    xt = x.reshape(NTILE, 128, E)
    maps = []
    for c in range(NCORES):
        m = dict(base)
        m["x_own"] = np.ascontiguousarray(xt[c::8].reshape(TOWN, E))
        db = np.zeros((128, JT), np.float32)
        db[:, c:] = NEG
        m["dbias"] = db
        maps.append(m)
    return maps


def _assemble(results):
    out = np.empty((T, E), np.float32)
    ot = out.reshape(NTILE, 128, E)
    for c in range(NCORES):
        ot[c::8] = np.asarray(results[c]["out"]).reshape(JT, 128, E)
    return out.reshape(1, T, E)


class Builder:
    def __init__(self):
        self.nc = bass.Bass("TRN2", target_bir_lowering=False)
        self.P = Prog()
        self.c_sp = Ctr("c_sp", 16)
        self.c_gp = Ctr("c_gp", 16)
        self.c_te = Ctr("c_te", 1)
        self.c_act = Ctr("c_act", 1)
        self.c_dve = Ctr("c_dve", 1)
        self.ctrs = [self.c_sp, self.c_gp, self.c_te, self.c_act, self.c_dve]

    def dram_in(self, name, shape, dtype=F32):
        return self.nc.dram_tensor(name, shape, dtype, kind="ExternalInput").ap()

    def dram_out(self, name, shape, dtype=F32):
        return self.nc.dram_tensor(name, shape, dtype, kind="ExternalOutput").ap()

    def declare(self):
        di = self.dram_in
        self.x_full = di("x_full", [T, E])
        self.x_own = di("x_own", [TOWN, E])
        self.xT_full = di("xT_full", [E, T])
        self.identf_in = di("identf", [128, 128])
        self.sel8_in = di("sel8", [8, 8, 128])
        self.wq_in = di("wq", [H, E, D])
        self.wk_in = di("wk", [G, E, D])
        self.wv_in = di("wv", [G, E, D])
        self.bqT_in = di("bqT", [D, H])
        self.bkT_in = di("bkT", [D, G])
        self.bvT_in = di("bvT", [D, G])
        self.sw1T_in = di("sw1T", [E, HID])
        self.swgT_in = di("swgT", [E, HID])
        self.sw2T_in = di("sw2T", [HID, E])
        self.wdT_in = di("wdT", [E, LAT])
        self.wgT_in = di("wgT", [E, NEXP])
        self.wupT_in = di("wupT", [LAT, E])
        self.rw1T_in = di("rw1T", [NEXP, LAT, HID])
        self.rwgT_in = di("rwgT", [NEXP, LAT, HID])
        self.rw2T_in = di("rw2T", [NEXP, HID, LAT])
        self.btrow_in = di("btrow", [1, JT * NEXP])
        self.ident_in = di("ident", [128, 128], BF16)
        self.tri_in = di("tri", [128, 128])
        self.eps_in = di("eps", [128, 1])
        self.onesb_in = di("onesb", [1, 128], BF16)
        self.onesf_in = di("onesf", [1, 128])
        self.dbias_in = di("dbias", [128, JT])
        self.out = self.dram_out("out", [TOWN, E])
        self.k_hbm = self.nc.dram_tensor("k_hbm", [G, D, T], BF16).ap()
        self.v_hbm = self.nc.dram_tensor("v_hbm", [G, D, T], BF16).ap()
        self.s_hbm = self.nc.dram_tensor("s_hbm", [TOWN, E], F32).ap()

    def sb(self, es, name, shape, dtype):
        return es.enter_context(self.nc.sbuf_tensor(name, shape, dtype))

    def ps(self, es, name, shape, dtype=F32):
        return es.enter_context(self.nc.psum_tensor(name, shape, dtype))

    def bar(self):
        self.P.barrier(self.ctrs)

    def ln_stats(self, bufs, src_dma_fn, st):
        P = self.P
        c_sp, c_act, c_dve = self.c_sp, self.c_act, self.c_dve
        xt, sqs, stats = bufs["xt"], bufs["sqs"], bufs["stats"]
        msq, var4 = bufs["msq"], bufs["var4"]
        P.op("sp", src_dma_fn, waits=[st.get("xt_free")], inc=[c_sp])
        xin = c_sp.snap()
        for t in range(4):
            P.op("dve", lambda e, t=t, xt=xt, stats=stats: e.reduce_sum(
                stats[:, t:t + 1], xt[:, t * E:(t + 1) * E], axis=AX.X),
                waits=[xin, st.get("stats_free")], inc=[c_dve])
            P.op("act", lambda e, t=t, xt=xt, sqs=sqs, stats=stats:
                 e.activation(
                     sqs[:, 0:E], xt[:, t * E:(t + 1) * E], AF.Square,
                     accum_out=stats[:, 4 + t:5 + t]),
                 waits=[xin, st.get("stats_free")], inc=[c_act])
        sums_done = c_dve.snap()
        sqs_done = c_act.snap()
        P.op("dve", lambda e, stats=stats: e.tensor_scalar(
            stats[:, 8:12], stats[:, 0:4], 1.0 / E, None, ALU.mult),
            waits=[sums_done], inc=[c_dve])
        P.op("dve", lambda e, stats=stats, msq=msq: e.tensor_tensor(
            msq[:, :], stats[:, 8:12], stats[:, 8:12], ALU.mult),
            waits=[c_dve], inc=[c_dve])
        P.op("dve", lambda e, stats=stats, msq=msq, var4=var4:
             e.scalar_tensor_tensor(
                 var4[:, :], stats[:, 4:8], 1.0 / E, msq[:, :],
                 ALU.mult, ALU.subtract),
             waits=[sqs_done, c_dve], inc=[c_dve])
        P.op("act", lambda e, var4=var4: e.activation(
            var4[:, :], var4[:, :], AF.Ln, bias=self.epsb[:, 0:1]),
            waits=[c_dve], inc=[c_act])
        P.op("act", lambda e, var4=var4, stats=stats: e.activation(
            stats[:, 12:16], var4[:, :], AF.Exp, scale=-0.5),
            waits=[c_act], inc=[c_act])
        rstd_done = c_act.snap()
        P.op("dve", lambda e, stats=stats: e.scalar_tensor_tensor(
            stats[:, 16:20], stats[:, 8:12], -1.0, stats[:, 12:16],
            ALU.mult, ALU.mult),
            waits=[rstd_done], inc=[c_dve])
        st["xt_free"] = self.c_dve.snap()
        return self.c_dve.snap()

    def ln_chunk(self, bufs, src_dma_fn, xnT_write_fn, st):
        P = self.P
        c_sp, c_te, c_act, c_dve = self.c_sp, self.c_te, self.c_act, self.c_dve
        xt, sqs, stats = bufs["xt"], bufs["sqs"], bufs["stats"]
        msq, var4, xnb, tp_ps = (bufs["msq"], bufs["var4"], bufs["xnb"],
                                 bufs["tp_ps"])
        P.op("sp", src_dma_fn, waits=[st.get("xt_free")], inc=[c_sp])
        xin = c_sp.snap()
        for t in range(4):
            P.op("dve", lambda e, t=t: e.reduce_sum(
                stats[:, t:t + 1], xt[:, t * E:(t + 1) * E], axis=AX.X),
                waits=[xin, st.get("stats_free")], inc=[c_dve])
            P.op("act", lambda e, t=t: e.activation(
                sqs[:, 0:E],
                xt[:, t * E:(t + 1) * E], AF.Square,
                accum_out=stats[:, 4 + t:5 + t]),
                waits=[xin, st.get("stats_free")],
                inc=[c_act])
        sums_done = c_dve.snap()
        sqs_done = c_act.snap()
        P.op("dve", lambda e: e.tensor_scalar(
            stats[:, 8:12], stats[:, 0:4], 1.0 / E, None, ALU.mult),
            waits=[sums_done], inc=[c_dve])
        P.op("dve", lambda e: e.tensor_tensor(
            msq[:, :], stats[:, 8:12], stats[:, 8:12], ALU.mult),
            waits=[c_dve], inc=[c_dve])
        P.op("dve", lambda e: e.scalar_tensor_tensor(
            var4[:, :], stats[:, 4:8], 1.0 / E, msq[:, :],
            ALU.mult, ALU.subtract),
            waits=[sqs_done, c_dve], inc=[c_dve])
        P.op("act", lambda e: e.activation(
            var4[:, :], var4[:, :], AF.Ln, bias=self.epsb[:, 0:1]),
            waits=[c_dve], inc=[c_act])
        P.op("act", lambda e: e.activation(
            stats[:, 12:16], var4[:, :], AF.Exp, scale=-0.5),
            waits=[c_act], inc=[c_act])
        rstd_done = c_act.snap()
        for t in range(4):
            P.op("dve", lambda e, t=t: e.tensor_scalar(
                xnb[:, t * E:(t + 1) * E], xt[:, t * E:(t + 1) * E],
                stats[:, 8 + t:9 + t], stats[:, 12 + t:13 + t],
                ALU.subtract, ALU.mult),
                waits=[rstd_done], inc=[c_dve])
        xnb_done = c_dve.snap()
        st["xt_free"] = xnb_done
        st["stats_free"] = xnb_done
        tp_ps2 = bufs["tp_ps2"]
        for b8 in range(8):
            tp = tp_ps if b8 % 2 == 0 else tp_ps2
            for k in range(8):
                idx = b8 * 8 + k
                t, ech = idx // ECH, idx % ECH
                P.op("te", lambda e, t=t, ech=ech, k=k, tp=tp: e.transpose(
                    tp[:, k * 128:(k + 1) * 128],
                    xnb[:, t * E + ech * 128: t * E + (ech + 1) * 128],
                    self.identb[:, :]),
                    waits=[xnb_done, st.get(f"tp_free{b8 % 2}")],
                    inc=([c_te] if k == 7 else []))
            tp_done = c_te.snap()
            t0, ech0 = (b8 * 8) // ECH, (b8 * 8) % ECH
            P.op("dve", lambda e, b8=b8, t0=t0, ech0=ech0:
                 xnT_write_fn(e, b8, t0, ech0),
                 waits=[tp_done, st.get("xnT_free")], inc=[c_dve])
            st[f"tp_free{b8 % 2}"] = c_dve.snap()
        return c_dve.snap()

    def build(self):
        self.declare()
        with ExitStack() as outer:
            with nc_sem_stack(self.nc, self.ctrs):
                s = self.sb
                self.identb = s(outer, "identb", [128, 128], BF16)
                self.identf = s(outer, "identf_b", [128, 128], F32)
                self.sel8b = s(outer, "sel8b", [8, 8 * 128], F32)
                self.trib = s(outer, "trib", [128, 128], F32)
                self.epsb = s(outer, "epsb", [128, 1], F32)
                self.onesbb = s(outer, "onesbb", [1, 128], BF16)
                self.onesfb = s(outer, "onesfb", [1, 128], F32)
                self.onescol = s(outer, "onescol", [128, 1], BF16)
                self.dbiasb = s(outer, "dbiasb", [128, JT], F32)
                self.bqTb = s(outer, "bqTb", [128, H], F32)
                self.bkTb = s(outer, "bkTb", [128, G], F32)
                self.bvTb = s(outer, "bvTb", [128, G], F32)
                self.brep8 = s(outer, "brep8", [128, JT * NEXP], F32)
                self.wdense = s(outer, "wdense", [128, JT * NEXP], F32)
                self.kmax64 = s(outer, "kmax64", [1, G * 16], F32)

                with self.nc.Block() as block:
                    with ExitStack() as p0:
                        self.phase0(p0)
                    self.bar()
                    if PHASES >= 1:
                        with ExitStack() as p1:
                            self.phase1(p1)
                        self.bar()
                    if PHASES >= 2:
                        self.hhT = s(outer, "hhT", [128, 8 * TOWN], BF16)
                        self.xlT = s(outer, "xlT", [128, 4 * TOWN], BF16)
                        with ExitStack() as mid:
                            self.xnT_own = s(mid, "xnT_own",
                                             [128, ECH * TOWN], BF16)
                            self.q_sb = s(mid, "q_sb", [128, H * TOWN], BF16)
                            self.kown = s(mid, "kown", [128, G * TOWN], BF16)
                            self.vown = s(mid, "vown", [128, G * JT * 129],
                                          BF16)
                            with ExitStack() as p1b:
                                self.phase1b(p1b)
                            self.bar()
                            if PHASES >= 3:
                                with ExitStack() as p15:
                                    self.phase15(p15)
                                self.bar()
                            if PHASES >= 4:
                                with ExitStack() as p2:
                                    self.phase2(p2)
                                self.bar()
                        if PHASES >= 5:
                            with ExitStack() as p3:
                                self.phase3(p3)
                            self.bar()
                    self.finale()
                    self.P.replay(block)
        return self.nc

    def phase0(self, es):
        P = self.P
        c_sp, c_te, c_dve = self.c_sp, self.c_te, self.c_dve
        for dst, src in [
            (self.identb, self.ident_in), (self.trib, self.tri_in),
            (self.identf, self.identf_in),
            (self.epsb, self.eps_in), (self.onesbb, self.onesb_in), (self.onesfb, self.onesf_in),
            (self.dbiasb, self.dbias_in), (self.bqTb, self.bqT_in),
            (self.bkTb, self.bkT_in), (self.bvTb, self.bvT_in),
        ]:
            P.op("sp", lambda e, d=dst, s=src: e.dma_start(out=d[:, :], in_=s),
                 inc=[c_sp])
        btrow = self.sb(es, "btrow_s", [1, JT * NEXP], F32)
        P.op("sp", lambda e: e.dma_start(out=btrow[:, :], in_=self.btrow_in),
             inc=[c_sp])
        P.op("sp", lambda e: e.dma_start(
            out=self.sel8b[:, :].rearrange("p (c n) -> p c n", c=8),
            in_=self.sel8_in), inc=[c_sp])
        ld = c_sp.snap()
        P.op("dve", lambda e: e.memset(self.onescol[:, :], 1.0), inc=[c_dve])
        rep_ps = self.ps(es, "rep_ps", [128, JT * NEXP])
        P.op("te", lambda e: e.matmul(rep_ps[:, 0:JT * NEXP],
                                      self.onesfb[0:1, :], btrow[0:1, :],
                                      start=True, stop=True),
             waits=[ld], inc=[c_te])
        mm2 = self.c_te.snap()
        P.op("dve", lambda e: e.tensor_copy(self.brep8[:, :],
                                            rep_ps[:, 0:JT * NEXP]),
             waits=[mm2], inc=[c_dve])

    def phase1(self, es):
        P = self.P
        c_sp, c_gp, c_te, c_act, c_dve = (self.c_sp, self.c_gp, self.c_te,
                                          self.c_act, self.c_dve)
        s = self.sb
        wkb = s(es, "wkb", [128, G * ECH * 128], BF16)
        wvb = s(es, "wvb", [128, G * ECH * 128], BF16)
        bufs2 = []
        for i in (0, 1):
            bufs2.append({
                "xt": s(es, f"xt{i}", [128, 4 * E], F32),
                "sqs": s(es, f"sqs{i}", [128, E], F32),
                "stats": s(es, f"stats{i}", [128, 24], F32),
                "msq": s(es, f"msq{i}", [128, 4], F32),
                "var4": s(es, f"var4{i}", [128, 4], F32),
            })
        xTb = [s(es, f"xTb{i}", [128, ECH * 512], BF16) for i in (0, 1)]
        statT = s(es, "statT", [8, 128], F32)
        reps = [s(es, f"reps{i}", [128, 1024], F32) for i in (0, 1)]
        t1b = [s(es, f"t1b{i}", [128, 512], F32) for i in (0, 1)]
        kst = [s(es, f"kst{i}", [128, G * 512], BF16) for i in (0, 1)]
        vst = [s(es, f"vst{i}", [128, G * 512], BF16) for i in (0, 1)]
        ukv = s(es, "ukv", [128, 2 * G], F32)
        ksq1 = s(es, "ksq1", [128, 512], BF16)
        kv_ps = [self.ps(es, f"kv_ps{i}", [128, 512]) for i in range(4)]
        st_ps = self.ps(es, "st_ps", [128, 128])
        rep_ps = self.ps(es, "rep_ps1", [128, 1024])

        for gi in range(G):
            P.op("gp", lambda e, gi=gi: e.dma_start(
                out=wkb[:, gi * ECH * 128:(gi + 1) * ECH * 128]
                .rearrange("p (c d) -> p c d", c=ECH),
                in_=self.wk_in[gi].rearrange("(c p) d -> p c d", p=128)),
                inc=[c_gp])
            P.op("gp", lambda e, gi=gi: e.dma_start(
                out=wvb[:, gi * ECH * 128:(gi + 1) * ECH * 128]
                .rearrange("p (c d) -> p c d", c=ECH),
                in_=self.wv_in[gi].rearrange("(c p) d -> p c d", p=128)),
                inc=[c_gp])
        wload = c_gp.snap()

        for gi in range(G):
            for which, wb in ((0, wkb), (1, wvb)):
                for ech in range(ECH):
                    P.op("te", lambda e, gi=gi, which=which, wb=wb, ech=ech:
                         e.matmul(
                             st_ps[:, which:which + 1],
                             wb[:, (gi * ECH + ech) * 128:
                                (gi * ECH + ech + 1) * 128],
                             self.onescol[:, 0:1],
                             start=(ech == 0), stop=(ech == ECH - 1)),
                         waits=[wload],
                         inc=([c_te] if ech == ECH - 1 else []),
                         dd=(ech > 0))
                mm = self.c_te.snap()
                P.op("dve", lambda e, gi=gi, which=which: e.tensor_copy(
                    ukv[:, which * G + gi:which * G + gi + 1],
                    st_ps[:, which:which + 1]),
                    waits=[mm], inc=[c_dve])
        u_done = c_dve.snap()

        st2 = [{}, {}]
        kst_free = [None, None]
        vst_free = [None, None]
        rep_free = [None, None]
        t1_free = [None, None]
        xT_free = [None, None]
        prev_vevac = [None]
        statT_free = [None]

        stt_prev = [None]
        rep_ps_free = [None]

        def rep_build(bi, stats_done):
            stats = bufs2[bi]["stats"]
            P.op("te", lambda e, stats=stats: e.transpose(
                st_ps[0:8, 0:128], stats[:, 12:20], self.identf[:, :]),
                waits=[stats_done, stt_prev[0], u_done, c_dve], inc=[c_te])
            P.op("dve", lambda e: e.tensor_copy(statT[:, :],
                                                st_ps[0:8, 0:128]),
                 waits=[c_te], inc=[c_dve])
            stt = c_dve.snap()
            stt_prev[0] = stt
            st2[bi]["stats_free"] = stt
            for c in range(8):
                dst = (c % 4) * 128 + (0 if c < 4 else 512)
                P.op("te", lambda e, c=c, dst=dst: e.matmul(
                    rep_ps[:, dst:dst + 128],
                    self.sel8b[:, c * 128:(c + 1) * 128],
                    statT[:, :], start=True, stop=True),
                    waits=[stt, rep_ps_free[0], rep_free[bi], c_dve],
                    inc=([c_te] if c == 7 else []))
            selmm = self.c_te.snap()
            P.op("dve", lambda e, bi=bi: e.tensor_copy(
                reps[bi][:, :], rep_ps[:, :]),
                waits=[selmm], inc=[c_dve])
            rep_ps_free[0] = self.c_dve.snap()
            return rep_ps_free[0]

        def kv_proj(ch, bi, rep_done):
            bank_guard = prev_vevac[0]
            for which, wb, ust, bT, dst_h, dst_s in (
                (0, wkb, 0, self.bkTb, self.k_hbm, kst),
                (1, wvb, G, self.bvTb, self.v_hbm, vst),
            ):
                for gi in range(G):
                    for ech in range(ECH):
                        P.op("te", lambda e, gi=gi, ech=ech, bi=bi, wb=wb:
                             e.matmul(
                                 kv_ps[gi][:, :],
                                 wb[:, (gi * ECH + ech) * 128:
                                    (gi * ECH + ech + 1) * 128],
                                 xTb[bi][:, ech * 512:(ech + 1) * 512],
                                 start=(ech == 0), stop=(ech == ECH - 1)),
                             waits=[st2[bi]["xT_ld"], wload, kst_free[bi],
                                    vst_free[bi], bank_guard],
                             inc=([c_te] if ech == ECH - 1 else []),
                             dd=(ech > 0))
                mm_done = self.c_te.snap()
                for gi in range(G):
                    tf = t1_free[gi % 2]
                    P.op("dve", lambda e, gi=gi, bi=bi: e.tensor_tensor(
                        t1b[gi % 2][:, :], kv_ps[gi][:, :],
                        reps[bi][:, 0:512], ALU.mult),
                        waits=[mm_done, rep_done, u_done, tf], inc=[c_dve])
                    P.op("dve", lambda e, gi=gi, bi=bi, ust=ust:
                         e.scalar_tensor_tensor(
                             t1b[gi % 2][:, :], reps[bi][:, 512:1024],
                             ukv[:, ust + gi:ust + gi + 1],
                             t1b[gi % 2][:, :], ALU.mult, ALU.add),
                         waits=[c_dve], inc=[c_dve])
                    fx = c_dve.snap()
                    P.op("act", lambda e, gi=gi, bi=bi, bT=bT, dst_s=dst_s:
                         e.activation(
                             dst_s[bi][:, gi * 512:(gi + 1) * 512],
                             t1b[gi % 2][:, :], AF.Identity,
                             bias=bT[:, gi:gi + 1]),
                         waits=[fx], inc=[c_act])
                    t1_free[gi % 2] = self.c_act.snap()
                ev_done = self.c_act.snap()
                bank_guard = self.c_dve.snap()
                if which == 0:
                    for gi in range(G):
                        P.op("act", lambda e, gi=gi, bi=bi: e.activation(
                            ksq1[:, :], kst[bi][:, gi * 512:(gi + 1) * 512],
                            AF.Square),
                            waits=[ev_done, c_te], inc=[c_act])
                        P.op("te", lambda e: e.matmul(
                            rep_ps[0:1, 0:512], self.onescol[:, 0:1],
                            ksq1[:, :], start=True, stop=True),
                            waits=[c_act, c_dve], inc=[c_te])
                        P.op("dve", lambda e, gi=gi, ch=ch: e.reduce_max(
                            self.kmax64[0:1, gi * 16 + ch:gi * 16 + ch + 1],
                            rep_ps[0:1, 0:512], axis=AX.X),
                            waits=[c_te], inc=[c_dve])
                if which == 1:
                    prev_vevac[0] = bank_guard
                    xT_free[bi] = mm_done
                for gi in range(G):
                    P.op("sp", lambda e, ch=ch, gi=gi, bi=bi, dst_h=dst_h,
                         dst_s=dst_s: e.dma_start(
                             out=dst_h[gi, :, ch * 512:(ch + 1) * 512],
                             in_=dst_s[bi][:, gi * 512:(gi + 1) * 512]),
                         waits=[ev_done], inc=[c_sp])
                if which == 0:
                    kst_free[bi] = self.c_sp.snap()
                else:
                    vst_free[bi] = self.c_sp.snap()

        prev = None
        for ch in range(T // 512):
            bi = ch % 2
            bufs = bufs2[bi]
            st = st2[bi]

            def src_dma(e, ch=ch, bufs=bufs):
                return e.dma_start(
                    out=bufs["xt"][:, :].rearrange("p (t e) -> p t e", t=4),
                    in_=self.x_full[512 * ch:512 * (ch + 1), :]
                    .rearrange("(t p) e -> p t e", p=128))

            P.op("gp", lambda e, ch=ch, bi=bi: e.dma_start(
                out=xTb[bi][:, :].rearrange("p (c n) -> p c n", c=ECH),
                in_=self.xT_full[:, 512 * ch:512 * (ch + 1)]
                .rearrange("(c p) n -> p c n", p=128)),
                waits=[xT_free[bi]], inc=[c_gp])
            st2[bi]["xT_ld"] = self.c_gp.snap()

            stats_done = self.ln_stats(bufs, src_dma, st)
            if prev is not None:
                pch, pbi = prev
                kv_proj(pch, pbi, st2[pbi]["rep_done"])
            st2[bi]["rep_done"] = rep_build(bi, stats_done)
            prev = (ch, bi)
        pch, pbi = prev
        kv_proj(pch, pbi, st2[pbi]["rep_done"])

        if DEBUG:
            fin = self.c_sp.snap()
            self.dbg_khbm = self.dram_out("dbg_khbm", [G, D, T], BF16)
            self.dbg_vhbm = self.dram_out("dbg_vhbm", [G, D, T], BF16)
            P.op("sp", lambda e: e.dma_start(out=self.dbg_khbm,
                                             in_=self.k_hbm), waits=[fin],
                 inc=[c_sp])
            P.op("sp", lambda e: e.dma_start(out=self.dbg_vhbm,
                                             in_=self.v_hbm), waits=[fin],
                 inc=[c_sp])

    def phase1b(self, es):
        P = self.P
        c_sp, c_gp, c_te, c_act, c_dve = (self.c_sp, self.c_gp, self.c_te,
                                          self.c_act, self.c_dve)
        s = self.sb
        wkb = [s(es, f"wkb2{i}", [128, ECH * 128], BF16) for i in (0, 1)]
        wvb = [s(es, f"wvb2{i}", [128, ECH * 128], BF16) for i in (0, 1)]
        wqb = [s(es, f"wqb{i}", [128, ECH * 128], BF16) for i in (0, 1)]
        bufs = {
            "xt": s(es, "xtb", [128, 4 * E], F32),
            "sqs": s(es, "sqsb", [128, E], F32),
            "stats": s(es, "statsb", [128, 16], F32),
            "msq": s(es, "msqb", [128, 4], F32),
            "var4": s(es, "var4b", [128, 4], F32),
            "xnb": s(es, "xnbb", [128, 4 * E], BF16),
            "tp_ps": self.ps(es, "tp_psb", [128, 1024], BF16),
            "tp_ps2": self.ps(es, "tp_psb2", [128, 1024], BF16),
        }
        mm_ps = [self.ps(es, f"mm_psb{i}", [128, 512]) for i in range(4)]
        vtb = s(es, "vtb", [128, 1024], BF16)
        vt_free = None
        vtp_free = [None, None]

        xnT_view = self.xnT_own[:, :].rearrange("p (c n) -> p c n", c=ECH)
        st = {}
        for ch in range(TOWN // 512):
            def src_dma(e, ch=ch):
                return e.dma_start(
                    out=bufs["xt"][:, :].rearrange("p (t e) -> p t e", t=4),
                    in_=self.x_own[512 * ch:512 * (ch + 1), :]
                    .rearrange("(t p) e -> p t e", p=128))

            def xnT_wr(e, b8, t0, ech0, ch=ch):
                tp = bufs["tp_ps"] if b8 % 2 == 0 else bufs["tp_ps2"]
                return e.tensor_copy(
                    xnT_view[:, ech0:ech0 + 8,
                             ch * 512 + t0 * 128: ch * 512 + (t0 + 1) * 128],
                    tp[:, :].rearrange("p (k n) -> p k n", k=8))

            self.ln_chunk(bufs, src_dma, xnT_wr, st)
        xnT_done = self.c_dve.snap()

        wq_free = [None, None]
        ps_free = [None, None, None, None]
        for h in range(H):
            wi = h % 2
            P.op("gp", lambda e, h=h, wi=wi: e.dma_start(
                out=wqb[wi][:, :].rearrange("p (c d) -> p c d", c=ECH),
                in_=self.wq_in[h].rearrange("(c p) d -> p c d", p=128)),
                waits=[wq_free[wi]], inc=[c_gp])
            wq_ld = c_gp.snap()
            for tc in range(2):
                pi = tc
                for ech in range(ECH):
                    P.op("te", lambda e, wi=wi, ech=ech, tc=tc, pi=pi: e.matmul(
                        mm_ps[pi][:, :],
                        wqb[wi][:, ech * 128:(ech + 1) * 128],
                        xnT_view[:, ech, tc * 512:(tc + 1) * 512],
                        start=(ech == 0), stop=(ech == ECH - 1)),
                        waits=[xnT_done, wq_ld, ps_free[pi]],
                        inc=([c_te] if ech == ECH - 1 else []))
                qmm = self.c_te.snap()
                P.op("act", lambda e, h=h, tc=tc, pi=pi: e.activation(
                    self.q_sb[:, h * TOWN + tc * 512:
                              h * TOWN + (tc + 1) * 512],
                    mm_ps[pi][:, :], AF.Identity, bias=self.bqTb[:, h:h + 1]),
                    waits=[qmm], inc=[c_act])
                ps_free[pi] = self.c_act.snap()
            wq_free[wi] = self.c_act.snap()

        P.op("dve", lambda e: e.memset(
            self.vown[:, :].rearrange("p (q s) -> p q s", s=129)
            [:, :, 128:129], 1.0), inc=[c_dve])
        ones_set = self.c_dve.snap()

        kvw_free = [None, None]
        for gi in range(G):
            wi = gi % 2
            P.op("gp", lambda e, gi=gi, wi=wi: e.dma_start(
                out=wkb[wi][:, :].rearrange("p (c d) -> p c d", c=ECH),
                in_=self.wk_in[gi].rearrange("(c p) d -> p c d", p=128)),
                waits=[kvw_free[wi]], inc=[c_gp])
            P.op("gp", lambda e, gi=gi, wi=wi: e.dma_start(
                out=wvb[wi][:, :].rearrange("p (c d) -> p c d", c=ECH),
                in_=self.wv_in[gi].rearrange("(c p) d -> p c d", p=128)),
                waits=[kvw_free[wi]], inc=[c_gp])
            kv_wload = c_gp.snap()
            for tc in range(2):
                pi = 2 + tc
                for ech in range(ECH):
                    P.op("te", lambda e, wi=wi, tc=tc, ech=ech, pi=pi: e.matmul(
                        mm_ps[pi][:, :],
                        wkb[wi][:, ech * 128:(ech + 1) * 128],
                        xnT_view[:, ech, tc * 512:(tc + 1) * 512],
                        start=(ech == 0), stop=(ech == ECH - 1)),
                        waits=[xnT_done, kv_wload, ps_free[pi]],
                        inc=([c_te] if ech == ECH - 1 else []))
                kmm = self.c_te.snap()
                P.op("act", lambda e, gi=gi, tc=tc, pi=pi: e.activation(
                    self.kown[:, gi * TOWN + tc * 512:
                              gi * TOWN + (tc + 1) * 512],
                    mm_ps[pi][:, :], AF.Identity, bias=self.bkTb[:, gi:gi + 1]),
                    waits=[kmm], inc=[c_act])
                ps_free[pi] = self.c_act.snap()
            for tc in range(2):
                pi = 2 + tc
                for ech in range(ECH):
                    P.op("te", lambda e, wi=wi, tc=tc, ech=ech, pi=pi: e.matmul(
                        mm_ps[pi][:, :],
                        wvb[wi][:, ech * 128:(ech + 1) * 128],
                        xnT_view[:, ech, tc * 512:(tc + 1) * 512],
                        start=(ech == 0), stop=(ech == ECH - 1)),
                        waits=[xnT_done, kv_wload, ps_free[pi]],
                        inc=([c_te] if ech == ECH - 1 else []))
                vmm = self.c_te.snap()
                P.op("act", lambda e, gi=gi, tc=tc, pi=pi: e.activation(
                    vtb[:, tc * 512:(tc + 1) * 512], mm_ps[pi][:, :],
                    AF.Identity, bias=self.bvTb[:, gi:gi + 1]),
                    waits=[vmm, vt_free], inc=[c_act])
                ps_free[pi] = self.c_act.snap()
            vt_done = self.c_act.snap()
            for half in range(2):
                tp = bufs["tp_ps"] if half == 0 else bufs["tp_ps2"]
                for k in range(4):
                    t = half * 4 + k
                    P.op("te", lambda e, t=t, k=k, tp=tp: e.transpose(
                        tp[:, k * 128:(k + 1) * 128],
                        vtb[:, t * 128:(t + 1) * 128], self.identb[:, :]),
                        waits=[vt_done, vtp_free[half]],
                        inc=([c_te] if k == 3 else []))
                tdone = self.c_te.snap()
                P.op("dve", lambda e, gi=gi, half=half, tp=tp: e.tensor_copy(
                    self.vown[:, :].rearrange("p (q s) -> p q s", s=129)
                    [:, gi * JT + half * 4: gi * JT + (half + 1) * 4, 0:128],
                    tp[:, 0:512].rearrange("p (k n) -> p k n", k=4)),
                    waits=[tdone, ones_set], inc=[c_dve])
                vtp_free[half] = self.c_dve.snap()
            vt_free = self.c_dve.snap()
            kvw_free[wi] = self.c_dve.snap()

        if DEBUG:
            fin = [self.c_act.snap(), self.c_dve.snap()]
            self.dbg_xnt = self.dram_out("dbg_xnt", [128, ECH * TOWN], BF16)
            self.dbg_q = self.dram_out("dbg_q", [128, H * TOWN], BF16)
            self.dbg_kown = self.dram_out("dbg_kown", [128, G * TOWN], BF16)
            self.dbg_vown = self.dram_out("dbg_vown", [128, G * JT * 129], BF16)
            for dst, src in [(self.dbg_xnt, self.xnT_own),
                             (self.dbg_q, self.q_sb),
                             (self.dbg_kown, self.kown),
                             (self.dbg_vown, self.vown)]:
                P.op("sp", lambda e, d=dst, s=src: e.dma_start(out=d,
                                                               in_=s[:, :]),
                     waits=fin, inc=[c_sp])

    def phase15(self, es):
        P = self.P
        c_sp, c_gp, c_te, c_act, c_dve = (self.c_sp, self.c_gp, self.c_te,
                                          self.c_act, self.c_dve)
        s = self.sb
        sw1b = s(es, "sw1b", [128, ECH * HID], BF16)
        swgb = s(es, "swgb", [128, ECH * HID], BF16)
        wdb = s(es, "wdb", [128, ECH * LAT], BF16)
        wgb = s(es, "wgb", [128, ECH * NEXP], BF16)
        prob = s(es, "prob", [128, JT * NEXP], F32)
        vbuf = s(es, "vbuf_g", [128, JT * NEXP], F32)
        cnt = [s(es, f"cnt{i}", [128, JT * NEXP], F32) for i in (0, 1)]
        rbuf = s(es, "rbuf", [128, HID], BF16)
        rbuf2 = s(es, "rbuf2", [128, HID], BF16)
        rsum = s(es, "rsum", [128, JT], F32)
        z_ps = self.ps(es, "z_ps", [128, 512])
        h_ps = self.ps(es, "h_ps", [128, 1024])
        g_ps = self.ps(es, "g_ps", [128, 1024])

        xnT_view = self.xnT_own[:, :].rearrange("p (c n) -> p c n", c=ECH)

        P.op("gp", lambda e: e.dma_start(
            out=sw1b[:, :].rearrange("p (c n) -> p c n", c=ECH),
            in_=self.sw1T_in.rearrange("(c p) n -> p c n", p=128)),
            inc=[c_gp])
        P.op("gp", lambda e: e.dma_start(
            out=swgb[:, :].rearrange("p (c n) -> p c n", c=ECH),
            in_=self.swgT_in.rearrange("(c p) n -> p c n", p=128)),
            inc=[c_gp])
        P.op("gp", lambda e: e.dma_start(
            out=wdb[:, :].rearrange("p (c n) -> p c n", c=ECH),
            in_=self.wdT_in.rearrange("(c p) n -> p c n", p=128)),
            inc=[c_gp])
        P.op("gp", lambda e: e.dma_start(
            out=wgb[:, :].rearrange("p (c n) -> p c n", c=ECH),
            in_=self.wgT_in.rearrange("(c p) n -> p c n", p=128)),
            inc=[c_gp])
        wload = c_gp.snap()

        for t in range(JT):
            for ech in range(ECH):
                P.op("te", lambda e, t=t, ech=ech: e.matmul(
                    z_ps[:, t * NEXP:(t + 1) * NEXP],
                    xnT_view[:, ech, t * 128:(t + 1) * 128],
                    wgb[:, ech * NEXP:(ech + 1) * NEXP],
                    start=(ech == 0), stop=(ech == ECH - 1)),
                    waits=[wload],
                    inc=([c_te] if ech == ECH - 1 and t == JT - 1 else []))
        zmm = self.c_te.snap()
        P.op("act", lambda e: e.activation(
            prob[:, :], z_ps[:, 0:JT * NEXP], AF.Exp, scale=-1.0),
            waits=[zmm], inc=[c_act])
        P.op("dve", lambda e: e.tensor_scalar(
            prob[:, :], prob[:, :], 1.0, None, ALU.add),
            waits=[c_act], inc=[c_dve])
        P.op("dve", lambda e: e.reciprocal(prob[:, :], prob[:, :]),
            waits=[c_dve], inc=[c_dve])
        P.op("dve", lambda e: e.tensor_tensor(
            vbuf[:, :], prob[:, :], self.brep8[:, :], ALU.add),
            waits=[c_dve], inc=[c_dve])
        P.op("dve", lambda e: e.memset(cnt[0][:, :], 0.0),
            waits=[c_dve], inc=[c_dve])
        cur = 0
        for ep in range(NEXP):
            for t in range(JT):
                P.op("dve", lambda e, ep=ep, t=t, cur=cur: e.scalar_tensor_tensor(
                    cnt[1 - cur][:, t * NEXP:(t + 1) * NEXP],
                    vbuf[:, t * NEXP:(t + 1) * NEXP],
                    vbuf[:, t * NEXP + ep:t * NEXP + ep + 1],
                    cnt[cur][:, t * NEXP:(t + 1) * NEXP],
                    ALU.is_lt, ALU.add),
                    waits=[c_dve], inc=[c_dve])
            cur = 1 - cur
        P.op("dve", lambda e, cur=cur: e.tensor_scalar(
            cnt[1 - cur][:, :], cnt[cur][:, :], 7.5, None, ALU.is_le),
            waits=[c_dve], inc=[c_dve])
        P.op("dve", lambda e, cur=cur: e.tensor_tensor(
            self.wdense[:, :], prob[:, :], cnt[1 - cur][:, :], ALU.mult),
            waits=[c_dve], inc=[c_dve])
        P.op("dve", lambda e: e.reduce_sum(
            rsum[:, :].rearrange("p (j o) -> p j o", o=1),
            self.wdense[:, :].rearrange("p (j n) -> p j n", j=JT), axis=AX.X),
            waits=[c_dve], inc=[c_dve])
        P.op("dve", lambda e: e.reciprocal(rsum[:, :], rsum[:, :]),
            waits=[c_dve], inc=[c_dve])
        for t in range(JT):
            P.op("dve", lambda e, t=t: e.tensor_scalar(
                self.wdense[:, t * NEXP:(t + 1) * NEXP],
                self.wdense[:, t * NEXP:(t + 1) * NEXP],
                rsum[:, t:t + 1], RSCALE, ALU.mult, ALU.mult),
                waits=[c_dve], inc=[c_dve])

        r_free = None
        for hch in range(8):
            for tc in range(2):
                for ech in range(ECH):
                    P.op("te", lambda e, hch=hch, tc=tc, ech=ech: e.matmul(
                        h_ps[:, tc * 512:(tc + 1) * 512],
                        sw1b[:, ech * HID + hch * 128:
                             ech * HID + (hch + 1) * 128],
                        xnT_view[:, ech, tc * 512:(tc + 1) * 512],
                        start=(ech == 0), stop=(ech == ECH - 1)),
                        waits=[wload, c_dve], inc=[], dd=(ech > 0))
                    P.op("te", lambda e, hch=hch, tc=tc, ech=ech: e.matmul(
                        g_ps[:, tc * 512:(tc + 1) * 512],
                        swgb[:, ech * HID + hch * 128:
                             ech * HID + (hch + 1) * 128],
                        xnT_view[:, ech, tc * 512:(tc + 1) * 512],
                        start=(ech == 0), stop=(ech == ECH - 1)),
                        waits=[wload, c_dve],
                        inc=([c_te] if ech == ECH - 1 and tc == 1 else []),
                        dd=(ech > 0))
            hmm = self.c_te.snap()
            P.op("act", lambda e: e.activation(
                rbuf[:, :], g_ps[:, :], AF.Relu),
                waits=[hmm, r_free], inc=[c_act])
            P.op("dve", lambda e: e.tensor_tensor(
                rbuf2[:, :], rbuf[:, :], rbuf[:, :], ALU.mult),
                waits=[c_act], inc=[c_dve])
            P.op("dve", lambda e, hch=hch: e.tensor_tensor(
                self.hhT[:, hch * TOWN:(hch + 1) * TOWN],
                h_ps[:, :], rbuf2[:, :], ALU.mult),
                waits=[c_dve], inc=[c_dve])
            r_free = self.c_dve.snap()

        for lch in range(4):
            for tc in range(2):
                for ech in range(ECH):
                    P.op("te", lambda e, lch=lch, tc=tc, ech=ech: e.matmul(
                        h_ps[:, tc * 512:(tc + 1) * 512],
                        wdb[:, ech * LAT + lch * 128:
                            ech * LAT + (lch + 1) * 128],
                        xnT_view[:, ech, tc * 512:(tc + 1) * 512],
                        start=(ech == 0), stop=(ech == ECH - 1)),
                        waits=[wload, c_act, c_dve],
                        inc=([c_te] if ech == ECH - 1 and tc == 1 else []),
                        dd=(ech > 0))
            xmm = self.c_te.snap()
            P.op("act", lambda e, lch=lch: e.activation(
                self.xlT[:, lch * TOWN:(lch + 1) * TOWN], h_ps[:, :],
                AF.Identity),
                waits=[xmm], inc=[c_act])

        if DEBUG:
            fin = [self.c_act.snap(), self.c_dve.snap()]
            self.dbg_w = self.dram_out("dbg_w", [128, JT * NEXP], F32)
            self.dbg_hh = self.dram_out("dbg_hh", [128, 8 * TOWN], BF16)
            self.dbg_xl = self.dram_out("dbg_xl", [128, 4 * TOWN], BF16)
            for dst, src in [(self.dbg_w, self.wdense), (self.dbg_hh, self.hhT),
                             (self.dbg_xl, self.xlT)]:
                P.op("sp", lambda e, d=dst, s=src: e.dma_start(out=d,
                                                               in_=s[:, :]),
                     waits=fin, inc=[c_sp])

    def phase2(self, es):
        P = self.P
        c_sp, c_gp, c_te, c_act, c_dve = (self.c_sp, self.c_gp, self.c_te,
                                          self.c_act, self.c_dve)
        s = self.sb
        NS = 2
        kbuf = [s(es, f"kbuf{i}", [128, T], BF16) for i in (0, 1)]
        vbuf = [s(es, f"vbuf{i}", [128, 64 * 144], BF16) for i in (0, 1)]
        pt = [s(es, f"pt{i}", [128, 1024], BF16) for i in range(NS)]
        sst = s(es, "sst", [128, JT * 128], F32)
        qsq = s(es, "qsqb", [128, 4 * TOWN], BF16)
        ksq = s(es, "ksqb", [128, 512], BF16)
        kmaxb = s(es, "kmaxb", [1, 20], F32)
        cbuf = s(es, "cbuf", [1, 16], F32)
        crep4 = s(es, "crep4", [128, 4], F32)
        dbadj4 = s(es, "dbadj4", [128, 4 * JT], F32)
        rcpb = s(es, "rcpb", [128, JT], F32)
        zbuf = s(es, "zbuf", [128, 512], BF16)
        s_ps = [self.ps(es, f"s_ps{i}", [128, 1024]) for i in range(NS)]
        o_ps = [self.ps(es, f"o_ps{i}", [128, 512]) for i in range(3)]
        cp_ps = self.ps(es, "cp_ps", [128, 512])

        def o_tile(j):
            return o_ps[j // 3][:, (j % 3) * 129:(j % 3) * 129 + 129]

        for i in (0, 1):
            P.op("dve", lambda e, i=i: e.memset(
                vbuf[i][:, :].rearrange("p (q s) -> p q s", s=144)
                [:, :, 128:129], 1.0), inc=[c_dve])
        P.op("dve", lambda e: e.memset(zbuf[:, :], 0.0), inc=[c_dve])
        ones_set = c_dve.snap()

        exp_done = [None] * NS
        av_done = [None] * NS
        head_eplg = None
        sst_free = None
        group_done = [None, None]
        slot = 0

        for gi in range(G):
            bi = gi % 2
            P.op("gp", lambda e, gi=gi, bi=bi: e.dma_start(
                out=kbuf[bi][:, :], in_=self.k_hbm[gi]),
                waits=[group_done[bi]], inc=[c_gp])
            for qt in range(64):
                P.op("sp", lambda e, gi=gi, bi=bi, qt=qt: e.dma_start_transpose(
                    vbuf[bi][:, qt * 144:qt * 144 + 128],
                    self.v_hbm[gi, :, qt * 128:(qt + 1) * 128]),
                    waits=([group_done[bi], ones_set] if qt == 0 else []),
                    inc=[c_sp])
            ready_sp = self.c_sp.snap()
            ready_gp = self.c_gp.snap()
            ready = ready_sp

            guard = [w for w in exp_done if w] + [ready_sp, ready_gp]
            P.op("dve", lambda e, gi=gi: e.reduce_max(
                kmaxb[0:1, 0:1], self.kmax64[0:1, gi * 16:(gi + 1) * 16],
                axis=AX.X), waits=[c_dve], inc=[c_dve])
            P.op("act", lambda e: e.activation(
                kmaxb[0:1, 1:2], kmaxb[0:1, 0:1], AF.Ln),
                waits=[c_dve], inc=[c_act])
            lnk_done = c_act.snap()

            P.op("act", lambda e, gi=gi: e.activation(
                qsq[:, :], self.q_sb[:, gi * 4 * TOWN:(gi + 1) * 4 * TOWN],
                AF.Square), waits=[self.c_dve.snap()], inc=[c_act])
            qsq_done = c_act.snap()
            qprev = [None, None]
            for i in range(8):
                sl = i % 2
                P.op("te", lambda e, i=i, sl=sl: e.matmul(
                    s_ps[1][0:1, sl * 512:(sl + 1) * 512],
                    self.onescol[:, 0:1],
                    qsq[:, i * 512:(i + 1) * 512],
                    start=True, stop=True),
                    waits=guard + [qsq_done, qprev[sl]], inc=[c_te])
                mm = self.c_te.snap()
                P.op("dve", lambda e, i=i, sl=sl: e.reduce_max(
                    cbuf[0:1, i:i + 1],
                    s_ps[1][0:1, sl * 512:(sl + 1) * 512], axis=AX.X),
                    waits=[mm], inc=[c_dve])
                qprev[sl] = c_dve.snap()
            P.op("dve", lambda e: e.reduce_max(
                cbuf[0:1, 8:12].rearrange("o (h x) -> o h x", h=4),
                cbuf[0:1, 0:8].rearrange("o (h x) -> o h x", h=4),
                axis=AX.X),
                waits=[qprev[0], qprev[1]], inc=[c_dve])
            P.op("act", lambda e: e.activation(
                cbuf[0:1, 12:16], cbuf[0:1, 8:12], AF.Ln),
                waits=[c_dve], inc=[c_act])
            P.op("dve", lambda e: e.tensor_scalar(
                cbuf[0:1, 8:12], cbuf[0:1, 12:16], kmaxb[0:1, 1:2], None,
                ALU.add), waits=[c_act, lnk_done], inc=[c_dve])
            P.op("act", lambda e: e.activation(
                cbuf[0:1, 12:16], cbuf[0:1, 8:12], AF.Exp, scale=0.5),
                waits=[c_dve], inc=[c_act])
            P.op("dve", lambda e: e.tensor_scalar(
                cbuf[0:1, 8:12], cbuf[0:1, 12:16], -SCALE_ATT, None,
                ALU.mult), waits=[c_act], inc=[c_dve])
            P.op("te", lambda e: e.matmul(
                cp_ps[:, 0:4], self.onesfb[0:1, :], cbuf[0:1, 8:12],
                start=True, stop=True), waits=[c_dve], inc=[c_te])
            P.op("dve", lambda e: e.tensor_copy(crep4[:, :], cp_ps[:, 0:4]),
                 waits=[c_te], inc=[c_dve])
            for hh in range(4):
                P.op("dve", lambda e, hh=hh: e.tensor_scalar(
                    dbadj4[:, hh * JT:(hh + 1) * JT], self.dbiasb[:, :],
                    crep4[:, hh:hh + 1], None, ALU.add),
                    waits=[c_dve], inc=[c_dve])
            cprep = c_dve.snap()

            for hh in range(4):
                h = gi * 4 + hh
                o_free = head_eplg
                qoff = h * TOWN
                for b in range(3):
                    P.op("te", lambda e, b=b: e.matmul(
                        o_ps[b][:, :], self.identb[:, :], zbuf[:, :],
                        start=True, stop=False),
                        waits=[ones_set, o_free, cprep],
                        inc=([c_te] if b == 2 else []))

                def emit_S(jobs, sl, cprep):
                    alln = sum(len(j["s_mms"]) for j in jobs)
                    i = 0
                    for base, jb in zip((0, 512), jobs):
                        for dst_off, w, lf, rf in jb["s_mms"]:
                            i += 1
                            P.op("te", lambda e, dst_off=base + dst_off, w=w,
                                 lf=lf, rf=rf, sl=sl: e.matmul(
                                     s_ps[sl][:, dst_off:dst_off + w],
                                     lf(), rf(), start=True, stop=True),
                                 waits=[ready, cprep, exp_done[sl]],
                                 inc=([c_te] if i == alln else []))
                    return self.c_te.snap()

                def emit_EA(jobs, sl, smm):
                    extra = []
                    ntri = sum(4 for jb in jobs if jb["tri"])
                    it = 0
                    for base, jb in zip((0, 512), jobs):
                        if not jb["tri"]:
                            continue
                        for i4 in range(4):
                            it += 1
                            P.op("dve", lambda e, o=base + i4 * 128, sl=sl:
                                 e.tensor_tensor(
                                     s_ps[sl][:, o:o + 128],
                                     s_ps[sl][:, o:o + 128],
                                     self.trib[:, :], ALU.add),
                                 waits=([smm] if it == 1 else []),
                                 inc=([c_dve] if it == ntri else []))
                    if ntri:
                        extra = [self.c_dve.snap()]
                    width = (512 + jobs[1]["width"]) if len(jobs) == 2 \
                        else jobs[0]["width"]
                    P.op("act", lambda e, sl=sl, width=width,
                         bias_ap=jobs[0]["bias"]: e.activation(
                             pt[sl][:, 0:width], s_ps[sl][:, 0:width],
                             AF.Exp, bias=bias_ap, scale=SCALE_ATT),
                         waits=[smm, av_done[sl]] + extra, inc=[c_act])
                    exp_done[sl] = c_act.snap()
                    na = sum(len(jb["avs"]) for jb in jobs)
                    i = 0
                    for base, jb in zip((0, 512), jobs):
                        for j, pt_off, vf in jb["avs"]:
                            i += 1
                            P.op("te", lambda e, j=j, pt_off=base + pt_off,
                                 vf=vf, sl=sl: e.matmul(
                                     o_tile(j), pt[sl][:, pt_off:pt_off + 128],
                                     vf(), start=False, stop=False),
                                 waits=[exp_done[sl]],
                                 inc=([c_te] if i == na else []))
                    av_done[sl] = self.c_te.snap()

                jl = []
                for kt in range(56):
                    jmin = kt // 8 + 1
                    w_all = (8 - jmin) * 128
                    off = 0
                    while off < w_all:
                        w = min(512, w_all - off)
                        q0 = qoff + jmin * 128 + off
                        jlo = jmin + off // 128
                        lf = (lambda bi=bi, kt=kt:
                              kbuf[bi][:, kt * 128:(kt + 1) * 128])
                        rf = (lambda q0=q0, w=w: self.q_sb[:, q0:q0 + w])
                        avs = [(j, (j - jlo) * 128,
                                (lambda bi=bi, kt=kt:
                                 vbuf[bi][:, kt * 144:kt * 144 + 129]))
                               for j in range(jlo, jlo + w // 128)]
                        jl.append(dict(s_mms=[(0, w, lf, rf)], width=w,
                                       bias=crep4[:, hh:hh + 1], avs=avs,
                                       tri=False, bkey=("c", hh)))
                        off += w
                for dd in range(8):
                    for half in range(2):
                        smms = []
                        avs = []
                        for i in range(4):
                            j = half * 4 + i
                            kk = 8 * j + dd
                            lf = (lambda bi=bi, kk=kk:
                                  kbuf[bi][:, kk * 128:(kk + 1) * 128])
                            q0 = qoff + j * 128
                            rf = (lambda q0=q0: self.q_sb[:, q0:q0 + 128])
                            smms.append((i * 128, 128, lf, rf))
                            avs.append((j, i * 128,
                                        (lambda bi=bi, kk=kk:
                                         vbuf[bi][:, kk * 144:kk * 144 + 129])))
                        jl.append(dict(
                            s_mms=smms, width=512,
                            bias=dbadj4[:, hh * JT + dd:hh * JT + dd + 1],
                            avs=avs, tri=False, bkey=("d", hh, dd)))
                for half in range(2):
                    smms = []
                    avs = []
                    for i in range(4):
                        j = half * 4 + i
                        k0 = gi * TOWN + j * 128
                        lf = (lambda k0=k0: self.kown[:, k0:k0 + 128])
                        q0 = qoff + j * 128
                        rf = (lambda q0=q0: self.q_sb[:, q0:q0 + 128])
                        smms.append((i * 128, 128, lf, rf))
                        avs.append((j, i * 128,
                                    (lambda q=gi * JT + j:
                                     self.vown[:, q * 129:(q + 1) * 129])))
                    jl.append(dict(s_mms=smms, width=512,
                                   bias=crep4[:, hh:hh + 1], avs=avs,
                                   tri=True, bkey=("t", hh)))
                plist = []
                p = 0
                while p < len(jl):
                    if (p + 1 < len(jl)
                            and jl[p]["bkey"] == jl[p + 1]["bkey"]):
                        plist.append([jl[p], jl[p + 1]])
                        p += 2
                    else:
                        plist.append([jl[p]])
                        p += 1
                pend = None
                for pr in plist:
                    sl = slot
                    slot = (slot + 1) % NS
                    smm = emit_S(pr, sl, cprep)
                    if pend is not None:
                        emit_EA(*pend)
                    pend = (pr, sl, smm)
                emit_EA(*pend)
                all_av = self.c_te.snap()

                for j in range(JT):
                    P.op("dve", lambda e, j=j: e.reciprocal(
                        rcpb[:, j:j + 1], o_tile(j)[:, 128:129]),
                        waits=[all_av, c_dve], inc=[c_dve])
                    P.op("dve", lambda e, j=j: e.tensor_scalar(
                        sst[:, j * 128:(j + 1) * 128], o_tile(j)[:, 0:128],
                        rcpb[:, j:j + 1], None, ALU.mult),
                        waits=[c_dve, sst_free], inc=[c_dve])
                head_eplg = self.c_dve.snap()
                P.op("sp", lambda e, h=h: e.dma_start(
                    out=self.s_hbm.rearrange("(j p) e -> p j e", p=128)
                    [:, :, h * 128:(h + 1) * 128],
                    in_=sst[:, :].rearrange("p (j d) -> p j d", j=JT)),
                    waits=[head_eplg], inc=[c_sp])
                sst_free = self.c_sp.snap()
            group_done[bi] = self.c_te.snap()

        if DEBUG:
            fin = self.c_sp.snap()
            self.dbg_s = self.dram_out("dbg_s", [TOWN, E], F32)
            P.op("sp", lambda e: e.dma_start(out=self.dbg_s, in_=self.s_hbm),
                 waits=[fin], inc=[c_sp])

    def phase3(self, es):
        P = self.P
        c_sp, c_gp, c_te, c_act, c_dve = (self.c_sp, self.c_gp, self.c_te,
                                          self.c_act, self.c_dve)
        s = self.sb
        acc = [s(es, f"acc{i}", [128, JT * 512], F32) for i in (0, 1)]

        with ExitStack() as pa:
            rw1b = [self.sb(pa, f"rw1b{i}", [128, 4 * HID], BF16)
                    for i in (0, 1)]
            rwgb = [self.sb(pa, f"rwgb{i}", [128, 4 * HID], BF16)
                    for i in (0, 1)]
            rw2b = [self.sb(pa, f"rw2b{i}", [128, 8 * LAT], BF16)
                    for i in (0, 1)]
            abT = self.sb(pa, "abT", [128, 8 * TOWN], BF16)
            rbu = [self.sb(pa, f"rb3{i}", [128, 512], BF16) for i in (0, 1)]
            rb2u = [self.sb(pa, f"rb23{i}", [128, 512], BF16) for i in (0, 1)]
            ab_ps = [self.ps(pa, f"ab_ps{i}", [128, 1024]) for i in (0, 1)]
            y_ps = [self.ps(pa, f"y_ps{i}", [128, 512]) for i in (0, 1)]
            unit_free = [None, None]
            rb_free = [None, None]

            xlT_view = self.xlT[:, :].rearrange("p (l n) -> p l n", l=4)
            P.op("dve", lambda e: e.memset(acc[0][:, :], 0.0), inc=[c_dve])
            acc_zero = c_dve.snap()

            w_free = [None, None]
            y_free = [None, None]
            for ex in range(NEXP_RUN):
                wi = ex % 2
                P.op("gp", lambda e, ex=ex, wi=wi: e.dma_start(
                    out=rw1b[wi][:, :].rearrange("p (l n) -> p l n", l=4),
                    in_=self.rw1T_in[ex].rearrange("(l p) n -> p l n", p=128)),
                    waits=[w_free[wi]], inc=[c_gp])
                P.op("gp", lambda e, ex=ex, wi=wi: e.dma_start(
                    out=rwgb[wi][:, :].rearrange("p (l n) -> p l n", l=4),
                    in_=self.rwgT_in[ex].rearrange("(l p) n -> p l n", p=128)),
                    waits=[c_gp], inc=[c_gp])
                P.op("gp", lambda e, ex=ex, wi=wi: e.dma_start(
                    out=rw2b[wi][:, :].rearrange("p (hh n) -> p hh n", hh=8),
                    in_=self.rw2T_in[ex].rearrange("(hh p) n -> p hh n", p=128)),
                    waits=[c_gp], inc=[c_gp])
                wld = self.c_gp.snap()
                if DEBUG and ex == 0:
                    self.dbg_rwge = self.dram_out("dbg_rwge", [128, 4 * HID], BF16)
                    P.op("sp", lambda e: e.dma_start(out=self.dbg_rwge,
                                                     in_=rwgb[0][:, :]),
                         waits=[wld], inc=[c_sp])

                for hch in range(8):
                    for tc in range(2):
                        pu = (hch * 2 + tc) % 2
                        for lch in range(4):
                            P.op("te", lambda e, wi=wi, hch=hch, tc=tc,
                                 lch=lch, pu=pu: e.matmul(
                                     ab_ps[pu][:, 0:512],
                                     rw1b[wi][:, lch * HID + hch * 128:
                                              lch * HID + (hch + 1) * 128],
                                     xlT_view[:, lch, tc * 512:(tc + 1) * 512],
                                     start=(lch == 0), stop=(lch == 3)),
                                 waits=[wld, unit_free[pu]], inc=[])
                            P.op("te", lambda e, wi=wi, hch=hch, tc=tc,
                                 lch=lch, pu=pu: e.matmul(
                                     ab_ps[pu][:, 512:1024],
                                     rwgb[wi][:, lch * HID + hch * 128:
                                              lch * HID + (hch + 1) * 128],
                                     xlT_view[:, lch, tc * 512:(tc + 1) * 512],
                                     start=(lch == 0), stop=(lch == 3)),
                                 waits=[wld, unit_free[pu]],
                                 inc=([c_te] if lch == 3 else []))
                        abmm = self.c_te.snap()
                        P.op("act", lambda e, pu=pu: e.activation(
                            rbu[pu][:, :], ab_ps[pu][:, 512:1024], AF.Relu),
                            waits=[abmm, rb_free[pu]], inc=[c_act])
                        P.op("dve", lambda e, pu=pu: e.tensor_tensor(
                            rb2u[pu][:, :], rbu[pu][:, :], rbu[pu][:, :],
                            ALU.mult),
                            waits=[c_act], inc=[c_dve])
                        P.op("dve", lambda e, hch=hch, tc=tc, pu=pu:
                             e.tensor_tensor(
                                 abT[:, hch * TOWN + tc * 512:
                                     hch * TOWN + (tc + 1) * 512],
                                 ab_ps[pu][:, 0:512], rb2u[pu][:, :],
                                 ALU.mult),
                             waits=[c_dve], inc=[c_dve])
                        unit_free[pu] = self.c_dve.snap()
                        rb_free[pu] = unit_free[pu]
                ab_done = self.c_dve.snap()

                prev_acc = acc[ex % 2]
                next_acc = acc[(ex + 1) % 2]
                for t in range(JT):
                    yi = t % 2
                    for hch in range(8):
                        P.op("te", lambda e, wi=wi, t=t, hch=hch, yi=yi:
                             e.matmul(
                                 y_ps[yi][:, :],
                                 abT[:, hch * TOWN + t * 128:
                                     hch * TOWN + (t + 1) * 128],
                                 rw2b[wi][:, hch * LAT:(hch + 1) * LAT],
                                 start=(hch == 0), stop=(hch == 7)),
                             waits=[ab_done, y_free[yi]],
                             inc=([c_te] if hch == 7 else []))
                    ymm = self.c_te.snap()
                    P.op("dve", lambda e, ex=ex, t=t, yi=yi,
                         pa_=prev_acc, na=next_acc: e.scalar_tensor_tensor(
                             na[:, t * 512:(t + 1) * 512],
                             y_ps[yi][:, :],
                             self.wdense[:, t * NEXP + ex:t * NEXP + ex + 1],
                             pa_[:, t * 512:(t + 1) * 512],
                             ALU.mult, ALU.add),
                         waits=[ymm, acc_zero], inc=[c_dve])
                    y_free[yi] = self.c_dve.snap()
                w_free[wi] = self.c_dve.snap()
            acc_done = self.c_dve.snap()
            acc_fin = acc[NEXP_RUN % 2]
            if DEBUG:
                self.dbg_acc = self.dram_out("dbg_acc", [128, JT * 512], F32)
                P.op("sp", lambda e: e.dma_start(out=self.dbg_acc,
                                                 in_=acc_fin[:, :]),
                     waits=[acc_done], inc=[c_sp])
                self.dbg_ab = self.dram_out("dbg_ab", [128, 8 * TOWN], BF16)
                P.op("sp", lambda e: e.dma_start(out=self.dbg_ab,
                                                 in_=abT[:, :]),
                     waits=[acc_done], inc=[c_sp])
                self.dbg_xl2 = self.dram_out("dbg_xl2", [128, 4 * TOWN], BF16)
                P.op("sp", lambda e: e.dma_start(out=self.dbg_xl2,
                                                 in_=self.xlT[:, :]),
                     waits=[acc_done], inc=[c_sp])


        with ExitStack() as pb:
            sw2b = self.sb(pb, "sw2b", [128, 8 * E], BF16)
            wupb = self.sb(pb, "wupb", [128, 4 * E], BF16)
            accb = self.sb(pb, "accb", [128, JT * 512], BF16)
            accT = self.sb(pb, "accT", [128, 4 * TOWN], BF16)
            gbuf = self.sb(pb, "gbuf", [128, E], F32)
            stile = self.sb(pb, "stile", [128, E], F32)
            xtile = self.sb(pb, "xtile", [128, E], F32)
            outst = self.sb(pb, "outst", [128, E], F32)
            o4_ps = self.ps(pb, "o4_ps", [128, 2048])
            tp2_ps = self.ps(pb, "tp2_ps", [128, 512], BF16)

            P.op("gp", lambda e: e.dma_start(
                out=sw2b[:, :].rearrange("p (hh n) -> p hh n", hh=8),
                in_=self.sw2T_in.rearrange("(hh p) n -> p hh n", p=128)),
                waits=[acc_done], inc=[c_gp])
            P.op("gp", lambda e: e.dma_start(
                out=wupb[:, :].rearrange("p (l n) -> p l n", l=4),
                in_=self.wupT_in.rearrange("(l p) n -> p l n", p=128)),
                waits=[acc_done], inc=[c_gp])
            wld2 = self.c_gp.snap()

            P.op("dve", lambda e: e.tensor_copy(accb[:, :], acc_fin[:, :]),
                 waits=[acc_done], inc=[c_dve])
            accb_done = self.c_dve.snap()
            tp_free = None
            for t in range(JT):
                for lch in range(4):
                    P.op("te", lambda e, t=t, lch=lch: e.transpose(
                        tp2_ps[:, lch * 128:(lch + 1) * 128],
                        accb[:, t * 512 + lch * 128: t * 512 + (lch + 1) * 128],
                        self.identb[:, :]),
                        waits=[accb_done, tp_free],
                        inc=([c_te] if lch == 3 else []))
                tmm = self.c_te.snap()
                P.op("dve", lambda e, t=t: e.tensor_copy(
                    accT[:, :].rearrange("p (l n) -> p l n", l=4)
                    [:, :, t * 128:(t + 1) * 128],
                    tp2_ps[:, :].rearrange("p (l n) -> p l n", l=4)),
                    waits=[tmm], inc=[c_dve])
                tp_free = self.c_dve.snap()
            accT_done = self.c_dve.snap()

            o4_free = None
            st_free = acc_done
            for t in range(JT):
                P.op("sp", lambda e, t=t: e.dma_start(
                    out=stile[:, :],
                    in_=self.s_hbm[t * 128:(t + 1) * 128, :]),
                    waits=[st_free], inc=[c_sp])
                P.op("sp", lambda e, t=t: e.dma_start(
                    out=xtile[:, :],
                    in_=self.x_own[t * 128:(t + 1) * 128, :]),
                    waits=[st_free], inc=[c_sp])
                sx_in = self.c_sp.snap()
                for nch in range(4):
                    for hch in range(8):
                        P.op("te", lambda e, t=t, nch=nch, hch=hch: e.matmul(
                            o4_ps[:, nch * 512:(nch + 1) * 512],
                            self.hhT[:, hch * TOWN + t * 128:
                                     hch * TOWN + (t + 1) * 128],
                            sw2b[:, hch * E + nch * 512:
                                 hch * E + (nch + 1) * 512],
                            start=(hch == 0), stop=False),
                            waits=[wld2, o4_free], inc=[])
                    for lch in range(4):
                        P.op("te", lambda e, t=t, nch=nch, lch=lch: e.matmul(
                            o4_ps[:, nch * 512:(nch + 1) * 512],
                            accT[:, lch * TOWN + t * 128:
                                 lch * TOWN + (t + 1) * 128],
                            wupb[:, lch * E + nch * 512:
                                 lch * E + (nch + 1) * 512],
                            start=False, stop=(lch == 3)),
                            waits=[accT_done, wld2, o4_free],
                            inc=([c_te] if lch == 3 and nch == 3 else []))
                omm = self.c_te.snap()
                P.op("act", lambda e: e.activation(
                    gbuf[:, :], o4_ps[:, :], AF.Silu),
                    waits=[omm], inc=[c_act])
                o4_free = self.c_act.snap()
                if DEBUG and t == 0:
                    self.dbg_g = self.dram_out("dbg_g", [128, E], F32)
                    P.op("sp", lambda e: e.dma_start(out=self.dbg_g,
                                                     in_=gbuf[:, :]),
                         waits=[o4_free], inc=[c_sp])
                P.op("dve", lambda e: e.tensor_tensor(
                    outst[:, :], stile[:, :], gbuf[:, :], ALU.mult),
                    waits=[c_act, sx_in, st_free], inc=[c_dve])
                P.op("dve", lambda e: e.tensor_tensor(
                    outst[:, :], outst[:, :], xtile[:, :], ALU.add),
                    waits=[c_dve], inc=[c_dve])
                comb = self.c_dve.snap()
                P.op("sp", lambda e, t=t: e.dma_start(
                    out=self.out[t * 128:(t + 1) * 128, :], in_=outst[:, :]),
                    waits=[comb], inc=[c_sp])
                st_free = self.c_sp.snap()

    def finale(self):
        self.P.op("sp", None, waits=[self.c_sp.snap()])



def nc_sem_stack(nc, ctrs):
    class _S:
        def __enter__(self):
            self.es = ExitStack()
            for c in ctrs:
                c.sem = self.es.enter_context(nc.semaphore(c.name))
            return self

        def __exit__(self, *a):
            self.es.close()
            return False
    return _S()


def kernel(**inputs):
    maps = _stage_inputs(inputs)
    b = Builder()
    nc = b.build()
    res = run_bass_kernel_spmd(nc, maps, core_ids=list(range(NCORES)))
    if DEBUG:
        return res
    return _assemble(res.results)


# version 16
# speedup vs baseline: 1.2635x; 1.1455x over previous
import math
from contextlib import ExitStack

import numpy as np
import ml_dtypes

import concourse.bass as bass
import concourse.mybir as mybir
from concourse.bass_utils import run_bass_kernel_spmd
import concourse.bass_utils as _bass_utils

_orig_run_command = _bass_utils.run_command

def _run_command_ldwopt(cmd, *a, **kw):
    cmd = ["--enable-ldw-opt=true" if c == "--enable-ldw-opt=false" else c
           for c in cmd]
    return _orig_run_command(cmd, *a, **kw)

_bass_utils.run_command = _run_command_ldwopt

F32 = mybir.dt.float32
BF16 = mybir.dt.bfloat16
AF = mybir.ActivationFunctionType
ALU = mybir.AluOpType
AX = mybir.AxisListType
BF = ml_dtypes.bfloat16

NCORES = 8
T, E, H, G, D = 8192, 2048, 16, 4, 128
HID, LAT, NEXP = 1024, 512, 16
TOWN = T // NCORES
NTILE = T // 128
JT = 8
ECH = E // 128
SCALE_ATT = 1.0 / math.sqrt(D)
NEG = -1.0e30
EPS = 1e-5
RSCALE = 2.5

DEBUG = 0
PHASES = 99
NEXP_RUN = NEXP
PIPE_P1 = 1


class Ctr:
    def __init__(self, name, step=1):
        self.name, self.step, self.val, self.sem = name, step, 0, None

    def bump(self):
        self.val += self.step
        return self.val

    def snap(self):
        return (self, self.val)


class Prog:
    ORDER = ("sp", "gp", "te", "act", "dve")
    QNAME = {"sp": "sync", "gp": "gpsimd", "te": "tensor",
             "act": "scalar", "dve": "vector"}

    def __init__(self):
        self.ops = []

    def op(self, eng, fn, waits=(), inc=(), dd=False):
        ws = []
        for w in waits:
            if w is None:
                continue
            ws.append((w, w.val) if isinstance(w, Ctr) else w)
        incs = [(c, c.bump()) for c in inc]
        self.ops.append((eng, fn, ws, incs, dd))

    def barrier(self, ctrs):
        snap = [c.snap() for c in ctrs]
        for eng in self.ORDER:
            self.ops.append((eng, None, snap, [], False))

    def replay(self, block):
        for name in self.ORDER:
            mine = [(fn, ws, incs, dd) for (e, fn, ws, incs, dd) in self.ops
                    if e == name]
            if not mine:
                continue

            def section(eng, mine=mine, name=name):
                hwm = {}
                for fn, ws, incs, dd in mine:
                    for c, v in ws:
                        if v > 0 and (not dd or hwm.get(id(c.sem), 0) < v):
                            eng.wait_ge(c.sem, v)
                            if v > hwm.get(id(c.sem), 0):
                                hwm[id(c.sem)] = v
                    if fn is None:
                        continue
                    inst = fn(eng)
                    for c, _ in incs:
                        inst = inst.then_inc(c.sem, c.step)

            getattr(block, self.QNAME[name])(section)


def _stage_inputs(ins):
    x = np.ascontiguousarray(np.asarray(ins["L0_attn_x"]).reshape(T, E))
    asnp = lambda a: np.ascontiguousarray(np.asarray(a))
    base = {
        "x_full": x,
        "xT_full": asnp(x.T),
        "identf": np.eye(128, dtype=np.float32),
        "sel8": np.ascontiguousarray(
            np.eye(8, dtype=np.float32)[:, :, None]
            * np.ones((1, 1, 128), np.float32)),
        "wq": asnp(ins["L0_attn_Wq"]), "wk": asnp(ins["L0_attn_Wk"]),
        "wv": asnp(ins["L0_attn_Wv"]),
        "bqT": asnp(np.asarray(ins["L0_attn_bq"]).T),
        "bkT": asnp(np.asarray(ins["L0_attn_bk"]).T),
        "bvT": asnp(np.asarray(ins["L0_attn_bv"]).T),
        "sw1T": asnp(np.asarray(ins["L0_ffn_sw1"]).T),
        "swgT": asnp(np.asarray(ins["L0_ffn_swg"]).T),
        "sw2T": asnp(np.asarray(ins["L0_ffn_sw2"]).T),
        "wdT": asnp(np.asarray(ins["L0_ffn_Wdown"]).T),
        "wgT": asnp(np.asarray(ins["L0_ffn_Wg"]).T),
        "wupT": asnp(np.asarray(ins["L0_ffn_Wup"]).T),
        "rw1T": asnp(np.asarray(ins["L0_ffn_rw1"]).transpose(0, 2, 1)),
        "rwgT": asnp(np.asarray(ins["L0_ffn_rwg"]).transpose(0, 2, 1)),
        "rw2T": asnp(np.asarray(ins["L0_ffn_rw2"]).transpose(0, 2, 1)),
        "btrow": asnp(np.tile(np.asarray(ins["L0_ffn_biases"]), JT)[None, :]),
        "ident": np.eye(128, dtype=BF),
        "tri": np.where(np.arange(128)[:, None] > np.arange(128)[None, :],
                        np.float32(NEG), np.float32(0.0)).astype(np.float32),
        "eps": np.full((128, 1), EPS, np.float32),
        "onesb": np.ones((1, 128), BF),
        "onesf": np.ones((1, 128), np.float32),
    }
    xt = x.reshape(NTILE, 128, E)
    maps = []
    for c in range(NCORES):
        m = dict(base)
        m["x_own"] = np.ascontiguousarray(xt[c::8].reshape(TOWN, E))
        db = np.zeros((128, JT), np.float32)
        db[:, c:] = NEG
        m["dbias"] = db
        maps.append(m)
    return maps


def _assemble(results):
    out = np.empty((T, E), np.float32)
    ot = out.reshape(NTILE, 128, E)
    for c in range(NCORES):
        ot[c::8] = np.asarray(results[c]["out"]).reshape(JT, 128, E)
    return out.reshape(1, T, E)


class Builder:
    def __init__(self):
        self.nc = bass.Bass("TRN2", target_bir_lowering=False)
        self.P = Prog()
        self.c_sp = Ctr("c_sp", 16)
        self.c_gp = Ctr("c_gp", 16)
        self.c_te = Ctr("c_te", 1)
        self.c_act = Ctr("c_act", 1)
        self.c_dve = Ctr("c_dve", 1)
        self.ctrs = [self.c_sp, self.c_gp, self.c_te, self.c_act, self.c_dve]

    def dram_in(self, name, shape, dtype=F32):
        return self.nc.dram_tensor(name, shape, dtype, kind="ExternalInput").ap()

    def dram_out(self, name, shape, dtype=F32):
        return self.nc.dram_tensor(name, shape, dtype, kind="ExternalOutput").ap()

    def declare(self):
        di = self.dram_in
        self.x_full = di("x_full", [T, E])
        self.x_own = di("x_own", [TOWN, E])
        self.xT_full = di("xT_full", [E, T])
        self.identf_in = di("identf", [128, 128])
        self.sel8_in = di("sel8", [8, 8, 128])
        self.wq_in = di("wq", [H, E, D])
        self.wk_in = di("wk", [G, E, D])
        self.wv_in = di("wv", [G, E, D])
        self.bqT_in = di("bqT", [D, H])
        self.bkT_in = di("bkT", [D, G])
        self.bvT_in = di("bvT", [D, G])
        self.sw1T_in = di("sw1T", [E, HID])
        self.swgT_in = di("swgT", [E, HID])
        self.sw2T_in = di("sw2T", [HID, E])
        self.wdT_in = di("wdT", [E, LAT])
        self.wgT_in = di("wgT", [E, NEXP])
        self.wupT_in = di("wupT", [LAT, E])
        self.rw1T_in = di("rw1T", [NEXP, LAT, HID])
        self.rwgT_in = di("rwgT", [NEXP, LAT, HID])
        self.rw2T_in = di("rw2T", [NEXP, HID, LAT])
        self.btrow_in = di("btrow", [1, JT * NEXP])
        self.ident_in = di("ident", [128, 128], BF16)
        self.tri_in = di("tri", [128, 128])
        self.eps_in = di("eps", [128, 1])
        self.onesb_in = di("onesb", [1, 128], BF16)
        self.onesf_in = di("onesf", [1, 128])
        self.dbias_in = di("dbias", [128, JT])
        self.out = self.dram_out("out", [TOWN, E])
        self.k_hbm = self.nc.dram_tensor("k_hbm", [G, D, T], BF16).ap()
        self.v_hbm = self.nc.dram_tensor("v_hbm", [G, D, T], BF16).ap()
        self.s_hbm = self.nc.dram_tensor("s_hbm", [TOWN, E], F32).ap()

    def sb(self, es, name, shape, dtype):
        return es.enter_context(self.nc.sbuf_tensor(name, shape, dtype))

    def ps(self, es, name, shape, dtype=F32):
        return es.enter_context(self.nc.psum_tensor(name, shape, dtype))

    def bar(self):
        self.P.barrier(self.ctrs)

    def ln_stats(self, bufs, src_dma_fn, st):
        P = self.P
        c_sp, c_act, c_dve = self.c_sp, self.c_act, self.c_dve
        xt, sqs, stats = bufs["xt"], bufs["sqs"], bufs["stats"]
        msq, var4 = bufs["msq"], bufs["var4"]
        P.op("sp", src_dma_fn, waits=[st.get("xt_free")], inc=[c_sp])
        xin = c_sp.snap()
        for t in range(4):
            P.op("dve", lambda e, t=t, xt=xt, stats=stats: e.reduce_sum(
                stats[:, t:t + 1], xt[:, t * E:(t + 1) * E], axis=AX.X),
                waits=[xin, st.get("stats_free")], inc=[c_dve])
            P.op("act", lambda e, t=t, xt=xt, sqs=sqs, stats=stats:
                 e.activation(
                     sqs[:, 0:E], xt[:, t * E:(t + 1) * E], AF.Square,
                     accum_out=stats[:, 4 + t:5 + t]),
                 waits=[xin, st.get("stats_free")], inc=[c_act])
        sums_done = c_dve.snap()
        sqs_done = c_act.snap()
        P.op("dve", lambda e, stats=stats: e.tensor_scalar(
            stats[:, 8:12], stats[:, 0:4], 1.0 / E, None, ALU.mult),
            waits=[sums_done], inc=[c_dve])
        P.op("dve", lambda e, stats=stats, msq=msq: e.tensor_tensor(
            msq[:, :], stats[:, 8:12], stats[:, 8:12], ALU.mult),
            waits=[c_dve], inc=[c_dve])
        P.op("dve", lambda e, stats=stats, msq=msq, var4=var4:
             e.scalar_tensor_tensor(
                 var4[:, :], stats[:, 4:8], 1.0 / E, msq[:, :],
                 ALU.mult, ALU.subtract),
             waits=[sqs_done, c_dve], inc=[c_dve])
        P.op("act", lambda e, var4=var4: e.activation(
            var4[:, :], var4[:, :], AF.Ln, bias=self.epsb[:, 0:1]),
            waits=[c_dve], inc=[c_act])
        P.op("act", lambda e, var4=var4, stats=stats: e.activation(
            stats[:, 12:16], var4[:, :], AF.Exp, scale=-0.5),
            waits=[c_act], inc=[c_act])
        rstd_done = c_act.snap()
        P.op("dve", lambda e, stats=stats: e.scalar_tensor_tensor(
            stats[:, 16:20], stats[:, 8:12], -1.0, stats[:, 12:16],
            ALU.mult, ALU.mult),
            waits=[rstd_done], inc=[c_dve])
        st["xt_free"] = self.c_dve.snap()
        return self.c_dve.snap()

    def ln_chunk(self, bufs, src_dma_fn, xnT_write_fn, st):
        P = self.P
        c_sp, c_te, c_act, c_dve = self.c_sp, self.c_te, self.c_act, self.c_dve
        xt, sqs, stats = bufs["xt"], bufs["sqs"], bufs["stats"]
        msq, var4, xnb, tp_ps = (bufs["msq"], bufs["var4"], bufs["xnb"],
                                 bufs["tp_ps"])
        P.op("sp", src_dma_fn, waits=[st.get("xt_free")], inc=[c_sp])
        xin = c_sp.snap()
        for t in range(4):
            P.op("dve", lambda e, t=t: e.reduce_sum(
                stats[:, t:t + 1], xt[:, t * E:(t + 1) * E], axis=AX.X),
                waits=[xin, st.get("stats_free")], inc=[c_dve])
            P.op("act", lambda e, t=t: e.activation(
                sqs[:, 0:E],
                xt[:, t * E:(t + 1) * E], AF.Square,
                accum_out=stats[:, 4 + t:5 + t]),
                waits=[xin, st.get("stats_free")],
                inc=[c_act])
        sums_done = c_dve.snap()
        sqs_done = c_act.snap()
        P.op("dve", lambda e: e.tensor_scalar(
            stats[:, 8:12], stats[:, 0:4], 1.0 / E, None, ALU.mult),
            waits=[sums_done], inc=[c_dve])
        P.op("dve", lambda e: e.tensor_tensor(
            msq[:, :], stats[:, 8:12], stats[:, 8:12], ALU.mult),
            waits=[c_dve], inc=[c_dve])
        P.op("dve", lambda e: e.scalar_tensor_tensor(
            var4[:, :], stats[:, 4:8], 1.0 / E, msq[:, :],
            ALU.mult, ALU.subtract),
            waits=[sqs_done, c_dve], inc=[c_dve])
        P.op("act", lambda e: e.activation(
            var4[:, :], var4[:, :], AF.Ln, bias=self.epsb[:, 0:1]),
            waits=[c_dve], inc=[c_act])
        P.op("act", lambda e: e.activation(
            stats[:, 12:16], var4[:, :], AF.Exp, scale=-0.5),
            waits=[c_act], inc=[c_act])
        rstd_done = c_act.snap()
        for t in range(4):
            P.op("dve", lambda e, t=t: e.tensor_scalar(
                xnb[:, t * E:(t + 1) * E], xt[:, t * E:(t + 1) * E],
                stats[:, 8 + t:9 + t], stats[:, 12 + t:13 + t],
                ALU.subtract, ALU.mult),
                waits=[rstd_done], inc=[c_dve])
        xnb_done = c_dve.snap()
        st["xt_free"] = xnb_done
        st["stats_free"] = xnb_done
        tp_ps2 = bufs["tp_ps2"]
        for b8 in range(8):
            tp = tp_ps if b8 % 2 == 0 else tp_ps2
            for k in range(8):
                idx = b8 * 8 + k
                t, ech = idx // ECH, idx % ECH
                P.op("te", lambda e, t=t, ech=ech, k=k, tp=tp: e.transpose(
                    tp[:, k * 128:(k + 1) * 128],
                    xnb[:, t * E + ech * 128: t * E + (ech + 1) * 128],
                    self.identb[:, :]),
                    waits=[xnb_done, st.get(f"tp_free{b8 % 2}")],
                    inc=([c_te] if k == 7 else []))
            tp_done = c_te.snap()
            t0, ech0 = (b8 * 8) // ECH, (b8 * 8) % ECH
            P.op("dve", lambda e, b8=b8, t0=t0, ech0=ech0:
                 xnT_write_fn(e, b8, t0, ech0),
                 waits=[tp_done, st.get("xnT_free")], inc=[c_dve])
            st[f"tp_free{b8 % 2}"] = c_dve.snap()
        return c_dve.snap()

    def build(self):
        self.declare()
        with ExitStack() as outer:
            with nc_sem_stack(self.nc, self.ctrs):
                s = self.sb
                self.identb = s(outer, "identb", [128, 128], BF16)
                self.identf = s(outer, "identf_b", [128, 128], F32)
                self.sel8b = s(outer, "sel8b", [8, 8 * 128], F32)
                self.trib = s(outer, "trib", [128, 128], F32)
                self.epsb = s(outer, "epsb", [128, 1], F32)
                self.onesbb = s(outer, "onesbb", [1, 128], BF16)
                self.onesfb = s(outer, "onesfb", [1, 128], F32)
                self.onescol = s(outer, "onescol", [128, 1], BF16)
                self.dbiasb = s(outer, "dbiasb", [128, JT], F32)
                self.bqTb = s(outer, "bqTb", [128, H], F32)
                self.bkTb = s(outer, "bkTb", [128, G], F32)
                self.bvTb = s(outer, "bvTb", [128, G], F32)
                self.brep8 = s(outer, "brep8", [128, JT * NEXP], F32)
                self.wdense = s(outer, "wdense", [128, JT * NEXP], F32)
                self.kmax64 = s(outer, "kmax64", [1, G * 16], F32)

                with self.nc.Block() as block:
                    with ExitStack() as p0:
                        self.phase0(p0)
                    self.bar()
                    if PHASES >= 1:
                        with ExitStack() as p1:
                            self.phase1(p1)
                        self.bar()
                    if PHASES >= 2:
                        self.hhT = s(outer, "hhT", [128, 8 * TOWN], BF16)
                        self.xlT = s(outer, "xlT", [128, 4 * TOWN], BF16)
                        with ExitStack() as mid:
                            self.xnT_own = s(mid, "xnT_own",
                                             [128, ECH * TOWN], BF16)
                            self.q_sb = s(mid, "q_sb", [128, H * TOWN], BF16)
                            self.kown = s(mid, "kown", [128, G * TOWN], BF16)
                            self.vown = s(mid, "vown", [128, G * JT * 129],
                                          BF16)
                            with ExitStack() as p1b:
                                self.phase1b(p1b)
                            self.bar()
                            if PHASES >= 3:
                                with ExitStack() as p15:
                                    self.phase15(p15)
                                self.bar()
                            if PHASES >= 4:
                                with ExitStack() as p2:
                                    self.phase2(p2)
                                self.bar()
                        if PHASES >= 5:
                            with ExitStack() as p3:
                                self.phase3(p3)
                            self.bar()
                    self.finale()
                    self.P.replay(block)
        return self.nc

    def phase0(self, es):
        P = self.P
        c_sp, c_te, c_dve = self.c_sp, self.c_te, self.c_dve
        for dst, src in [
            (self.identb, self.ident_in), (self.trib, self.tri_in),
            (self.identf, self.identf_in),
            (self.epsb, self.eps_in), (self.onesbb, self.onesb_in), (self.onesfb, self.onesf_in),
            (self.dbiasb, self.dbias_in), (self.bqTb, self.bqT_in),
            (self.bkTb, self.bkT_in), (self.bvTb, self.bvT_in),
        ]:
            P.op("sp", lambda e, d=dst, s=src: e.dma_start(out=d[:, :], in_=s),
                 inc=[c_sp])
        btrow = self.sb(es, "btrow_s", [1, JT * NEXP], F32)
        P.op("sp", lambda e: e.dma_start(out=btrow[:, :], in_=self.btrow_in),
             inc=[c_sp])
        P.op("sp", lambda e: e.dma_start(
            out=self.sel8b[:, :].rearrange("p (c n) -> p c n", c=8),
            in_=self.sel8_in), inc=[c_sp])
        ld = c_sp.snap()
        P.op("dve", lambda e: e.memset(self.onescol[:, :], 1.0), inc=[c_dve])
        rep_ps = self.ps(es, "rep_ps", [128, JT * NEXP])
        P.op("te", lambda e: e.matmul(rep_ps[:, 0:JT * NEXP],
                                      self.onesfb[0:1, :], btrow[0:1, :],
                                      start=True, stop=True),
             waits=[ld], inc=[c_te])
        mm2 = self.c_te.snap()
        P.op("dve", lambda e: e.tensor_copy(self.brep8[:, :],
                                            rep_ps[:, 0:JT * NEXP]),
             waits=[mm2], inc=[c_dve])

    def phase1(self, es):
        P = self.P
        c_sp, c_gp, c_te, c_act, c_dve = (self.c_sp, self.c_gp, self.c_te,
                                          self.c_act, self.c_dve)
        s = self.sb
        wkb = s(es, "wkb", [128, G * ECH * 128], BF16)
        wvb = s(es, "wvb", [128, G * ECH * 128], BF16)
        bufs2 = []
        for i in (0, 1):
            bufs2.append({
                "xt": s(es, f"xt{i}", [128, 4 * E], F32),
                "sqs": s(es, f"sqs{i}", [128, E], F32),
                "stats": s(es, f"stats{i}", [128, 24], F32),
                "msq": s(es, f"msq{i}", [128, 4], F32),
                "var4": s(es, f"var4{i}", [128, 4], F32),
            })
        xTb = [s(es, f"xTb{i}", [128, ECH * 512], BF16) for i in (0, 1)]
        statT = s(es, "statT", [8, 128], F32)
        reps = [s(es, f"reps{i}", [128, 1024], F32) for i in (0, 1)]
        t1b = [s(es, f"t1b{i}", [128, 512], F32) for i in (0, 1)]
        kst = [s(es, f"kst{i}", [128, G * 512], BF16) for i in (0, 1)]
        vst = [s(es, f"vst{i}", [128, G * 512], BF16) for i in (0, 1)]
        ukv = s(es, "ukv", [128, 2 * G], F32)
        ksq1 = s(es, "ksq1", [128, 512], BF16)
        kv_ps = [self.ps(es, f"kv_ps{i}", [128, 512]) for i in range(4)]
        st_ps = self.ps(es, "st_ps", [128, 128])
        rep_ps = self.ps(es, "rep_ps1", [128, 1024])

        for gi in range(G):
            P.op("gp", lambda e, gi=gi: e.dma_start(
                out=wkb[:, gi * ECH * 128:(gi + 1) * ECH * 128]
                .rearrange("p (c d) -> p c d", c=ECH),
                in_=self.wk_in[gi].rearrange("(c p) d -> p c d", p=128)),
                inc=[c_gp])
            P.op("gp", lambda e, gi=gi: e.dma_start(
                out=wvb[:, gi * ECH * 128:(gi + 1) * ECH * 128]
                .rearrange("p (c d) -> p c d", c=ECH),
                in_=self.wv_in[gi].rearrange("(c p) d -> p c d", p=128)),
                inc=[c_gp])
        wload = c_gp.snap()

        for gi in range(G):
            for which, wb in ((0, wkb), (1, wvb)):
                for ech in range(ECH):
                    P.op("te", lambda e, gi=gi, which=which, wb=wb, ech=ech:
                         e.matmul(
                             st_ps[:, which:which + 1],
                             wb[:, (gi * ECH + ech) * 128:
                                (gi * ECH + ech + 1) * 128],
                             self.onescol[:, 0:1],
                             start=(ech == 0), stop=(ech == ECH - 1)),
                         waits=[wload],
                         inc=([c_te] if ech == ECH - 1 else []),
                         dd=(ech > 0))
                mm = self.c_te.snap()
                P.op("dve", lambda e, gi=gi, which=which: e.tensor_copy(
                    ukv[:, which * G + gi:which * G + gi + 1],
                    st_ps[:, which:which + 1]),
                    waits=[mm], inc=[c_dve])
        u_done = c_dve.snap()

        st2 = [{}, {}]
        kst_free = [None, None]
        vst_free = [None, None]
        rep_free = [None, None]
        t1_free = [None, None]
        xT_free = [None, None]
        prev_vevac = [None]
        statT_free = [None]

        stt_prev = [None]
        rep_ps_free = [None]

        def rep_build(bi, stats_done):
            stats = bufs2[bi]["stats"]
            P.op("te", lambda e, stats=stats: e.transpose(
                st_ps[0:8, 0:128], stats[:, 12:20], self.identf[:, :]),
                waits=[stats_done, stt_prev[0], u_done, c_dve], inc=[c_te])
            P.op("dve", lambda e: e.tensor_copy(statT[:, :],
                                                st_ps[0:8, 0:128]),
                 waits=[c_te], inc=[c_dve])
            stt = c_dve.snap()
            stt_prev[0] = stt
            st2[bi]["stats_free"] = stt
            for c in range(8):
                dst = (c % 4) * 128 + (0 if c < 4 else 512)
                P.op("te", lambda e, c=c, dst=dst: e.matmul(
                    rep_ps[:, dst:dst + 128],
                    self.sel8b[:, c * 128:(c + 1) * 128],
                    statT[:, :], start=True, stop=True),
                    waits=[stt, rep_ps_free[0], rep_free[bi], c_dve],
                    inc=([c_te] if c == 7 else []))
            selmm = self.c_te.snap()
            P.op("dve", lambda e, bi=bi: e.tensor_copy(
                reps[bi][:, :], rep_ps[:, :]),
                waits=[selmm], inc=[c_dve])
            rep_ps_free[0] = self.c_dve.snap()
            return rep_ps_free[0]

        def kv_proj(ch, bi, rep_done):
            bank_guard = prev_vevac[0]
            for which, wb, ust, bT, dst_h, dst_s in (
                (0, wkb, 0, self.bkTb, self.k_hbm, kst),
                (1, wvb, G, self.bvTb, self.v_hbm, vst),
            ):
                for gi in range(G):
                    for ech in range(ECH):
                        P.op("te", lambda e, gi=gi, ech=ech, bi=bi, wb=wb:
                             e.matmul(
                                 kv_ps[gi][:, :],
                                 wb[:, (gi * ECH + ech) * 128:
                                    (gi * ECH + ech + 1) * 128],
                                 xTb[bi][:, ech * 512:(ech + 1) * 512],
                                 start=(ech == 0), stop=(ech == ECH - 1)),
                             waits=[st2[bi]["xT_ld"], wload, kst_free[bi],
                                    vst_free[bi], bank_guard],
                             inc=([c_te] if ech == ECH - 1 else []),
                             dd=(ech > 0))
                mm_done = self.c_te.snap()
                for gi in range(G):
                    tf = t1_free[gi % 2]
                    P.op("dve", lambda e, gi=gi, bi=bi: e.tensor_tensor(
                        t1b[gi % 2][:, :], kv_ps[gi][:, :],
                        reps[bi][:, 0:512], ALU.mult),
                        waits=[mm_done, rep_done, u_done, tf], inc=[c_dve])
                    P.op("dve", lambda e, gi=gi, bi=bi, ust=ust:
                         e.scalar_tensor_tensor(
                             t1b[gi % 2][:, :], reps[bi][:, 512:1024],
                             ukv[:, ust + gi:ust + gi + 1],
                             t1b[gi % 2][:, :], ALU.mult, ALU.add),
                         waits=[c_dve], inc=[c_dve])
                    fx = c_dve.snap()
                    P.op("act", lambda e, gi=gi, bi=bi, bT=bT, dst_s=dst_s:
                         e.activation(
                             dst_s[bi][:, gi * 512:(gi + 1) * 512],
                             t1b[gi % 2][:, :], AF.Identity,
                             bias=bT[:, gi:gi + 1]),
                         waits=[fx], inc=[c_act])
                    t1_free[gi % 2] = self.c_act.snap()
                ev_done = self.c_act.snap()
                bank_guard = self.c_dve.snap()
                if which == 0:
                    for gi in range(G):
                        P.op("act", lambda e, gi=gi, bi=bi: e.activation(
                            ksq1[:, :], kst[bi][:, gi * 512:(gi + 1) * 512],
                            AF.Square),
                            waits=[ev_done, c_te], inc=[c_act])
                        P.op("te", lambda e: e.matmul(
                            rep_ps[0:1, 0:512], self.onescol[:, 0:1],
                            ksq1[:, :], start=True, stop=True),
                            waits=[c_act, c_dve], inc=[c_te])
                        P.op("dve", lambda e, gi=gi, ch=ch: e.reduce_max(
                            self.kmax64[0:1, gi * 16 + ch:gi * 16 + ch + 1],
                            rep_ps[0:1, 0:512], axis=AX.X),
                            waits=[c_te], inc=[c_dve])
                if which == 1:
                    prev_vevac[0] = bank_guard
                    xT_free[bi] = mm_done
                for gi in range(G):
                    P.op("sp", lambda e, ch=ch, gi=gi, bi=bi, dst_h=dst_h,
                         dst_s=dst_s: e.dma_start(
                             out=dst_h[gi, :, ch * 512:(ch + 1) * 512],
                             in_=dst_s[bi][:, gi * 512:(gi + 1) * 512]),
                         waits=[ev_done], inc=[c_sp])
                if which == 0:
                    kst_free[bi] = self.c_sp.snap()
                else:
                    vst_free[bi] = self.c_sp.snap()

        prev = None
        for ch in range(T // 512):
            bi = ch % 2
            bufs = bufs2[bi]
            st = st2[bi]

            def src_dma(e, ch=ch, bufs=bufs):
                return e.dma_start(
                    out=bufs["xt"][:, :].rearrange("p (t e) -> p t e", t=4),
                    in_=self.x_full[512 * ch:512 * (ch + 1), :]
                    .rearrange("(t p) e -> p t e", p=128))

            P.op("gp", lambda e, ch=ch, bi=bi: e.dma_start(
                out=xTb[bi][:, :].rearrange("p (c n) -> p c n", c=ECH),
                in_=self.xT_full[:, 512 * ch:512 * (ch + 1)]
                .rearrange("(c p) n -> p c n", p=128)),
                waits=[xT_free[bi]], inc=[c_gp])
            st2[bi]["xT_ld"] = self.c_gp.snap()

            stats_done = self.ln_stats(bufs, src_dma, st)
            if prev is not None:
                pch, pbi = prev
                kv_proj(pch, pbi, st2[pbi]["rep_done"])
            st2[bi]["rep_done"] = rep_build(bi, stats_done)
            prev = (ch, bi)
        pch, pbi = prev
        kv_proj(pch, pbi, st2[pbi]["rep_done"])

        if DEBUG:
            fin = self.c_sp.snap()
            self.dbg_khbm = self.dram_out("dbg_khbm", [G, D, T], BF16)
            self.dbg_vhbm = self.dram_out("dbg_vhbm", [G, D, T], BF16)
            P.op("sp", lambda e: e.dma_start(out=self.dbg_khbm,
                                             in_=self.k_hbm), waits=[fin],
                 inc=[c_sp])
            P.op("sp", lambda e: e.dma_start(out=self.dbg_vhbm,
                                             in_=self.v_hbm), waits=[fin],
                 inc=[c_sp])

    def phase1b(self, es):
        P = self.P
        c_sp, c_gp, c_te, c_act, c_dve = (self.c_sp, self.c_gp, self.c_te,
                                          self.c_act, self.c_dve)
        s = self.sb
        wkb = [s(es, f"wkb2{i}", [128, ECH * 128], BF16) for i in (0, 1)]
        wvb = [s(es, f"wvb2{i}", [128, ECH * 128], BF16) for i in (0, 1)]
        wqb = [s(es, f"wqb{i}", [128, ECH * 128], BF16) for i in (0, 1)]
        bufs = {
            "xt": s(es, "xtb", [128, 4 * E], F32),
            "sqs": s(es, "sqsb", [128, E], F32),
            "stats": s(es, "statsb", [128, 16], F32),
            "msq": s(es, "msqb", [128, 4], F32),
            "var4": s(es, "var4b", [128, 4], F32),
            "xnb": s(es, "xnbb", [128, 4 * E], BF16),
            "tp_ps": self.ps(es, "tp_psb", [128, 1024], BF16),
            "tp_ps2": self.ps(es, "tp_psb2", [128, 1024], BF16),
        }
        mm_ps = [self.ps(es, f"mm_psb{i}", [128, 512]) for i in range(4)]
        vtb = s(es, "vtb", [128, 1024], BF16)
        vt_free = None
        vtp_free = [None, None]

        xnT_view = self.xnT_own[:, :].rearrange("p (c n) -> p c n", c=ECH)
        st = {}
        for ch in range(TOWN // 512):
            def src_dma(e, ch=ch):
                return e.dma_start(
                    out=bufs["xt"][:, :].rearrange("p (t e) -> p t e", t=4),
                    in_=self.x_own[512 * ch:512 * (ch + 1), :]
                    .rearrange("(t p) e -> p t e", p=128))

            def xnT_wr(e, b8, t0, ech0, ch=ch):
                tp = bufs["tp_ps"] if b8 % 2 == 0 else bufs["tp_ps2"]
                return e.tensor_copy(
                    xnT_view[:, ech0:ech0 + 8,
                             ch * 512 + t0 * 128: ch * 512 + (t0 + 1) * 128],
                    tp[:, :].rearrange("p (k n) -> p k n", k=8))

            self.ln_chunk(bufs, src_dma, xnT_wr, st)
        xnT_done = self.c_dve.snap()

        wq_free = [None, None]
        ps_free = [None, None, None, None]

        def issue_wq(h):
            wi = h % 2
            P.op("gp", lambda e, h=h, wi=wi: e.dma_start(
                out=wqb[wi][:, :].rearrange("p (c d) -> p c d", c=ECH),
                in_=self.wq_in[h].rearrange("(c p) d -> p c d", p=128)),
                waits=[wq_free[wi]], inc=[c_gp])
            return c_gp.snap()

        wq_ld = issue_wq(0)
        for h in range(H):
            wi = h % 2
            wq_next = issue_wq(h + 1) if h + 1 < H else None
            for tc in range(2):
                pi = tc
                for ech in range(ECH):
                    P.op("te", lambda e, wi=wi, ech=ech, tc=tc, pi=pi: e.matmul(
                        mm_ps[pi][:, :],
                        wqb[wi][:, ech * 128:(ech + 1) * 128],
                        xnT_view[:, ech, tc * 512:(tc + 1) * 512],
                        start=(ech == 0), stop=(ech == ECH - 1)),
                        waits=([xnT_done, wq_ld, ps_free[pi]]
                               + ([wq_next] if tc == 1 and ech == 0 else [])),
                        inc=([c_te] if ech == ECH - 1 else []),
                        dd=(h >= 1))
                qmm = self.c_te.snap()
                P.op("act", lambda e, h=h, tc=tc, pi=pi: e.activation(
                    self.q_sb[:, h * TOWN + tc * 512:
                              h * TOWN + (tc + 1) * 512],
                    mm_ps[pi][:, :], AF.Identity, bias=self.bqTb[:, h:h + 1]),
                    waits=[qmm], inc=[c_act])
                ps_free[pi] = self.c_act.snap()
            wq_free[wi] = self.c_act.snap()
            wq_ld = wq_next

        P.op("dve", lambda e: e.memset(
            self.vown[:, :].rearrange("p (q s) -> p q s", s=129)
            [:, :, 128:129], 1.0), inc=[c_dve])
        ones_set = self.c_dve.snap()

        kvw_free = [None, None]

        def issue_kv(gi):
            wi = gi % 2
            P.op("gp", lambda e, gi=gi, wi=wi: e.dma_start(
                out=wkb[wi][:, :].rearrange("p (c d) -> p c d", c=ECH),
                in_=self.wk_in[gi].rearrange("(c p) d -> p c d", p=128)),
                waits=[kvw_free[wi]], inc=[c_gp])
            P.op("gp", lambda e, gi=gi, wi=wi: e.dma_start(
                out=wvb[wi][:, :].rearrange("p (c d) -> p c d", c=ECH),
                in_=self.wv_in[gi].rearrange("(c p) d -> p c d", p=128)),
                waits=[kvw_free[wi]], inc=[c_gp])
            return c_gp.snap()

        kv_wload = issue_kv(0)
        for gi in range(G):
            wi = gi % 2
            kv_next = issue_kv(gi + 1) if gi + 1 < G else None
            for tc in range(2):
                pi = 2 + tc
                for ech in range(ECH):
                    P.op("te", lambda e, wi=wi, tc=tc, ech=ech, pi=pi: e.matmul(
                        mm_ps[pi][:, :],
                        wkb[wi][:, ech * 128:(ech + 1) * 128],
                        xnT_view[:, ech, tc * 512:(tc + 1) * 512],
                        start=(ech == 0), stop=(ech == ECH - 1)),
                        waits=[xnT_done, kv_wload, ps_free[pi]],
                        inc=([c_te] if ech == ECH - 1 else []),
                        dd=(gi >= 1))
                kmm = self.c_te.snap()
                P.op("act", lambda e, gi=gi, tc=tc, pi=pi: e.activation(
                    self.kown[:, gi * TOWN + tc * 512:
                              gi * TOWN + (tc + 1) * 512],
                    mm_ps[pi][:, :], AF.Identity, bias=self.bkTb[:, gi:gi + 1]),
                    waits=[kmm], inc=[c_act])
                ps_free[pi] = self.c_act.snap()
            for tc in range(2):
                pi = 2 + tc
                for ech in range(ECH):
                    P.op("te", lambda e, wi=wi, tc=tc, ech=ech, pi=pi: e.matmul(
                        mm_ps[pi][:, :],
                        wvb[wi][:, ech * 128:(ech + 1) * 128],
                        xnT_view[:, ech, tc * 512:(tc + 1) * 512],
                        start=(ech == 0), stop=(ech == ECH - 1)),
                        waits=([xnT_done, kv_wload, ps_free[pi]]
                               + ([kv_next] if tc == 1 and ech == 0 else [])),
                        inc=([c_te] if ech == ECH - 1 else []),
                        dd=(gi >= 1))
                vmm = self.c_te.snap()
                P.op("act", lambda e, gi=gi, tc=tc, pi=pi: e.activation(
                    vtb[:, tc * 512:(tc + 1) * 512], mm_ps[pi][:, :],
                    AF.Identity, bias=self.bvTb[:, gi:gi + 1]),
                    waits=[vmm, vt_free], inc=[c_act])
                ps_free[pi] = self.c_act.snap()
            vt_done = self.c_act.snap()
            for half in range(2):
                tp = bufs["tp_ps"] if half == 0 else bufs["tp_ps2"]
                for k in range(4):
                    t = half * 4 + k
                    P.op("te", lambda e, t=t, k=k, tp=tp: e.transpose(
                        tp[:, k * 128:(k + 1) * 128],
                        vtb[:, t * 128:(t + 1) * 128], self.identb[:, :]),
                        waits=[vt_done, vtp_free[half]],
                        inc=([c_te] if k == 3 else []))
                tdone = self.c_te.snap()
                P.op("dve", lambda e, gi=gi, half=half, tp=tp: e.tensor_copy(
                    self.vown[:, :].rearrange("p (q s) -> p q s", s=129)
                    [:, gi * JT + half * 4: gi * JT + (half + 1) * 4, 0:128],
                    tp[:, 0:512].rearrange("p (k n) -> p k n", k=4)),
                    waits=[tdone, ones_set], inc=[c_dve])
                vtp_free[half] = self.c_dve.snap()
            vt_free = self.c_dve.snap()
            kvw_free[wi] = self.c_dve.snap()
            kv_wload = kv_next

        if DEBUG:
            fin = [self.c_act.snap(), self.c_dve.snap()]
            self.dbg_xnt = self.dram_out("dbg_xnt", [128, ECH * TOWN], BF16)
            self.dbg_q = self.dram_out("dbg_q", [128, H * TOWN], BF16)
            self.dbg_kown = self.dram_out("dbg_kown", [128, G * TOWN], BF16)
            self.dbg_vown = self.dram_out("dbg_vown", [128, G * JT * 129], BF16)
            for dst, src in [(self.dbg_xnt, self.xnT_own),
                             (self.dbg_q, self.q_sb),
                             (self.dbg_kown, self.kown),
                             (self.dbg_vown, self.vown)]:
                P.op("sp", lambda e, d=dst, s=src: e.dma_start(out=d,
                                                               in_=s[:, :]),
                     waits=fin, inc=[c_sp])

    def phase15(self, es):
        P = self.P
        c_sp, c_gp, c_te, c_act, c_dve = (self.c_sp, self.c_gp, self.c_te,
                                          self.c_act, self.c_dve)
        s = self.sb
        sw1b = s(es, "sw1b", [128, ECH * HID], BF16)
        swgb = s(es, "swgb", [128, ECH * HID], BF16)
        wdb = s(es, "wdb", [128, ECH * LAT], BF16)
        wgb = s(es, "wgb", [128, ECH * NEXP], BF16)
        prob = s(es, "prob", [128, JT * NEXP], F32)
        vbuf = s(es, "vbuf_g", [128, JT * NEXP], F32)
        cnt = [s(es, f"cnt{i}", [128, JT * NEXP], F32) for i in (0, 1)]
        rbuf = s(es, "rbuf", [128, HID], BF16)
        rbuf2 = s(es, "rbuf2", [128, HID], BF16)
        rsum = s(es, "rsum", [128, JT], F32)
        z_ps = self.ps(es, "z_ps", [128, 512])
        h_ps = self.ps(es, "h_ps", [128, 1024])
        g_ps = self.ps(es, "g_ps", [128, 1024])

        xnT_view = self.xnT_own[:, :].rearrange("p (c n) -> p c n", c=ECH)

        P.op("gp", lambda e: e.dma_start(
            out=sw1b[:, :].rearrange("p (c n) -> p c n", c=ECH),
            in_=self.sw1T_in.rearrange("(c p) n -> p c n", p=128)),
            inc=[c_gp])
        P.op("gp", lambda e: e.dma_start(
            out=swgb[:, :].rearrange("p (c n) -> p c n", c=ECH),
            in_=self.swgT_in.rearrange("(c p) n -> p c n", p=128)),
            inc=[c_gp])
        P.op("gp", lambda e: e.dma_start(
            out=wdb[:, :].rearrange("p (c n) -> p c n", c=ECH),
            in_=self.wdT_in.rearrange("(c p) n -> p c n", p=128)),
            inc=[c_gp])
        P.op("gp", lambda e: e.dma_start(
            out=wgb[:, :].rearrange("p (c n) -> p c n", c=ECH),
            in_=self.wgT_in.rearrange("(c p) n -> p c n", p=128)),
            inc=[c_gp])
        wload = c_gp.snap()

        for t in range(JT):
            for ech in range(ECH):
                P.op("te", lambda e, t=t, ech=ech: e.matmul(
                    z_ps[:, t * NEXP:(t + 1) * NEXP],
                    xnT_view[:, ech, t * 128:(t + 1) * 128],
                    wgb[:, ech * NEXP:(ech + 1) * NEXP],
                    start=(ech == 0), stop=(ech == ECH - 1)),
                    waits=[wload],
                    inc=([c_te] if ech == ECH - 1 and t == JT - 1 else []))
        zmm = self.c_te.snap()
        P.op("act", lambda e: e.activation(
            prob[:, :], z_ps[:, 0:JT * NEXP], AF.Exp, scale=-1.0),
            waits=[zmm], inc=[c_act])
        P.op("dve", lambda e: e.tensor_scalar(
            prob[:, :], prob[:, :], 1.0, None, ALU.add),
            waits=[c_act], inc=[c_dve])
        P.op("dve", lambda e: e.reciprocal(prob[:, :], prob[:, :]),
            waits=[c_dve], inc=[c_dve])
        P.op("dve", lambda e: e.tensor_tensor(
            vbuf[:, :], prob[:, :], self.brep8[:, :], ALU.add),
            waits=[c_dve], inc=[c_dve])
        P.op("dve", lambda e: e.memset(cnt[0][:, :], 0.0),
            waits=[c_dve], inc=[c_dve])
        cur = 0
        for ep in range(NEXP):
            for t in range(JT):
                P.op("dve", lambda e, ep=ep, t=t, cur=cur: e.scalar_tensor_tensor(
                    cnt[1 - cur][:, t * NEXP:(t + 1) * NEXP],
                    vbuf[:, t * NEXP:(t + 1) * NEXP],
                    vbuf[:, t * NEXP + ep:t * NEXP + ep + 1],
                    cnt[cur][:, t * NEXP:(t + 1) * NEXP],
                    ALU.is_lt, ALU.add),
                    waits=[c_dve], inc=[c_dve])
            cur = 1 - cur
        P.op("dve", lambda e, cur=cur: e.tensor_scalar(
            cnt[1 - cur][:, :], cnt[cur][:, :], 7.5, None, ALU.is_le),
            waits=[c_dve], inc=[c_dve])
        P.op("dve", lambda e, cur=cur: e.tensor_tensor(
            self.wdense[:, :], prob[:, :], cnt[1 - cur][:, :], ALU.mult),
            waits=[c_dve], inc=[c_dve])
        P.op("dve", lambda e: e.reduce_sum(
            rsum[:, :].rearrange("p (j o) -> p j o", o=1),
            self.wdense[:, :].rearrange("p (j n) -> p j n", j=JT), axis=AX.X),
            waits=[c_dve], inc=[c_dve])
        P.op("dve", lambda e: e.reciprocal(rsum[:, :], rsum[:, :]),
            waits=[c_dve], inc=[c_dve])
        for t in range(JT):
            P.op("dve", lambda e, t=t: e.tensor_scalar(
                self.wdense[:, t * NEXP:(t + 1) * NEXP],
                self.wdense[:, t * NEXP:(t + 1) * NEXP],
                rsum[:, t:t + 1], RSCALE, ALU.mult, ALU.mult),
                waits=[c_dve], inc=[c_dve])

        r_free = None
        for hch in range(8):
            for tc in range(2):
                for ech in range(ECH):
                    P.op("te", lambda e, hch=hch, tc=tc, ech=ech: e.matmul(
                        h_ps[:, tc * 512:(tc + 1) * 512],
                        sw1b[:, ech * HID + hch * 128:
                             ech * HID + (hch + 1) * 128],
                        xnT_view[:, ech, tc * 512:(tc + 1) * 512],
                        start=(ech == 0), stop=(ech == ECH - 1)),
                        waits=[wload, c_dve], inc=[], dd=(ech > 0))
                    P.op("te", lambda e, hch=hch, tc=tc, ech=ech: e.matmul(
                        g_ps[:, tc * 512:(tc + 1) * 512],
                        swgb[:, ech * HID + hch * 128:
                             ech * HID + (hch + 1) * 128],
                        xnT_view[:, ech, tc * 512:(tc + 1) * 512],
                        start=(ech == 0), stop=(ech == ECH - 1)),
                        waits=[wload, c_dve],
                        inc=([c_te] if ech == ECH - 1 and tc == 1 else []),
                        dd=(ech > 0))
            hmm = self.c_te.snap()
            P.op("act", lambda e: e.activation(
                rbuf[:, :], g_ps[:, :], AF.Relu),
                waits=[hmm, r_free], inc=[c_act])
            P.op("dve", lambda e: e.tensor_tensor(
                rbuf2[:, :], rbuf[:, :], rbuf[:, :], ALU.mult),
                waits=[c_act], inc=[c_dve])
            P.op("dve", lambda e, hch=hch: e.tensor_tensor(
                self.hhT[:, hch * TOWN:(hch + 1) * TOWN],
                h_ps[:, :], rbuf2[:, :], ALU.mult),
                waits=[c_dve], inc=[c_dve])
            r_free = self.c_dve.snap()

        for lch in range(4):
            for tc in range(2):
                for ech in range(ECH):
                    P.op("te", lambda e, lch=lch, tc=tc, ech=ech: e.matmul(
                        h_ps[:, tc * 512:(tc + 1) * 512],
                        wdb[:, ech * LAT + lch * 128:
                            ech * LAT + (lch + 1) * 128],
                        xnT_view[:, ech, tc * 512:(tc + 1) * 512],
                        start=(ech == 0), stop=(ech == ECH - 1)),
                        waits=[wload, c_act, c_dve],
                        inc=([c_te] if ech == ECH - 1 and tc == 1 else []),
                        dd=(ech > 0))
            xmm = self.c_te.snap()
            P.op("act", lambda e, lch=lch: e.activation(
                self.xlT[:, lch * TOWN:(lch + 1) * TOWN], h_ps[:, :],
                AF.Identity),
                waits=[xmm], inc=[c_act])

        if DEBUG:
            fin = [self.c_act.snap(), self.c_dve.snap()]
            self.dbg_w = self.dram_out("dbg_w", [128, JT * NEXP], F32)
            self.dbg_hh = self.dram_out("dbg_hh", [128, 8 * TOWN], BF16)
            self.dbg_xl = self.dram_out("dbg_xl", [128, 4 * TOWN], BF16)
            for dst, src in [(self.dbg_w, self.wdense), (self.dbg_hh, self.hhT),
                             (self.dbg_xl, self.xlT)]:
                P.op("sp", lambda e, d=dst, s=src: e.dma_start(out=d,
                                                               in_=s[:, :]),
                     waits=fin, inc=[c_sp])

    def phase2(self, es):
        P = self.P
        c_sp, c_gp, c_te, c_act, c_dve = (self.c_sp, self.c_gp, self.c_te,
                                          self.c_act, self.c_dve)
        s = self.sb
        NS = 2
        kbuf = [s(es, f"kbuf{i}", [128, T], BF16) for i in (0, 1)]
        vbuf = [s(es, f"vbuf{i}", [128, 64 * 144], BF16) for i in (0, 1)]
        pt = [s(es, f"pt{i}", [128, 1024], BF16) for i in range(NS)]
        sst = s(es, "sst", [128, JT * 128], F32)
        qsq = s(es, "qsqb", [128, 4 * TOWN], BF16)
        ksq = s(es, "ksqb", [128, 512], BF16)
        kmaxb = s(es, "kmaxb", [1, 20], F32)
        cbuf = s(es, "cbuf", [1, 16], F32)
        crep4 = s(es, "crep4", [128, 4], F32)
        dbadj4 = s(es, "dbadj4", [128, 4 * JT], F32)
        rcpb = s(es, "rcpb", [128, JT], F32)
        zbuf = s(es, "zbuf", [128, 512], BF16)
        s_ps = [self.ps(es, f"s_ps{i}", [128, 1024]) for i in range(NS)]
        o_ps = [self.ps(es, f"o_ps{i}", [128, 512]) for i in range(3)]
        cp_ps = self.ps(es, "cp_ps", [128, 512])

        def o_tile(j):
            return o_ps[j // 3][:, (j % 3) * 129:(j % 3) * 129 + 129]

        for i in (0, 1):
            P.op("dve", lambda e, i=i: e.memset(
                vbuf[i][:, :].rearrange("p (q s) -> p q s", s=144)
                [:, :, 128:129], 1.0), inc=[c_dve])
        P.op("dve", lambda e: e.memset(zbuf[:, :], 0.0), inc=[c_dve])
        ones_set = c_dve.snap()

        exp_done = [None] * NS
        av_done = [None] * NS
        head_eplg = None
        sst_free = None
        group_done = [None, None]
        slot = 0

        for gi in range(G):
            bi = gi % 2
            P.op("gp", lambda e, gi=gi, bi=bi: e.dma_start(
                out=kbuf[bi][:, :], in_=self.k_hbm[gi]),
                waits=[group_done[bi]], inc=[c_gp])
            for qt in range(64):
                P.op("sp", lambda e, gi=gi, bi=bi, qt=qt: e.dma_start_transpose(
                    vbuf[bi][:, qt * 144:qt * 144 + 128],
                    self.v_hbm[gi, :, qt * 128:(qt + 1) * 128]),
                    waits=([group_done[bi], ones_set] if qt == 0 else []),
                    inc=[c_sp])
            ready_sp = self.c_sp.snap()
            ready_gp = self.c_gp.snap()
            ready = ready_sp

            guard = [w for w in exp_done if w] + [ready_sp, ready_gp]
            P.op("dve", lambda e, gi=gi: e.reduce_max(
                kmaxb[0:1, 0:1], self.kmax64[0:1, gi * 16:(gi + 1) * 16],
                axis=AX.X), waits=[c_dve], inc=[c_dve])
            P.op("act", lambda e: e.activation(
                kmaxb[0:1, 1:2], kmaxb[0:1, 0:1], AF.Ln),
                waits=[c_dve], inc=[c_act])
            lnk_done = c_act.snap()

            P.op("act", lambda e, gi=gi: e.activation(
                qsq[:, :], self.q_sb[:, gi * 4 * TOWN:(gi + 1) * 4 * TOWN],
                AF.Square), waits=[self.c_dve.snap()], inc=[c_act])
            qsq_done = c_act.snap()
            qprev = [None, None]
            for i in range(8):
                sl = i % 2
                P.op("te", lambda e, i=i, sl=sl: e.matmul(
                    s_ps[1][0:1, sl * 512:(sl + 1) * 512],
                    self.onescol[:, 0:1],
                    qsq[:, i * 512:(i + 1) * 512],
                    start=True, stop=True),
                    waits=guard + [qsq_done, qprev[sl]], inc=[c_te])
                mm = self.c_te.snap()
                P.op("dve", lambda e, i=i, sl=sl: e.reduce_max(
                    cbuf[0:1, i:i + 1],
                    s_ps[1][0:1, sl * 512:(sl + 1) * 512], axis=AX.X),
                    waits=[mm], inc=[c_dve])
                qprev[sl] = c_dve.snap()
            P.op("dve", lambda e: e.reduce_max(
                cbuf[0:1, 8:12].rearrange("o (h x) -> o h x", h=4),
                cbuf[0:1, 0:8].rearrange("o (h x) -> o h x", h=4),
                axis=AX.X),
                waits=[qprev[0], qprev[1]], inc=[c_dve])
            P.op("act", lambda e: e.activation(
                cbuf[0:1, 12:16], cbuf[0:1, 8:12], AF.Ln),
                waits=[c_dve], inc=[c_act])
            P.op("dve", lambda e: e.tensor_scalar(
                cbuf[0:1, 8:12], cbuf[0:1, 12:16], kmaxb[0:1, 1:2], None,
                ALU.add), waits=[c_act, lnk_done], inc=[c_dve])
            P.op("act", lambda e: e.activation(
                cbuf[0:1, 12:16], cbuf[0:1, 8:12], AF.Exp, scale=0.5),
                waits=[c_dve], inc=[c_act])
            P.op("dve", lambda e: e.tensor_scalar(
                cbuf[0:1, 8:12], cbuf[0:1, 12:16], -SCALE_ATT, None,
                ALU.mult), waits=[c_act], inc=[c_dve])
            P.op("te", lambda e: e.matmul(
                cp_ps[:, 0:4], self.onesfb[0:1, :], cbuf[0:1, 8:12],
                start=True, stop=True), waits=[c_dve], inc=[c_te])
            P.op("dve", lambda e: e.tensor_copy(crep4[:, :], cp_ps[:, 0:4]),
                 waits=[c_te], inc=[c_dve])
            for hh in range(4):
                P.op("dve", lambda e, hh=hh: e.tensor_scalar(
                    dbadj4[:, hh * JT:(hh + 1) * JT], self.dbiasb[:, :],
                    crep4[:, hh:hh + 1], None, ALU.add),
                    waits=[c_dve], inc=[c_dve])
            cprep = c_dve.snap()

            for hh in range(4):
                h = gi * 4 + hh
                o_free = head_eplg
                qoff = h * TOWN
                for b in range(3):
                    P.op("te", lambda e, b=b: e.matmul(
                        o_ps[b][:, :], self.identb[:, :], zbuf[:, :],
                        start=True, stop=False),
                        waits=[ones_set, o_free, cprep],
                        inc=([c_te] if b == 2 else []),
                        dd=(b > 0))

                def emit_S(jobs, sl, cprep):
                    alln = sum(len(j["s_mms"]) for j in jobs)
                    i = 0
                    for base, jb in zip((0, 512), jobs):
                        for dst_off, w, lf, rf in jb["s_mms"]:
                            i += 1
                            P.op("te", lambda e, dst_off=base + dst_off, w=w,
                                 lf=lf, rf=rf, sl=sl: e.matmul(
                                     s_ps[sl][:, dst_off:dst_off + w],
                                     lf(), rf(), start=True, stop=True),
                                 waits=[ready, cprep, exp_done[sl]],
                                 inc=([c_te] if i == alln else []),
                                 dd=True)
                    return self.c_te.snap()

                def emit_EA(jobs, sl, smm):
                    extra = []
                    ntri = sum(4 for jb in jobs if jb["tri"])
                    it = 0
                    for base, jb in zip((0, 512), jobs):
                        if not jb["tri"]:
                            continue
                        for i4 in range(4):
                            it += 1
                            P.op("dve", lambda e, o=base + i4 * 128, sl=sl:
                                 e.tensor_tensor(
                                     s_ps[sl][:, o:o + 128],
                                     s_ps[sl][:, o:o + 128],
                                     self.trib[:, :], ALU.add),
                                 waits=([smm] if it == 1 else []),
                                 inc=([c_dve] if it == ntri else []))
                    if ntri:
                        extra = [self.c_dve.snap()]
                    width = (512 + jobs[1]["width"]) if len(jobs) == 2 \
                        else jobs[0]["width"]
                    P.op("act", lambda e, sl=sl, width=width,
                         bias_ap=jobs[0]["bias"]: e.activation(
                             pt[sl][:, 0:width], s_ps[sl][:, 0:width],
                             AF.Exp, bias=bias_ap, scale=SCALE_ATT),
                         waits=[smm, av_done[sl]] + extra, inc=[c_act])
                    exp_done[sl] = c_act.snap()
                    na = sum(len(jb["avs"]) for jb in jobs)
                    i = 0
                    for base, jb in zip((0, 512), jobs):
                        for j, pt_off, vf in jb["avs"]:
                            i += 1
                            P.op("te", lambda e, j=j, pt_off=base + pt_off,
                                 vf=vf, sl=sl: e.matmul(
                                     o_tile(j), pt[sl][:, pt_off:pt_off + 128],
                                     vf(), start=False, stop=False),
                                 waits=[exp_done[sl]],
                                 inc=([c_te] if i == na else []))
                    av_done[sl] = self.c_te.snap()

                jl = []
                for kt in range(56):
                    jmin = kt // 8 + 1
                    w_all = (8 - jmin) * 128
                    off = 0
                    while off < w_all:
                        w = min(512, w_all - off)
                        q0 = qoff + jmin * 128 + off
                        jlo = jmin + off // 128
                        lf = (lambda bi=bi, kt=kt:
                              kbuf[bi][:, kt * 128:(kt + 1) * 128])
                        rf = (lambda q0=q0, w=w: self.q_sb[:, q0:q0 + w])
                        avs = [(j, (j - jlo) * 128,
                                (lambda bi=bi, kt=kt:
                                 vbuf[bi][:, kt * 144:kt * 144 + 129]))
                               for j in range(jlo, jlo + w // 128)]
                        jl.append(dict(s_mms=[(0, w, lf, rf)], width=w,
                                       bias=crep4[:, hh:hh + 1], avs=avs,
                                       tri=False, bkey=("c", hh)))
                        off += w
                for dd in range(8):
                    for half in range(2):
                        smms = []
                        avs = []
                        for i in range(4):
                            j = half * 4 + i
                            kk = 8 * j + dd
                            lf = (lambda bi=bi, kk=kk:
                                  kbuf[bi][:, kk * 128:(kk + 1) * 128])
                            q0 = qoff + j * 128
                            rf = (lambda q0=q0: self.q_sb[:, q0:q0 + 128])
                            smms.append((i * 128, 128, lf, rf))
                            avs.append((j, i * 128,
                                        (lambda bi=bi, kk=kk:
                                         vbuf[bi][:, kk * 144:kk * 144 + 129])))
                        jl.append(dict(
                            s_mms=smms, width=512,
                            bias=dbadj4[:, hh * JT + dd:hh * JT + dd + 1],
                            avs=avs, tri=False, bkey=("d", hh, dd)))
                for half in range(2):
                    smms = []
                    avs = []
                    for i in range(4):
                        j = half * 4 + i
                        k0 = gi * TOWN + j * 128
                        lf = (lambda k0=k0: self.kown[:, k0:k0 + 128])
                        q0 = qoff + j * 128
                        rf = (lambda q0=q0: self.q_sb[:, q0:q0 + 128])
                        smms.append((i * 128, 128, lf, rf))
                        avs.append((j, i * 128,
                                    (lambda q=gi * JT + j:
                                     self.vown[:, q * 129:(q + 1) * 129])))
                    jl.append(dict(s_mms=smms, width=512,
                                   bias=crep4[:, hh:hh + 1], avs=avs,
                                   tri=True, bkey=("t", hh)))
                plist = []
                p = 0
                while p < len(jl):
                    if (p + 1 < len(jl)
                            and jl[p]["bkey"] == jl[p + 1]["bkey"]):
                        plist.append([jl[p], jl[p + 1]])
                        p += 2
                    else:
                        plist.append([jl[p]])
                        p += 1
                pend = None
                for pr in plist:
                    sl = slot
                    slot = (slot + 1) % NS
                    smm = emit_S(pr, sl, cprep)
                    if pend is not None:
                        emit_EA(*pend)
                    pend = (pr, sl, smm)
                emit_EA(*pend)
                all_av = self.c_te.snap()

                for j in range(JT):
                    P.op("dve", lambda e, j=j: e.reciprocal(
                        rcpb[:, j:j + 1], o_tile(j)[:, 128:129]),
                        waits=[all_av, c_dve], inc=[c_dve])
                    P.op("dve", lambda e, j=j: e.tensor_scalar(
                        sst[:, j * 128:(j + 1) * 128], o_tile(j)[:, 0:128],
                        rcpb[:, j:j + 1], None, ALU.mult),
                        waits=[c_dve, sst_free], inc=[c_dve])
                head_eplg = self.c_dve.snap()
                P.op("sp", lambda e, h=h: e.dma_start(
                    out=self.s_hbm.rearrange("(j p) e -> p j e", p=128)
                    [:, :, h * 128:(h + 1) * 128],
                    in_=sst[:, :].rearrange("p (j d) -> p j d", j=JT)),
                    waits=[head_eplg], inc=[c_sp])
                sst_free = self.c_sp.snap()
            group_done[bi] = self.c_te.snap()

        if DEBUG:
            fin = self.c_sp.snap()
            self.dbg_s = self.dram_out("dbg_s", [TOWN, E], F32)
            P.op("sp", lambda e: e.dma_start(out=self.dbg_s, in_=self.s_hbm),
                 waits=[fin], inc=[c_sp])

    def phase3(self, es):
        P = self.P
        c_sp, c_gp, c_te, c_act, c_dve = (self.c_sp, self.c_gp, self.c_te,
                                          self.c_act, self.c_dve)
        s = self.sb
        acc = [s(es, f"acc{i}", [128, JT * 512], F32) for i in (0, 1)]

        with ExitStack() as pa:
            rw1b = [self.sb(pa, f"rw1b{i}", [128, 4 * HID], BF16)
                    for i in (0, 1)]
            rwgb = [self.sb(pa, f"rwgb{i}", [128, 4 * HID], BF16)
                    for i in (0, 1)]
            rw2b = [self.sb(pa, f"rw2b{i}", [128, 8 * LAT], BF16)
                    for i in (0, 1)]
            abT = self.sb(pa, "abT", [128, 8 * TOWN], BF16)
            rbu = [self.sb(pa, f"rb3{i}", [128, 512], BF16) for i in (0, 1)]
            rb2u = [self.sb(pa, f"rb23{i}", [128, 512], BF16) for i in (0, 1)]
            ab_ps = [self.ps(pa, f"ab_ps{i}", [128, 1024]) for i in (0, 1)]
            y_ps = [self.ps(pa, f"y_ps{i}", [128, 512]) for i in (0, 1)]
            unit_free = [None, None]
            rb_free = [None, None]

            xlT_view = self.xlT[:, :].rearrange("p (l n) -> p l n", l=4)
            P.op("dve", lambda e: e.memset(acc[0][:, :], 0.0), inc=[c_dve])
            acc_zero = c_dve.snap()

            w_free = [None, None]
            y_free = [None, None]

            def issue_wloads(ex):
                wi = ex % 2
                P.op("gp", lambda e, ex=ex, wi=wi: e.dma_start(
                    out=rw1b[wi][:, :].rearrange("p (l n) -> p l n", l=4),
                    in_=self.rw1T_in[ex].rearrange("(l p) n -> p l n", p=128)),
                    waits=[w_free[wi]], inc=[c_gp])
                P.op("gp", lambda e, ex=ex, wi=wi: e.dma_start(
                    out=rwgb[wi][:, :].rearrange("p (l n) -> p l n", l=4),
                    in_=self.rwgT_in[ex].rearrange("(l p) n -> p l n", p=128)),
                    waits=[c_gp], inc=[c_gp])
                P.op("gp", lambda e, ex=ex, wi=wi: e.dma_start(
                    out=rw2b[wi][:, :].rearrange("p (hh n) -> p hh n", hh=8),
                    in_=self.rw2T_in[ex].rearrange("(hh p) n -> p hh n", p=128)),
                    waits=[c_gp], inc=[c_gp])
                return self.c_gp.snap()

            wld = issue_wloads(0)
            for ex in range(NEXP_RUN):
                wi = ex % 2
                if DEBUG and ex == 0:
                    self.dbg_rwge = self.dram_out("dbg_rwge", [128, 4 * HID], BF16)
                    P.op("sp", lambda e: e.dma_start(out=self.dbg_rwge,
                                                     in_=rwgb[0][:, :]),
                         waits=[wld], inc=[c_sp])

                ddw = ex > 0
                for hch in range(8):
                    for tc in range(2):
                        pu = (hch * 2 + tc) % 2
                        for lch in range(4):
                            P.op("te", lambda e, wi=wi, hch=hch, tc=tc,
                                 lch=lch, pu=pu: e.matmul(
                                     ab_ps[pu][:, 0:512],
                                     rw1b[wi][:, lch * HID + hch * 128:
                                              lch * HID + (hch + 1) * 128],
                                     xlT_view[:, lch, tc * 512:(tc + 1) * 512],
                                     start=(lch == 0), stop=(lch == 3)),
                                 waits=[wld, unit_free[pu]], inc=[], dd=ddw)
                            P.op("te", lambda e, wi=wi, hch=hch, tc=tc,
                                 lch=lch, pu=pu: e.matmul(
                                     ab_ps[pu][:, 512:1024],
                                     rwgb[wi][:, lch * HID + hch * 128:
                                              lch * HID + (hch + 1) * 128],
                                     xlT_view[:, lch, tc * 512:(tc + 1) * 512],
                                     start=(lch == 0), stop=(lch == 3)),
                                 waits=[wld, unit_free[pu]],
                                 inc=([c_te] if lch == 3 else []), dd=ddw)
                        abmm = self.c_te.snap()
                        P.op("act", lambda e, pu=pu: e.activation(
                            rbu[pu][:, :], ab_ps[pu][:, 512:1024], AF.Relu),
                            waits=[abmm, rb_free[pu]], inc=[c_act])
                        P.op("dve", lambda e, pu=pu: e.tensor_tensor(
                            rb2u[pu][:, :], rbu[pu][:, :], rbu[pu][:, :],
                            ALU.mult),
                            waits=[c_act], inc=[c_dve])
                        P.op("dve", lambda e, hch=hch, tc=tc, pu=pu:
                             e.tensor_tensor(
                                 abT[:, hch * TOWN + tc * 512:
                                     hch * TOWN + (tc + 1) * 512],
                                 ab_ps[pu][:, 0:512], rb2u[pu][:, :],
                                 ALU.mult),
                             waits=[c_dve], inc=[c_dve])
                        unit_free[pu] = self.c_dve.snap()
                        rb_free[pu] = unit_free[pu]
                ab_done = self.c_dve.snap()

                wld = (issue_wloads(ex + 1) if ex + 1 < NEXP_RUN else None)

                prev_acc = acc[ex % 2]
                next_acc = acc[(ex + 1) % 2]
                for t in range(JT):
                    yi = t % 2
                    for hch in range(8):
                        P.op("te", lambda e, wi=wi, t=t, hch=hch, yi=yi:
                             e.matmul(
                                 y_ps[yi][:, :],
                                 abT[:, hch * TOWN + t * 128:
                                     hch * TOWN + (t + 1) * 128],
                                 rw2b[wi][:, hch * LAT:(hch + 1) * LAT],
                                 start=(hch == 0), stop=(hch == 7)),
                             waits=([ab_done, y_free[yi]]
                                    + ([wld] if t == 0 and hch == 0 else [])),
                             inc=([c_te] if hch == 7 else []),
                             dd=(t >= 2))
                    ymm = self.c_te.snap()
                    P.op("dve", lambda e, ex=ex, t=t, yi=yi,
                         pa_=prev_acc, na=next_acc: e.scalar_tensor_tensor(
                             na[:, t * 512:(t + 1) * 512],
                             y_ps[yi][:, :],
                             self.wdense[:, t * NEXP + ex:t * NEXP + ex + 1],
                             pa_[:, t * 512:(t + 1) * 512],
                             ALU.mult, ALU.add),
                         waits=[ymm, acc_zero], inc=[c_dve])
                    y_free[yi] = self.c_dve.snap()
                w_free[wi] = self.c_dve.snap()
            acc_done = self.c_dve.snap()
            acc_fin = acc[NEXP_RUN % 2]
            if DEBUG:
                self.dbg_acc = self.dram_out("dbg_acc", [128, JT * 512], F32)
                P.op("sp", lambda e: e.dma_start(out=self.dbg_acc,
                                                 in_=acc_fin[:, :]),
                     waits=[acc_done], inc=[c_sp])
                self.dbg_ab = self.dram_out("dbg_ab", [128, 8 * TOWN], BF16)
                P.op("sp", lambda e: e.dma_start(out=self.dbg_ab,
                                                 in_=abT[:, :]),
                     waits=[acc_done], inc=[c_sp])
                self.dbg_xl2 = self.dram_out("dbg_xl2", [128, 4 * TOWN], BF16)
                P.op("sp", lambda e: e.dma_start(out=self.dbg_xl2,
                                                 in_=self.xlT[:, :]),
                     waits=[acc_done], inc=[c_sp])


        with ExitStack() as pb:
            sw2b = self.sb(pb, "sw2b", [128, 8 * E], BF16)
            wupb = self.sb(pb, "wupb", [128, 4 * E], BF16)
            accb = self.sb(pb, "accb", [128, JT * 512], BF16)
            accT = self.sb(pb, "accT", [128, 4 * TOWN], BF16)
            gbuf = self.sb(pb, "gbuf", [128, E], F32)
            stile = self.sb(pb, "stile", [128, E], F32)
            xtile = self.sb(pb, "xtile", [128, E], F32)
            outst = self.sb(pb, "outst", [128, E], F32)
            o4_ps = self.ps(pb, "o4_ps", [128, 2048])
            tp2_ps = self.ps(pb, "tp2_ps", [128, 512], BF16)

            P.op("gp", lambda e: e.dma_start(
                out=sw2b[:, :].rearrange("p (hh n) -> p hh n", hh=8),
                in_=self.sw2T_in.rearrange("(hh p) n -> p hh n", p=128)),
                waits=[acc_done], inc=[c_gp])
            P.op("gp", lambda e: e.dma_start(
                out=wupb[:, :].rearrange("p (l n) -> p l n", l=4),
                in_=self.wupT_in.rearrange("(l p) n -> p l n", p=128)),
                waits=[acc_done], inc=[c_gp])
            wld2 = self.c_gp.snap()

            P.op("dve", lambda e: e.tensor_copy(accb[:, :], acc_fin[:, :]),
                 waits=[acc_done], inc=[c_dve])
            accb_done = self.c_dve.snap()
            tp_free = None
            for t in range(JT):
                for lch in range(4):
                    P.op("te", lambda e, t=t, lch=lch: e.transpose(
                        tp2_ps[:, lch * 128:(lch + 1) * 128],
                        accb[:, t * 512 + lch * 128: t * 512 + (lch + 1) * 128],
                        self.identb[:, :]),
                        waits=[accb_done, tp_free],
                        inc=([c_te] if lch == 3 else []))
                tmm = self.c_te.snap()
                P.op("dve", lambda e, t=t: e.tensor_copy(
                    accT[:, :].rearrange("p (l n) -> p l n", l=4)
                    [:, :, t * 128:(t + 1) * 128],
                    tp2_ps[:, :].rearrange("p (l n) -> p l n", l=4)),
                    waits=[tmm], inc=[c_dve])
                tp_free = self.c_dve.snap()
            accT_done = self.c_dve.snap()

            o4_free = None
            st_free = acc_done
            for t in range(JT):
                P.op("sp", lambda e, t=t: e.dma_start(
                    out=stile[:, :],
                    in_=self.s_hbm[t * 128:(t + 1) * 128, :]),
                    waits=[st_free], inc=[c_sp])
                P.op("sp", lambda e, t=t: e.dma_start(
                    out=xtile[:, :],
                    in_=self.x_own[t * 128:(t + 1) * 128, :]),
                    waits=[st_free], inc=[c_sp])
                sx_in = self.c_sp.snap()
                for nch in range(4):
                    for hch in range(8):
                        P.op("te", lambda e, t=t, nch=nch, hch=hch: e.matmul(
                            o4_ps[:, nch * 512:(nch + 1) * 512],
                            self.hhT[:, hch * TOWN + t * 128:
                                     hch * TOWN + (t + 1) * 128],
                            sw2b[:, hch * E + nch * 512:
                                 hch * E + (nch + 1) * 512],
                            start=(hch == 0), stop=False),
                            waits=[wld2, o4_free], inc=[])
                    for lch in range(4):
                        P.op("te", lambda e, t=t, nch=nch, lch=lch: e.matmul(
                            o4_ps[:, nch * 512:(nch + 1) * 512],
                            accT[:, lch * TOWN + t * 128:
                                 lch * TOWN + (t + 1) * 128],
                            wupb[:, lch * E + nch * 512:
                                 lch * E + (nch + 1) * 512],
                            start=False, stop=(lch == 3)),
                            waits=[accT_done, wld2, o4_free],
                            inc=([c_te] if lch == 3 and nch == 3 else []))
                omm = self.c_te.snap()
                P.op("act", lambda e: e.activation(
                    gbuf[:, :], o4_ps[:, :], AF.Silu),
                    waits=[omm], inc=[c_act])
                o4_free = self.c_act.snap()
                if DEBUG and t == 0:
                    self.dbg_g = self.dram_out("dbg_g", [128, E], F32)
                    P.op("sp", lambda e: e.dma_start(out=self.dbg_g,
                                                     in_=gbuf[:, :]),
                         waits=[o4_free], inc=[c_sp])
                P.op("dve", lambda e: e.tensor_tensor(
                    outst[:, :], stile[:, :], gbuf[:, :], ALU.mult),
                    waits=[c_act, sx_in, st_free], inc=[c_dve])
                P.op("dve", lambda e: e.tensor_tensor(
                    outst[:, :], outst[:, :], xtile[:, :], ALU.add),
                    waits=[c_dve], inc=[c_dve])
                comb = self.c_dve.snap()
                P.op("sp", lambda e, t=t: e.dma_start(
                    out=self.out[t * 128:(t + 1) * 128, :], in_=outst[:, :]),
                    waits=[comb], inc=[c_sp])
                st_free = self.c_sp.snap()

    def finale(self):
        self.P.op("sp", None, waits=[self.c_sp.snap()])



def nc_sem_stack(nc, ctrs):
    class _S:
        def __enter__(self):
            self.es = ExitStack()
            for c in ctrs:
                c.sem = self.es.enter_context(nc.semaphore(c.name))
            return self

        def __exit__(self, *a):
            self.es.close()
            return False
    return _S()


def kernel(**inputs):
    maps = _stage_inputs(inputs)
    b = Builder()
    nc = b.build()
    res = run_bass_kernel_spmd(nc, maps, core_ids=list(range(NCORES)))
    if DEBUG:
        return res
    return _assemble(res.results)


# version 17 (final)
# speedup vs baseline: 1.2834x; 1.0158x over previous
import math
from contextlib import ExitStack

import numpy as np
import ml_dtypes

import concourse.bass as bass
import concourse.mybir as mybir
from concourse.bass_utils import run_bass_kernel_spmd
import concourse.bass_utils as _bass_utils

_orig_run_command = _bass_utils.run_command

def _run_command_ldwopt(cmd, *a, **kw):
    cmd = ["--enable-ldw-opt=true" if c == "--enable-ldw-opt=false" else c
           for c in cmd]
    return _orig_run_command(cmd, *a, **kw)

_bass_utils.run_command = _run_command_ldwopt

F32 = mybir.dt.float32
BF16 = mybir.dt.bfloat16
AF = mybir.ActivationFunctionType
ALU = mybir.AluOpType
AX = mybir.AxisListType
BF = ml_dtypes.bfloat16

NCORES = 8
T, E, H, G, D = 8192, 2048, 16, 4, 128
HID, LAT, NEXP = 1024, 512, 16
TOWN = T // NCORES
NTILE = T // 128
JT = 8
ECH = E // 128
SCALE_ATT = 1.0 / math.sqrt(D)
NEG = -1.0e30
EPS = 1e-5
RSCALE = 2.5

DEBUG = 0
PHASES = 99
NEXP_RUN = NEXP
PIPE_P1 = 1


class Ctr:
    def __init__(self, name, step=1):
        self.name, self.step, self.val, self.sem = name, step, 0, None

    def bump(self):
        self.val += self.step
        return self.val

    def snap(self):
        return (self, self.val)


class Prog:
    ORDER = ("sp", "gp", "te", "act", "dve")
    QNAME = {"sp": "sync", "gp": "gpsimd", "te": "tensor",
             "act": "scalar", "dve": "vector"}

    def __init__(self):
        self.ops = []

    def op(self, eng, fn, waits=(), inc=(), dd=False):
        ws = []
        for w in waits:
            if w is None:
                continue
            ws.append((w, w.val) if isinstance(w, Ctr) else w)
        incs = [(c, c.bump()) for c in inc]
        self.ops.append((eng, fn, ws, incs, dd))

    def barrier(self, ctrs):
        snap = [c.snap() for c in ctrs]
        for eng in self.ORDER:
            self.ops.append((eng, None, snap, [], False))

    def replay(self, block):
        for name in self.ORDER:
            mine = [(fn, ws, incs, dd) for (e, fn, ws, incs, dd) in self.ops
                    if e == name]
            if not mine:
                continue

            def section(eng, mine=mine, name=name):
                hwm = {}
                for fn, ws, incs, dd in mine:
                    for c, v in ws:
                        if v > 0 and (not dd or hwm.get(id(c.sem), 0) < v):
                            eng.wait_ge(c.sem, v)
                            if v > hwm.get(id(c.sem), 0):
                                hwm[id(c.sem)] = v
                    if fn is None:
                        continue
                    inst = fn(eng)
                    for c, _ in incs:
                        inst = inst.then_inc(c.sem, c.step)

            getattr(block, self.QNAME[name])(section)


def _stage_inputs(ins):
    x = np.ascontiguousarray(np.asarray(ins["L0_attn_x"]).reshape(T, E))
    asnp = lambda a: np.ascontiguousarray(np.asarray(a))
    base = {
        "x_full": x,
        "xT_full": asnp(x.T),
        "identf": np.eye(128, dtype=np.float32),
        "sel8": np.ascontiguousarray(
            np.eye(8, dtype=np.float32)[:, :, None]
            * np.ones((1, 1, 128), np.float32)),
        "wq": asnp(ins["L0_attn_Wq"]), "wk": asnp(ins["L0_attn_Wk"]),
        "wv": asnp(ins["L0_attn_Wv"]),
        "bqT": asnp(np.asarray(ins["L0_attn_bq"]).T),
        "bkT": asnp(np.asarray(ins["L0_attn_bk"]).T),
        "bvT": asnp(np.asarray(ins["L0_attn_bv"]).T),
        "sw1T": asnp(np.asarray(ins["L0_ffn_sw1"]).T),
        "swgT": asnp(np.asarray(ins["L0_ffn_swg"]).T),
        "sw2T": asnp(np.asarray(ins["L0_ffn_sw2"]).T),
        "wdT": asnp(np.asarray(ins["L0_ffn_Wdown"]).T),
        "wgT": asnp(np.asarray(ins["L0_ffn_Wg"]).T),
        "wupT": asnp(np.asarray(ins["L0_ffn_Wup"]).T),
        "rw1T": asnp(np.asarray(ins["L0_ffn_rw1"]).transpose(0, 2, 1)),
        "rwgT": asnp(np.asarray(ins["L0_ffn_rwg"]).transpose(0, 2, 1)),
        "rw2T": asnp(np.asarray(ins["L0_ffn_rw2"]).transpose(0, 2, 1)),
        "btrow": asnp(np.tile(np.asarray(ins["L0_ffn_biases"]), JT)[None, :]),
        "ident": np.eye(128, dtype=BF),
        "tri": np.where(np.arange(128)[:, None] > np.arange(128)[None, :],
                        np.float32(NEG), np.float32(0.0)).astype(np.float32),
        "eps": np.full((128, 1), EPS, np.float32),
        "onesb": np.ones((1, 128), BF),
        "onesf": np.ones((1, 128), np.float32),
    }
    xt = x.reshape(NTILE, 128, E)
    maps = []
    for c in range(NCORES):
        m = dict(base)
        m["x_own"] = np.ascontiguousarray(xt[c::8].reshape(TOWN, E))
        db = np.zeros((128, JT), np.float32)
        db[:, c:] = NEG
        m["dbias"] = db
        maps.append(m)
    return maps


def _assemble(results):
    out = np.empty((T, E), np.float32)
    ot = out.reshape(NTILE, 128, E)
    for c in range(NCORES):
        ot[c::8] = np.asarray(results[c]["out"]).reshape(JT, 128, E)
    return out.reshape(1, T, E)


class Builder:
    def __init__(self):
        self.nc = bass.Bass("TRN2", target_bir_lowering=False)
        self.P = Prog()
        self.c_sp = Ctr("c_sp", 16)
        self.c_gp = Ctr("c_gp", 16)
        self.c_te = Ctr("c_te", 1)
        self.c_act = Ctr("c_act", 1)
        self.c_dve = Ctr("c_dve", 1)
        self.ctrs = [self.c_sp, self.c_gp, self.c_te, self.c_act, self.c_dve]

    def dram_in(self, name, shape, dtype=F32):
        return self.nc.dram_tensor(name, shape, dtype, kind="ExternalInput").ap()

    def dram_out(self, name, shape, dtype=F32):
        return self.nc.dram_tensor(name, shape, dtype, kind="ExternalOutput").ap()

    def declare(self):
        di = self.dram_in
        self.x_full = di("x_full", [T, E])
        self.x_own = di("x_own", [TOWN, E])
        self.xT_full = di("xT_full", [E, T])
        self.identf_in = di("identf", [128, 128])
        self.sel8_in = di("sel8", [8, 8, 128])
        self.wq_in = di("wq", [H, E, D])
        self.wk_in = di("wk", [G, E, D])
        self.wv_in = di("wv", [G, E, D])
        self.bqT_in = di("bqT", [D, H])
        self.bkT_in = di("bkT", [D, G])
        self.bvT_in = di("bvT", [D, G])
        self.sw1T_in = di("sw1T", [E, HID])
        self.swgT_in = di("swgT", [E, HID])
        self.sw2T_in = di("sw2T", [HID, E])
        self.wdT_in = di("wdT", [E, LAT])
        self.wgT_in = di("wgT", [E, NEXP])
        self.wupT_in = di("wupT", [LAT, E])
        self.rw1T_in = di("rw1T", [NEXP, LAT, HID])
        self.rwgT_in = di("rwgT", [NEXP, LAT, HID])
        self.rw2T_in = di("rw2T", [NEXP, HID, LAT])
        self.btrow_in = di("btrow", [1, JT * NEXP])
        self.ident_in = di("ident", [128, 128], BF16)
        self.tri_in = di("tri", [128, 128])
        self.eps_in = di("eps", [128, 1])
        self.onesb_in = di("onesb", [1, 128], BF16)
        self.onesf_in = di("onesf", [1, 128])
        self.dbias_in = di("dbias", [128, JT])
        self.out = self.dram_out("out", [TOWN, E])
        self.k_hbm = self.nc.dram_tensor("k_hbm", [G, D, T], BF16).ap()
        self.v_hbm = self.nc.dram_tensor("v_hbm", [G, D, T], BF16).ap()
        self.s_hbm = self.nc.dram_tensor("s_hbm", [TOWN, E], F32).ap()

    def sb(self, es, name, shape, dtype):
        return es.enter_context(self.nc.sbuf_tensor(name, shape, dtype))

    def ps(self, es, name, shape, dtype=F32):
        return es.enter_context(self.nc.psum_tensor(name, shape, dtype))

    def bar(self):
        self.P.barrier(self.ctrs)

    def ln_stats(self, bufs, src_dma_fn, st):
        P = self.P
        c_sp, c_act, c_dve = self.c_sp, self.c_act, self.c_dve
        xt, sqs, stats = bufs["xt"], bufs["sqs"], bufs["stats"]
        msq, var4 = bufs["msq"], bufs["var4"]
        P.op("sp", src_dma_fn, waits=[st.get("xt_free")], inc=[c_sp])
        xin = c_sp.snap()
        for t in range(4):
            P.op("dve", lambda e, t=t, xt=xt, stats=stats: e.reduce_sum(
                stats[:, t:t + 1], xt[:, t * E:(t + 1) * E], axis=AX.X),
                waits=[xin, st.get("stats_free")], inc=[c_dve])
            P.op("act", lambda e, t=t, xt=xt, sqs=sqs, stats=stats:
                 e.activation(
                     sqs[:, 0:E], xt[:, t * E:(t + 1) * E], AF.Square,
                     accum_out=stats[:, 4 + t:5 + t]),
                 waits=[xin, st.get("stats_free")], inc=[c_act])
        sums_done = c_dve.snap()
        sqs_done = c_act.snap()
        P.op("dve", lambda e, stats=stats: e.tensor_scalar(
            stats[:, 8:12], stats[:, 0:4], 1.0 / E, None, ALU.mult),
            waits=[sums_done], inc=[c_dve])
        P.op("dve", lambda e, stats=stats, msq=msq: e.tensor_tensor(
            msq[:, :], stats[:, 8:12], stats[:, 8:12], ALU.mult),
            waits=[c_dve], inc=[c_dve])
        P.op("dve", lambda e, stats=stats, msq=msq, var4=var4:
             e.scalar_tensor_tensor(
                 var4[:, :], stats[:, 4:8], 1.0 / E, msq[:, :],
                 ALU.mult, ALU.subtract),
             waits=[sqs_done, c_dve], inc=[c_dve])
        P.op("act", lambda e, var4=var4: e.activation(
            var4[:, :], var4[:, :], AF.Ln, bias=self.epsb[:, 0:1]),
            waits=[c_dve], inc=[c_act])
        P.op("act", lambda e, var4=var4, stats=stats: e.activation(
            stats[:, 12:16], var4[:, :], AF.Exp, scale=-0.5),
            waits=[c_act], inc=[c_act])
        rstd_done = c_act.snap()
        P.op("dve", lambda e, stats=stats: e.scalar_tensor_tensor(
            stats[:, 16:20], stats[:, 8:12], -1.0, stats[:, 12:16],
            ALU.mult, ALU.mult),
            waits=[rstd_done], inc=[c_dve])
        st["xt_free"] = self.c_dve.snap()
        return self.c_dve.snap()

    def ln_chunk(self, bufs, src_dma_fn, xnT_write_fn, st):
        P = self.P
        c_sp, c_te, c_act, c_dve = self.c_sp, self.c_te, self.c_act, self.c_dve
        xt, sqs, stats = bufs["xt"], bufs["sqs"], bufs["stats"]
        msq, var4, xnb, tp_ps = (bufs["msq"], bufs["var4"], bufs["xnb"],
                                 bufs["tp_ps"])
        P.op("sp", src_dma_fn, waits=[st.get("xt_free")], inc=[c_sp])
        xin = c_sp.snap()
        for t in range(4):
            P.op("dve", lambda e, t=t: e.reduce_sum(
                stats[:, t:t + 1], xt[:, t * E:(t + 1) * E], axis=AX.X),
                waits=[xin, st.get("stats_free")], inc=[c_dve])
            P.op("act", lambda e, t=t: e.activation(
                sqs[:, 0:E],
                xt[:, t * E:(t + 1) * E], AF.Square,
                accum_out=stats[:, 4 + t:5 + t]),
                waits=[xin, st.get("stats_free")],
                inc=[c_act])
        sums_done = c_dve.snap()
        sqs_done = c_act.snap()
        P.op("dve", lambda e: e.tensor_scalar(
            stats[:, 8:12], stats[:, 0:4], 1.0 / E, None, ALU.mult),
            waits=[sums_done], inc=[c_dve])
        P.op("dve", lambda e: e.tensor_tensor(
            msq[:, :], stats[:, 8:12], stats[:, 8:12], ALU.mult),
            waits=[c_dve], inc=[c_dve])
        P.op("dve", lambda e: e.scalar_tensor_tensor(
            var4[:, :], stats[:, 4:8], 1.0 / E, msq[:, :],
            ALU.mult, ALU.subtract),
            waits=[sqs_done, c_dve], inc=[c_dve])
        P.op("act", lambda e: e.activation(
            var4[:, :], var4[:, :], AF.Ln, bias=self.epsb[:, 0:1]),
            waits=[c_dve], inc=[c_act])
        P.op("act", lambda e: e.activation(
            stats[:, 12:16], var4[:, :], AF.Exp, scale=-0.5),
            waits=[c_act], inc=[c_act])
        rstd_done = c_act.snap()
        for t in range(4):
            P.op("dve", lambda e, t=t: e.tensor_scalar(
                xnb[:, t * E:(t + 1) * E], xt[:, t * E:(t + 1) * E],
                stats[:, 8 + t:9 + t], stats[:, 12 + t:13 + t],
                ALU.subtract, ALU.mult),
                waits=[rstd_done], inc=[c_dve])
        xnb_done = c_dve.snap()
        st["xt_free"] = xnb_done
        st["stats_free"] = xnb_done
        tp_ps2 = bufs["tp_ps2"]
        for b8 in range(8):
            tp = tp_ps if b8 % 2 == 0 else tp_ps2
            for k in range(8):
                idx = b8 * 8 + k
                t, ech = idx // ECH, idx % ECH
                P.op("te", lambda e, t=t, ech=ech, k=k, tp=tp: e.transpose(
                    tp[:, k * 128:(k + 1) * 128],
                    xnb[:, t * E + ech * 128: t * E + (ech + 1) * 128],
                    self.identb[:, :]),
                    waits=[xnb_done, st.get(f"tp_free{b8 % 2}")],
                    inc=([c_te] if k == 7 else []))
            tp_done = c_te.snap()
            t0, ech0 = (b8 * 8) // ECH, (b8 * 8) % ECH
            P.op("dve", lambda e, b8=b8, t0=t0, ech0=ech0:
                 xnT_write_fn(e, b8, t0, ech0),
                 waits=[tp_done, st.get("xnT_free")], inc=[c_dve])
            st[f"tp_free{b8 % 2}"] = c_dve.snap()
        return c_dve.snap()

    def build(self):
        self.declare()
        with ExitStack() as outer:
            with nc_sem_stack(self.nc, self.ctrs):
                s = self.sb
                self.identb = s(outer, "identb", [128, 128], BF16)
                self.identf = s(outer, "identf_b", [128, 128], F32)
                self.sel8b = s(outer, "sel8b", [8, 8 * 128], F32)
                self.trib = s(outer, "trib", [128, 128], F32)
                self.epsb = s(outer, "epsb", [128, 1], F32)
                self.onesbb = s(outer, "onesbb", [1, 128], BF16)
                self.onesfb = s(outer, "onesfb", [1, 128], F32)
                self.onescol = s(outer, "onescol", [128, 1], BF16)
                self.dbiasb = s(outer, "dbiasb", [128, JT], F32)
                self.bqTb = s(outer, "bqTb", [128, H], F32)
                self.bkTb = s(outer, "bkTb", [128, G], F32)
                self.bvTb = s(outer, "bvTb", [128, G], F32)
                self.brep8 = s(outer, "brep8", [128, JT * NEXP], F32)
                self.wdense = s(outer, "wdense", [128, JT * NEXP], F32)
                self.kmax64 = s(outer, "kmax64", [1, G * 16], F32)

                with self.nc.Block() as block:
                    with ExitStack() as p0:
                        self.phase0(p0)
                    self.bar()
                    if PHASES >= 1:
                        with ExitStack() as p1:
                            self.phase1(p1)
                        self.bar()
                    if PHASES >= 2:
                        self.hhT = s(outer, "hhT", [128, 8 * TOWN], BF16)
                        self.xlT = s(outer, "xlT", [128, 4 * TOWN], BF16)
                        with ExitStack() as mid:
                            self.xnT_own = s(mid, "xnT_own",
                                             [128, ECH * TOWN], BF16)
                            self.q_sb = s(mid, "q_sb", [128, H * TOWN], BF16)
                            self.kown = s(mid, "kown", [128, G * TOWN], BF16)
                            self.vown = s(mid, "vown", [128, G * JT * 129],
                                          BF16)
                            with ExitStack() as p1b:
                                self.phase1b(p1b)
                            self.bar()
                            if PHASES >= 3:
                                with ExitStack() as p15:
                                    self.phase15(p15)
                                self.bar()
                            if PHASES >= 4:
                                with ExitStack() as p2:
                                    self.phase2(p2)
                                self.bar()
                        if PHASES >= 5:
                            with ExitStack() as p3:
                                self.phase3(p3)
                            self.bar()
                    self.finale()
                    self.P.replay(block)
        return self.nc

    def phase0(self, es):
        P = self.P
        c_sp, c_te, c_dve = self.c_sp, self.c_te, self.c_dve
        for dst, src in [
            (self.identb, self.ident_in), (self.trib, self.tri_in),
            (self.identf, self.identf_in),
            (self.epsb, self.eps_in), (self.onesbb, self.onesb_in), (self.onesfb, self.onesf_in),
            (self.dbiasb, self.dbias_in), (self.bqTb, self.bqT_in),
            (self.bkTb, self.bkT_in), (self.bvTb, self.bvT_in),
        ]:
            P.op("sp", lambda e, d=dst, s=src: e.dma_start(out=d[:, :], in_=s),
                 inc=[c_sp])
        btrow = self.sb(es, "btrow_s", [1, JT * NEXP], F32)
        P.op("sp", lambda e: e.dma_start(out=btrow[:, :], in_=self.btrow_in),
             inc=[c_sp])
        P.op("sp", lambda e: e.dma_start(
            out=self.sel8b[:, :].rearrange("p (c n) -> p c n", c=8),
            in_=self.sel8_in), inc=[c_sp])
        ld = c_sp.snap()
        P.op("dve", lambda e: e.memset(self.onescol[:, :], 1.0), inc=[c_dve])
        rep_ps = self.ps(es, "rep_ps", [128, JT * NEXP])
        P.op("te", lambda e: e.matmul(rep_ps[:, 0:JT * NEXP],
                                      self.onesfb[0:1, :], btrow[0:1, :],
                                      start=True, stop=True),
             waits=[ld], inc=[c_te])
        mm2 = self.c_te.snap()
        P.op("dve", lambda e: e.tensor_copy(self.brep8[:, :],
                                            rep_ps[:, 0:JT * NEXP]),
             waits=[mm2], inc=[c_dve])

    def phase1(self, es):
        P = self.P
        c_sp, c_gp, c_te, c_act, c_dve = (self.c_sp, self.c_gp, self.c_te,
                                          self.c_act, self.c_dve)
        s = self.sb
        wkb = s(es, "wkb", [128, G * ECH * 128], BF16)
        wvb = s(es, "wvb", [128, G * ECH * 128], BF16)
        bufs2 = []
        for i in (0, 1):
            bufs2.append({
                "xt": s(es, f"xt{i}", [128, 4 * E], F32),
                "sqs": s(es, f"sqs{i}", [128, E], F32),
                "stats": s(es, f"stats{i}", [128, 24], F32),
                "msq": s(es, f"msq{i}", [128, 4], F32),
                "var4": s(es, f"var4{i}", [128, 4], F32),
            })
        xTb = [s(es, f"xTb{i}", [128, ECH * 512], BF16) for i in (0, 1)]
        statT = s(es, "statT", [8, 128], F32)
        reps = [s(es, f"reps{i}", [128, 1024], F32) for i in (0, 1)]
        t1b = [s(es, f"t1b{i}", [128, 512], F32) for i in (0, 1)]
        kst = [s(es, f"kst{i}", [128, G * 512], BF16) for i in (0, 1)]
        vst = [s(es, f"vst{i}", [128, G * 512], BF16) for i in (0, 1)]
        ukv = s(es, "ukv", [128, 2 * G], F32)
        ksq1 = s(es, "ksq1", [128, 512], BF16)
        kv_ps = [self.ps(es, f"kv_ps{i}", [128, 512]) for i in range(4)]
        st_ps = self.ps(es, "st_ps", [128, 128])
        rep_ps = self.ps(es, "rep_ps1", [128, 1024])

        for gi in range(G):
            P.op("gp", lambda e, gi=gi: e.dma_start(
                out=wkb[:, gi * ECH * 128:(gi + 1) * ECH * 128]
                .rearrange("p (c d) -> p c d", c=ECH),
                in_=self.wk_in[gi].rearrange("(c p) d -> p c d", p=128)),
                inc=[c_gp])
            P.op("gp", lambda e, gi=gi: e.dma_start(
                out=wvb[:, gi * ECH * 128:(gi + 1) * ECH * 128]
                .rearrange("p (c d) -> p c d", c=ECH),
                in_=self.wv_in[gi].rearrange("(c p) d -> p c d", p=128)),
                inc=[c_gp])
        wload = c_gp.snap()

        for gi in range(G):
            for which, wb in ((0, wkb), (1, wvb)):
                for ech in range(ECH):
                    P.op("te", lambda e, gi=gi, which=which, wb=wb, ech=ech:
                         e.matmul(
                             st_ps[:, which:which + 1],
                             wb[:, (gi * ECH + ech) * 128:
                                (gi * ECH + ech + 1) * 128],
                             self.onescol[:, 0:1],
                             start=(ech == 0), stop=(ech == ECH - 1)),
                         waits=[wload],
                         inc=([c_te] if ech == ECH - 1 else []),
                         dd=(ech > 0))
                mm = self.c_te.snap()
                P.op("dve", lambda e, gi=gi, which=which: e.tensor_copy(
                    ukv[:, which * G + gi:which * G + gi + 1],
                    st_ps[:, which:which + 1]),
                    waits=[mm], inc=[c_dve])
        u_done = c_dve.snap()

        st2 = [{}, {}]
        kst_free = [None, None]
        vst_free = [None, None]
        rep_free = [None, None]
        t1_free = [None, None]
        xT_free = [None, None]
        prev_vevac = [None]
        statT_free = [None]

        stt_prev = [None]
        rep_ps_free = [None]

        def rep_build(bi, stats_done):
            stats = bufs2[bi]["stats"]
            P.op("te", lambda e, stats=stats: e.transpose(
                st_ps[0:8, 0:128], stats[:, 12:20], self.identf[:, :]),
                waits=[stats_done, stt_prev[0], u_done, c_dve], inc=[c_te])
            P.op("dve", lambda e: e.tensor_copy(statT[:, :],
                                                st_ps[0:8, 0:128]),
                 waits=[c_te], inc=[c_dve])
            stt = c_dve.snap()
            stt_prev[0] = stt
            st2[bi]["stats_free"] = stt
            for c in range(8):
                dst = (c % 4) * 128 + (0 if c < 4 else 512)
                P.op("te", lambda e, c=c, dst=dst: e.matmul(
                    rep_ps[:, dst:dst + 128],
                    self.sel8b[:, c * 128:(c + 1) * 128],
                    statT[:, :], start=True, stop=True),
                    waits=[stt, rep_ps_free[0], rep_free[bi], c_dve],
                    inc=([c_te] if c == 7 else []))
            selmm = self.c_te.snap()
            P.op("dve", lambda e, bi=bi: e.tensor_copy(
                reps[bi][:, :], rep_ps[:, :]),
                waits=[selmm], inc=[c_dve])
            rep_ps_free[0] = self.c_dve.snap()
            return rep_ps_free[0]

        kmax_pend = []

        def emit_kmax(ch, bi, ev_done):
            for gi in range(G):
                P.op("act", lambda e, gi=gi, bi=bi: e.activation(
                    ksq1[:, :], kst[bi][:, gi * 512:(gi + 1) * 512],
                    AF.Square),
                    waits=[ev_done, c_te], inc=[c_act])
                P.op("te", lambda e: e.matmul(
                    rep_ps[0:1, 0:512], self.onescol[:, 0:1],
                    ksq1[:, :], start=True, stop=True),
                    waits=[c_act, c_dve], inc=[c_te])
                P.op("dve", lambda e, gi=gi, ch=ch: e.reduce_max(
                    self.kmax64[0:1, gi * 16 + ch:gi * 16 + ch + 1],
                    rep_ps[0:1, 0:512], axis=AX.X),
                    waits=[c_te], inc=[c_dve])

        def kv_proj(ch, bi, rep_done):
            pend, kmax_pend[:] = list(kmax_pend), []
            bank_guard = prev_vevac[0]
            for which, wb, ust, bT, dst_h, dst_s in (
                (0, wkb, 0, self.bkTb, self.k_hbm, kst),
                (1, wvb, G, self.bvTb, self.v_hbm, vst),
            ):
                for gi in range(G):
                    for ech in range(ECH):
                        P.op("te", lambda e, gi=gi, ech=ech, bi=bi, wb=wb:
                             e.matmul(
                                 kv_ps[gi][:, :],
                                 wb[:, (gi * ECH + ech) * 128:
                                    (gi * ECH + ech + 1) * 128],
                                 xTb[bi][:, ech * 512:(ech + 1) * 512],
                                 start=(ech == 0), stop=(ech == ECH - 1)),
                             waits=[st2[bi]["xT_ld"], wload, kst_free[bi],
                                    vst_free[bi], bank_guard],
                             inc=([c_te] if ech == ECH - 1 else []),
                             dd=(ech > 0))
                mm_done = self.c_te.snap()
                for gi in range(G):
                    tf = t1_free[gi % 2]
                    P.op("dve", lambda e, gi=gi, bi=bi: e.tensor_tensor(
                        t1b[gi % 2][:, :], kv_ps[gi][:, :],
                        reps[bi][:, 0:512], ALU.mult),
                        waits=[mm_done, rep_done, u_done, tf], inc=[c_dve])
                    P.op("dve", lambda e, gi=gi, bi=bi, ust=ust:
                         e.scalar_tensor_tensor(
                             t1b[gi % 2][:, :], reps[bi][:, 512:1024],
                             ukv[:, ust + gi:ust + gi + 1],
                             t1b[gi % 2][:, :], ALU.mult, ALU.add),
                         waits=[c_dve], inc=[c_dve])
                    fx = c_dve.snap()
                    P.op("act", lambda e, gi=gi, bi=bi, bT=bT, dst_s=dst_s:
                         e.activation(
                             dst_s[bi][:, gi * 512:(gi + 1) * 512],
                             t1b[gi % 2][:, :], AF.Identity,
                             bias=bT[:, gi:gi + 1]),
                         waits=[fx], inc=[c_act])
                    t1_free[gi % 2] = self.c_act.snap()
                ev_done = self.c_act.snap()
                bank_guard = self.c_dve.snap()
                if which == 0:
                    kmax_pend.append(
                        lambda ch=ch, bi=bi, ev=ev_done: emit_kmax(ch, bi, ev))
                if which == 1:
                    prev_vevac[0] = bank_guard
                    xT_free[bi] = mm_done
                for gi in range(G):
                    P.op("sp", lambda e, ch=ch, gi=gi, bi=bi, dst_h=dst_h,
                         dst_s=dst_s: e.dma_start(
                             out=dst_h[gi, :, ch * 512:(ch + 1) * 512],
                             in_=dst_s[bi][:, gi * 512:(gi + 1) * 512]),
                         waits=[ev_done], inc=[c_sp])
                if which == 0:
                    kst_free[bi] = self.c_sp.snap()
                else:
                    vst_free[bi] = self.c_sp.snap()
            for f in pend:
                f()

        prev = None
        for ch in range(T // 512):
            bi = ch % 2
            bufs = bufs2[bi]
            st = st2[bi]

            def src_dma(e, ch=ch, bufs=bufs):
                return e.dma_start(
                    out=bufs["xt"][:, :].rearrange("p (t e) -> p t e", t=4),
                    in_=self.x_full[512 * ch:512 * (ch + 1), :]
                    .rearrange("(t p) e -> p t e", p=128))

            P.op("gp", lambda e, ch=ch, bi=bi: e.dma_start(
                out=xTb[bi][:, :].rearrange("p (c n) -> p c n", c=ECH),
                in_=self.xT_full[:, 512 * ch:512 * (ch + 1)]
                .rearrange("(c p) n -> p c n", p=128)),
                waits=[xT_free[bi]], inc=[c_gp])
            st2[bi]["xT_ld"] = self.c_gp.snap()

            stats_done = self.ln_stats(bufs, src_dma, st)
            if prev is not None:
                pch, pbi = prev
                kv_proj(pch, pbi, st2[pbi]["rep_done"])
            st2[bi]["rep_done"] = rep_build(bi, stats_done)
            prev = (ch, bi)
        pch, pbi = prev
        kv_proj(pch, pbi, st2[pbi]["rep_done"])
        for f in kmax_pend:
            f()
        kmax_pend.clear()

        if DEBUG:
            fin = self.c_sp.snap()
            self.dbg_khbm = self.dram_out("dbg_khbm", [G, D, T], BF16)
            self.dbg_vhbm = self.dram_out("dbg_vhbm", [G, D, T], BF16)
            P.op("sp", lambda e: e.dma_start(out=self.dbg_khbm,
                                             in_=self.k_hbm), waits=[fin],
                 inc=[c_sp])
            P.op("sp", lambda e: e.dma_start(out=self.dbg_vhbm,
                                             in_=self.v_hbm), waits=[fin],
                 inc=[c_sp])

    def phase1b(self, es):
        P = self.P
        c_sp, c_gp, c_te, c_act, c_dve = (self.c_sp, self.c_gp, self.c_te,
                                          self.c_act, self.c_dve)
        s = self.sb
        wkb = [s(es, f"wkb2{i}", [128, ECH * 128], BF16) for i in (0, 1)]
        wvb = [s(es, f"wvb2{i}", [128, ECH * 128], BF16) for i in (0, 1)]
        wqb = [s(es, f"wqb{i}", [128, ECH * 128], BF16) for i in (0, 1)]
        bufs = {
            "xt": s(es, "xtb", [128, 4 * E], F32),
            "sqs": s(es, "sqsb", [128, E], F32),
            "stats": s(es, "statsb", [128, 16], F32),
            "msq": s(es, "msqb", [128, 4], F32),
            "var4": s(es, "var4b", [128, 4], F32),
            "xnb": s(es, "xnbb", [128, 4 * E], BF16),
            "tp_ps": self.ps(es, "tp_psb", [128, 1024], BF16),
            "tp_ps2": self.ps(es, "tp_psb2", [128, 1024], BF16),
        }
        mm_ps = [self.ps(es, f"mm_psb{i}", [128, 512]) for i in range(4)]
        vtb = s(es, "vtb", [128, 1024], BF16)
        vt_free = None
        vtp_free = [None, None]

        xnT_view = self.xnT_own[:, :].rearrange("p (c n) -> p c n", c=ECH)
        st = {}
        for ch in range(TOWN // 512):
            def src_dma(e, ch=ch):
                return e.dma_start(
                    out=bufs["xt"][:, :].rearrange("p (t e) -> p t e", t=4),
                    in_=self.x_own[512 * ch:512 * (ch + 1), :]
                    .rearrange("(t p) e -> p t e", p=128))

            def xnT_wr(e, b8, t0, ech0, ch=ch):
                tp = bufs["tp_ps"] if b8 % 2 == 0 else bufs["tp_ps2"]
                return e.tensor_copy(
                    xnT_view[:, ech0:ech0 + 8,
                             ch * 512 + t0 * 128: ch * 512 + (t0 + 1) * 128],
                    tp[:, :].rearrange("p (k n) -> p k n", k=8))

            self.ln_chunk(bufs, src_dma, xnT_wr, st)
        xnT_done = self.c_dve.snap()

        wq_free = [None, None]
        ps_free = [None, None, None, None]

        def issue_wq(h):
            wi = h % 2
            P.op("gp", lambda e, h=h, wi=wi: e.dma_start(
                out=wqb[wi][:, :].rearrange("p (c d) -> p c d", c=ECH),
                in_=self.wq_in[h].rearrange("(c p) d -> p c d", p=128)),
                waits=[wq_free[wi]], inc=[c_gp])
            return c_gp.snap()

        wq_ld = issue_wq(0)
        for h in range(H):
            wi = h % 2
            wq_next = issue_wq(h + 1) if h + 1 < H else None
            for tc in range(2):
                pi = tc
                for ech in range(ECH):
                    P.op("te", lambda e, wi=wi, ech=ech, tc=tc, pi=pi: e.matmul(
                        mm_ps[pi][:, :],
                        wqb[wi][:, ech * 128:(ech + 1) * 128],
                        xnT_view[:, ech, tc * 512:(tc + 1) * 512],
                        start=(ech == 0), stop=(ech == ECH - 1)),
                        waits=([xnT_done, wq_ld, ps_free[pi]]
                               + ([wq_next] if tc == 1 and ech == 0 else [])),
                        inc=([c_te] if ech == ECH - 1 else []),
                        dd=(h >= 1))
                qmm = self.c_te.snap()
                P.op("act", lambda e, h=h, tc=tc, pi=pi: e.activation(
                    self.q_sb[:, h * TOWN + tc * 512:
                              h * TOWN + (tc + 1) * 512],
                    mm_ps[pi][:, :], AF.Identity, bias=self.bqTb[:, h:h + 1]),
                    waits=[qmm], inc=[c_act])
                ps_free[pi] = self.c_act.snap()
            wq_free[wi] = self.c_act.snap()
            wq_ld = wq_next

        P.op("dve", lambda e: e.memset(
            self.vown[:, :].rearrange("p (q s) -> p q s", s=129)
            [:, :, 128:129], 1.0), inc=[c_dve])
        ones_set = self.c_dve.snap()

        kvw_free = [None, None]

        def issue_kv(gi):
            wi = gi % 2
            P.op("gp", lambda e, gi=gi, wi=wi: e.dma_start(
                out=wkb[wi][:, :].rearrange("p (c d) -> p c d", c=ECH),
                in_=self.wk_in[gi].rearrange("(c p) d -> p c d", p=128)),
                waits=[kvw_free[wi]], inc=[c_gp])
            P.op("gp", lambda e, gi=gi, wi=wi: e.dma_start(
                out=wvb[wi][:, :].rearrange("p (c d) -> p c d", c=ECH),
                in_=self.wv_in[gi].rearrange("(c p) d -> p c d", p=128)),
                waits=[kvw_free[wi]], inc=[c_gp])
            return c_gp.snap()

        kv_wload = issue_kv(0)
        for gi in range(G):
            wi = gi % 2
            kv_next = issue_kv(gi + 1) if gi + 1 < G else None
            for tc in range(2):
                pi = 2 + tc
                for ech in range(ECH):
                    P.op("te", lambda e, wi=wi, tc=tc, ech=ech, pi=pi: e.matmul(
                        mm_ps[pi][:, :],
                        wkb[wi][:, ech * 128:(ech + 1) * 128],
                        xnT_view[:, ech, tc * 512:(tc + 1) * 512],
                        start=(ech == 0), stop=(ech == ECH - 1)),
                        waits=[xnT_done, kv_wload, ps_free[pi]],
                        inc=([c_te] if ech == ECH - 1 else []),
                        dd=(gi >= 1))
                kmm = self.c_te.snap()
                P.op("act", lambda e, gi=gi, tc=tc, pi=pi: e.activation(
                    self.kown[:, gi * TOWN + tc * 512:
                              gi * TOWN + (tc + 1) * 512],
                    mm_ps[pi][:, :], AF.Identity, bias=self.bkTb[:, gi:gi + 1]),
                    waits=[kmm], inc=[c_act])
                ps_free[pi] = self.c_act.snap()
            for tc in range(2):
                pi = 2 + tc
                for ech in range(ECH):
                    P.op("te", lambda e, wi=wi, tc=tc, ech=ech, pi=pi: e.matmul(
                        mm_ps[pi][:, :],
                        wvb[wi][:, ech * 128:(ech + 1) * 128],
                        xnT_view[:, ech, tc * 512:(tc + 1) * 512],
                        start=(ech == 0), stop=(ech == ECH - 1)),
                        waits=([xnT_done, kv_wload, ps_free[pi]]
                               + ([kv_next] if tc == 1 and ech == 0 else [])),
                        inc=([c_te] if ech == ECH - 1 else []),
                        dd=(gi >= 1))
                vmm = self.c_te.snap()
                P.op("act", lambda e, gi=gi, tc=tc, pi=pi: e.activation(
                    vtb[:, tc * 512:(tc + 1) * 512], mm_ps[pi][:, :],
                    AF.Identity, bias=self.bvTb[:, gi:gi + 1]),
                    waits=[vmm, vt_free], inc=[c_act])
                ps_free[pi] = self.c_act.snap()
            vt_done = self.c_act.snap()
            for half in range(2):
                tp = bufs["tp_ps"] if half == 0 else bufs["tp_ps2"]
                for k in range(4):
                    t = half * 4 + k
                    P.op("te", lambda e, t=t, k=k, tp=tp: e.transpose(
                        tp[:, k * 128:(k + 1) * 128],
                        vtb[:, t * 128:(t + 1) * 128], self.identb[:, :]),
                        waits=[vt_done, vtp_free[half]],
                        inc=([c_te] if k == 3 else []))
                tdone = self.c_te.snap()
                P.op("dve", lambda e, gi=gi, half=half, tp=tp: e.tensor_copy(
                    self.vown[:, :].rearrange("p (q s) -> p q s", s=129)
                    [:, gi * JT + half * 4: gi * JT + (half + 1) * 4, 0:128],
                    tp[:, 0:512].rearrange("p (k n) -> p k n", k=4)),
                    waits=[tdone, ones_set], inc=[c_dve])
                vtp_free[half] = self.c_dve.snap()
            vt_free = self.c_dve.snap()
            kvw_free[wi] = self.c_dve.snap()
            kv_wload = kv_next

        if DEBUG:
            fin = [self.c_act.snap(), self.c_dve.snap()]
            self.dbg_xnt = self.dram_out("dbg_xnt", [128, ECH * TOWN], BF16)
            self.dbg_q = self.dram_out("dbg_q", [128, H * TOWN], BF16)
            self.dbg_kown = self.dram_out("dbg_kown", [128, G * TOWN], BF16)
            self.dbg_vown = self.dram_out("dbg_vown", [128, G * JT * 129], BF16)
            for dst, src in [(self.dbg_xnt, self.xnT_own),
                             (self.dbg_q, self.q_sb),
                             (self.dbg_kown, self.kown),
                             (self.dbg_vown, self.vown)]:
                P.op("sp", lambda e, d=dst, s=src: e.dma_start(out=d,
                                                               in_=s[:, :]),
                     waits=fin, inc=[c_sp])

    def phase15(self, es):
        P = self.P
        c_sp, c_gp, c_te, c_act, c_dve = (self.c_sp, self.c_gp, self.c_te,
                                          self.c_act, self.c_dve)
        s = self.sb
        sw1b = s(es, "sw1b", [128, ECH * HID], BF16)
        swgb = s(es, "swgb", [128, ECH * HID], BF16)
        wdb = s(es, "wdb", [128, ECH * LAT], BF16)
        wgb = s(es, "wgb", [128, ECH * NEXP], BF16)
        prob = s(es, "prob", [128, JT * NEXP], F32)
        vbuf = s(es, "vbuf_g", [128, JT * NEXP], F32)
        cnt = [s(es, f"cnt{i}", [128, JT * NEXP], F32) for i in (0, 1)]
        rbuf = s(es, "rbuf", [128, HID], BF16)
        rbuf2 = s(es, "rbuf2", [128, HID], BF16)
        rsum = s(es, "rsum", [128, JT], F32)
        z_ps = self.ps(es, "z_ps", [128, 512])
        h_ps = self.ps(es, "h_ps", [128, 1024])
        g_ps = self.ps(es, "g_ps", [128, 1024])

        xnT_view = self.xnT_own[:, :].rearrange("p (c n) -> p c n", c=ECH)

        P.op("gp", lambda e: e.dma_start(
            out=sw1b[:, :].rearrange("p (c n) -> p c n", c=ECH),
            in_=self.sw1T_in.rearrange("(c p) n -> p c n", p=128)),
            inc=[c_gp])
        P.op("gp", lambda e: e.dma_start(
            out=swgb[:, :].rearrange("p (c n) -> p c n", c=ECH),
            in_=self.swgT_in.rearrange("(c p) n -> p c n", p=128)),
            inc=[c_gp])
        P.op("gp", lambda e: e.dma_start(
            out=wdb[:, :].rearrange("p (c n) -> p c n", c=ECH),
            in_=self.wdT_in.rearrange("(c p) n -> p c n", p=128)),
            inc=[c_gp])
        P.op("gp", lambda e: e.dma_start(
            out=wgb[:, :].rearrange("p (c n) -> p c n", c=ECH),
            in_=self.wgT_in.rearrange("(c p) n -> p c n", p=128)),
            inc=[c_gp])
        wload = c_gp.snap()

        for t in range(JT):
            for ech in range(ECH):
                P.op("te", lambda e, t=t, ech=ech: e.matmul(
                    z_ps[:, t * NEXP:(t + 1) * NEXP],
                    xnT_view[:, ech, t * 128:(t + 1) * 128],
                    wgb[:, ech * NEXP:(ech + 1) * NEXP],
                    start=(ech == 0), stop=(ech == ECH - 1)),
                    waits=[wload],
                    inc=([c_te] if ech == ECH - 1 and t == JT - 1 else []))
        zmm = self.c_te.snap()
        P.op("act", lambda e: e.activation(
            prob[:, :], z_ps[:, 0:JT * NEXP], AF.Exp, scale=-1.0),
            waits=[zmm], inc=[c_act])
        P.op("dve", lambda e: e.tensor_scalar(
            prob[:, :], prob[:, :], 1.0, None, ALU.add),
            waits=[c_act], inc=[c_dve])
        P.op("dve", lambda e: e.reciprocal(prob[:, :], prob[:, :]),
            waits=[c_dve], inc=[c_dve])
        P.op("dve", lambda e: e.tensor_tensor(
            vbuf[:, :], prob[:, :], self.brep8[:, :], ALU.add),
            waits=[c_dve], inc=[c_dve])
        P.op("dve", lambda e: e.memset(cnt[0][:, :], 0.0),
            waits=[c_dve], inc=[c_dve])
        cur = 0
        for ep in range(NEXP):
            for t in range(JT):
                P.op("dve", lambda e, ep=ep, t=t, cur=cur: e.scalar_tensor_tensor(
                    cnt[1 - cur][:, t * NEXP:(t + 1) * NEXP],
                    vbuf[:, t * NEXP:(t + 1) * NEXP],
                    vbuf[:, t * NEXP + ep:t * NEXP + ep + 1],
                    cnt[cur][:, t * NEXP:(t + 1) * NEXP],
                    ALU.is_lt, ALU.add),
                    waits=[c_dve], inc=[c_dve])
            cur = 1 - cur
        P.op("dve", lambda e, cur=cur: e.tensor_scalar(
            cnt[1 - cur][:, :], cnt[cur][:, :], 7.5, None, ALU.is_le),
            waits=[c_dve], inc=[c_dve])
        P.op("dve", lambda e, cur=cur: e.tensor_tensor(
            self.wdense[:, :], prob[:, :], cnt[1 - cur][:, :], ALU.mult),
            waits=[c_dve], inc=[c_dve])
        P.op("dve", lambda e: e.reduce_sum(
            rsum[:, :].rearrange("p (j o) -> p j o", o=1),
            self.wdense[:, :].rearrange("p (j n) -> p j n", j=JT), axis=AX.X),
            waits=[c_dve], inc=[c_dve])
        P.op("dve", lambda e: e.reciprocal(rsum[:, :], rsum[:, :]),
            waits=[c_dve], inc=[c_dve])
        for t in range(JT):
            P.op("dve", lambda e, t=t: e.tensor_scalar(
                self.wdense[:, t * NEXP:(t + 1) * NEXP],
                self.wdense[:, t * NEXP:(t + 1) * NEXP],
                rsum[:, t:t + 1], RSCALE, ALU.mult, ALU.mult),
                waits=[c_dve], inc=[c_dve])

        r_free = None
        for hch in range(8):
            for tc in range(2):
                for ech in range(ECH):
                    P.op("te", lambda e, hch=hch, tc=tc, ech=ech: e.matmul(
                        h_ps[:, tc * 512:(tc + 1) * 512],
                        sw1b[:, ech * HID + hch * 128:
                             ech * HID + (hch + 1) * 128],
                        xnT_view[:, ech, tc * 512:(tc + 1) * 512],
                        start=(ech == 0), stop=(ech == ECH - 1)),
                        waits=[wload, c_dve], inc=[], dd=(ech > 0))
                    P.op("te", lambda e, hch=hch, tc=tc, ech=ech: e.matmul(
                        g_ps[:, tc * 512:(tc + 1) * 512],
                        swgb[:, ech * HID + hch * 128:
                             ech * HID + (hch + 1) * 128],
                        xnT_view[:, ech, tc * 512:(tc + 1) * 512],
                        start=(ech == 0), stop=(ech == ECH - 1)),
                        waits=[wload, c_dve],
                        inc=([c_te] if ech == ECH - 1 and tc == 1 else []),
                        dd=(ech > 0))
            hmm = self.c_te.snap()
            P.op("act", lambda e: e.activation(
                rbuf[:, :], g_ps[:, :], AF.Relu),
                waits=[hmm, r_free], inc=[c_act])
            P.op("dve", lambda e: e.tensor_tensor(
                rbuf2[:, :], rbuf[:, :], rbuf[:, :], ALU.mult),
                waits=[c_act], inc=[c_dve])
            P.op("dve", lambda e, hch=hch: e.tensor_tensor(
                self.hhT[:, hch * TOWN:(hch + 1) * TOWN],
                h_ps[:, :], rbuf2[:, :], ALU.mult),
                waits=[c_dve], inc=[c_dve])
            r_free = self.c_dve.snap()

        for lch in range(4):
            for tc in range(2):
                for ech in range(ECH):
                    P.op("te", lambda e, lch=lch, tc=tc, ech=ech: e.matmul(
                        h_ps[:, tc * 512:(tc + 1) * 512],
                        wdb[:, ech * LAT + lch * 128:
                            ech * LAT + (lch + 1) * 128],
                        xnT_view[:, ech, tc * 512:(tc + 1) * 512],
                        start=(ech == 0), stop=(ech == ECH - 1)),
                        waits=[wload, c_act, c_dve],
                        inc=([c_te] if ech == ECH - 1 and tc == 1 else []),
                        dd=(ech > 0))
            xmm = self.c_te.snap()
            P.op("act", lambda e, lch=lch: e.activation(
                self.xlT[:, lch * TOWN:(lch + 1) * TOWN], h_ps[:, :],
                AF.Identity),
                waits=[xmm], inc=[c_act])

        if DEBUG:
            fin = [self.c_act.snap(), self.c_dve.snap()]
            self.dbg_w = self.dram_out("dbg_w", [128, JT * NEXP], F32)
            self.dbg_hh = self.dram_out("dbg_hh", [128, 8 * TOWN], BF16)
            self.dbg_xl = self.dram_out("dbg_xl", [128, 4 * TOWN], BF16)
            for dst, src in [(self.dbg_w, self.wdense), (self.dbg_hh, self.hhT),
                             (self.dbg_xl, self.xlT)]:
                P.op("sp", lambda e, d=dst, s=src: e.dma_start(out=d,
                                                               in_=s[:, :]),
                     waits=fin, inc=[c_sp])

    def phase2(self, es):
        P = self.P
        c_sp, c_gp, c_te, c_act, c_dve = (self.c_sp, self.c_gp, self.c_te,
                                          self.c_act, self.c_dve)
        s = self.sb
        NS = 2
        kbuf = [s(es, f"kbuf{i}", [128, T], BF16) for i in (0, 1)]
        vbuf = [s(es, f"vbuf{i}", [128, 64 * 144], BF16) for i in (0, 1)]
        pt = [s(es, f"pt{i}", [128, 1024], BF16) for i in range(NS)]
        sst = s(es, "sst", [128, JT * 128], F32)
        qsq = s(es, "qsqb", [128, 4 * TOWN], BF16)
        ksq = s(es, "ksqb", [128, 512], BF16)
        kmaxb = s(es, "kmaxb", [1, 20], F32)
        cbuf = s(es, "cbuf", [1, 16], F32)
        crep4 = s(es, "crep4", [128, 4], F32)
        dbadj4 = s(es, "dbadj4", [128, 4 * JT], F32)
        rcpb = s(es, "rcpb", [128, JT], F32)
        zbuf = s(es, "zbuf", [128, 512], BF16)
        s_ps = [self.ps(es, f"s_ps{i}", [128, 1024]) for i in range(NS)]
        o_ps = [self.ps(es, f"o_ps{i}", [128, 512]) for i in range(3)]
        cp_ps = self.ps(es, "cp_ps", [128, 512])

        def o_tile(j):
            return o_ps[j // 3][:, (j % 3) * 129:(j % 3) * 129 + 129]

        for i in (0, 1):
            P.op("dve", lambda e, i=i: e.memset(
                vbuf[i][:, :].rearrange("p (q s) -> p q s", s=144)
                [:, :, 128:129], 1.0), inc=[c_dve])
        P.op("dve", lambda e: e.memset(zbuf[:, :], 0.0), inc=[c_dve])
        ones_set = c_dve.snap()

        exp_done = [None] * NS
        av_done = [None] * NS
        head_eplg = None
        sst_free = None
        group_done = [None, None]
        slot = 0

        for gi in range(G):
            bi = gi % 2
            P.op("gp", lambda e, gi=gi, bi=bi: e.dma_start(
                out=kbuf[bi][:, :], in_=self.k_hbm[gi]),
                waits=[group_done[bi]], inc=[c_gp])
            for qt in range(64):
                P.op("sp", lambda e, gi=gi, bi=bi, qt=qt: e.dma_start_transpose(
                    vbuf[bi][:, qt * 144:qt * 144 + 128],
                    self.v_hbm[gi, :, qt * 128:(qt + 1) * 128]),
                    waits=([group_done[bi], ones_set] if qt == 0 else []),
                    inc=[c_sp])
            ready_sp = self.c_sp.snap()
            ready_gp = self.c_gp.snap()
            ready = ready_sp

            guard = [w for w in exp_done if w] + [ready_sp, ready_gp]
            P.op("dve", lambda e, gi=gi: e.reduce_max(
                kmaxb[0:1, 0:1], self.kmax64[0:1, gi * 16:(gi + 1) * 16],
                axis=AX.X), waits=[c_dve], inc=[c_dve])
            P.op("act", lambda e: e.activation(
                kmaxb[0:1, 1:2], kmaxb[0:1, 0:1], AF.Ln),
                waits=[c_dve], inc=[c_act])
            lnk_done = c_act.snap()

            P.op("act", lambda e, gi=gi: e.activation(
                qsq[:, :], self.q_sb[:, gi * 4 * TOWN:(gi + 1) * 4 * TOWN],
                AF.Square), waits=[self.c_dve.snap()], inc=[c_act])
            qsq_done = c_act.snap()
            qprev = [None, None]
            for i in range(8):
                sl = i % 2
                P.op("te", lambda e, i=i, sl=sl: e.matmul(
                    s_ps[1][0:1, sl * 512:(sl + 1) * 512],
                    self.onescol[:, 0:1],
                    qsq[:, i * 512:(i + 1) * 512],
                    start=True, stop=True),
                    waits=guard + [qsq_done, qprev[sl]], inc=[c_te])
                mm = self.c_te.snap()
                P.op("dve", lambda e, i=i, sl=sl: e.reduce_max(
                    cbuf[0:1, i:i + 1],
                    s_ps[1][0:1, sl * 512:(sl + 1) * 512], axis=AX.X),
                    waits=[mm], inc=[c_dve])
                qprev[sl] = c_dve.snap()
            P.op("dve", lambda e: e.reduce_max(
                cbuf[0:1, 8:12].rearrange("o (h x) -> o h x", h=4),
                cbuf[0:1, 0:8].rearrange("o (h x) -> o h x", h=4),
                axis=AX.X),
                waits=[qprev[0], qprev[1]], inc=[c_dve])
            P.op("act", lambda e: e.activation(
                cbuf[0:1, 12:16], cbuf[0:1, 8:12], AF.Ln),
                waits=[c_dve], inc=[c_act])
            P.op("dve", lambda e: e.tensor_scalar(
                cbuf[0:1, 8:12], cbuf[0:1, 12:16], kmaxb[0:1, 1:2], None,
                ALU.add), waits=[c_act, lnk_done], inc=[c_dve])
            P.op("act", lambda e: e.activation(
                cbuf[0:1, 12:16], cbuf[0:1, 8:12], AF.Exp, scale=0.5),
                waits=[c_dve], inc=[c_act])
            P.op("dve", lambda e: e.tensor_scalar(
                cbuf[0:1, 8:12], cbuf[0:1, 12:16], -SCALE_ATT, None,
                ALU.mult), waits=[c_act], inc=[c_dve])
            P.op("te", lambda e: e.matmul(
                cp_ps[:, 0:4], self.onesfb[0:1, :], cbuf[0:1, 8:12],
                start=True, stop=True), waits=[c_dve], inc=[c_te])
            P.op("dve", lambda e: e.tensor_copy(crep4[:, :], cp_ps[:, 0:4]),
                 waits=[c_te], inc=[c_dve])
            for hh in range(4):
                P.op("dve", lambda e, hh=hh: e.tensor_scalar(
                    dbadj4[:, hh * JT:(hh + 1) * JT], self.dbiasb[:, :],
                    crep4[:, hh:hh + 1], None, ALU.add),
                    waits=[c_dve], inc=[c_dve])
            cprep = c_dve.snap()

            for hh in range(4):
                h = gi * 4 + hh
                o_free = head_eplg
                qoff = h * TOWN
                for b in range(3):
                    P.op("te", lambda e, b=b: e.matmul(
                        o_ps[b][:, :], self.identb[:, :], zbuf[:, :],
                        start=True, stop=False),
                        waits=[ones_set, o_free, cprep],
                        inc=([c_te] if b == 2 else []),
                        dd=(b > 0))

                def emit_S(jobs, sl, cprep):
                    alln = sum(len(j["s_mms"]) for j in jobs)
                    i = 0
                    for base, jb in zip((0, 512), jobs):
                        for dst_off, w, lf, rf in jb["s_mms"]:
                            i += 1
                            P.op("te", lambda e, dst_off=base + dst_off, w=w,
                                 lf=lf, rf=rf, sl=sl: e.matmul(
                                     s_ps[sl][:, dst_off:dst_off + w],
                                     lf(), rf(), start=True, stop=True),
                                 waits=[ready, cprep, exp_done[sl]],
                                 inc=([c_te] if i == alln else []),
                                 dd=True)
                    return self.c_te.snap()

                def emit_EA(jobs, sl, smm):
                    extra = []
                    ntri = sum(4 for jb in jobs if jb["tri"])
                    it = 0
                    for base, jb in zip((0, 512), jobs):
                        if not jb["tri"]:
                            continue
                        for i4 in range(4):
                            it += 1
                            P.op("dve", lambda e, o=base + i4 * 128, sl=sl:
                                 e.tensor_tensor(
                                     s_ps[sl][:, o:o + 128],
                                     s_ps[sl][:, o:o + 128],
                                     self.trib[:, :], ALU.add),
                                 waits=([smm] if it == 1 else []),
                                 inc=([c_dve] if it == ntri else []))
                    if ntri:
                        extra = [self.c_dve.snap()]
                    width = (512 + jobs[1]["width"]) if len(jobs) == 2 \
                        else jobs[0]["width"]
                    P.op("act", lambda e, sl=sl, width=width,
                         bias_ap=jobs[0]["bias"]: e.activation(
                             pt[sl][:, 0:width], s_ps[sl][:, 0:width],
                             AF.Exp, bias=bias_ap, scale=SCALE_ATT),
                         waits=[smm, av_done[sl]] + extra, inc=[c_act])
                    exp_done[sl] = c_act.snap()
                    na = sum(len(jb["avs"]) for jb in jobs)
                    i = 0
                    for base, jb in zip((0, 512), jobs):
                        for j, pt_off, vf in jb["avs"]:
                            i += 1
                            P.op("te", lambda e, j=j, pt_off=base + pt_off,
                                 vf=vf, sl=sl: e.matmul(
                                     o_tile(j), pt[sl][:, pt_off:pt_off + 128],
                                     vf(), start=False, stop=False),
                                 waits=[exp_done[sl]],
                                 inc=([c_te] if i == na else []))
                    av_done[sl] = self.c_te.snap()

                jl = []
                for kt in range(56):
                    jmin = kt // 8 + 1
                    w_all = (8 - jmin) * 128
                    off = 0
                    while off < w_all:
                        w = min(512, w_all - off)
                        q0 = qoff + jmin * 128 + off
                        jlo = jmin + off // 128
                        lf = (lambda bi=bi, kt=kt:
                              kbuf[bi][:, kt * 128:(kt + 1) * 128])
                        rf = (lambda q0=q0, w=w: self.q_sb[:, q0:q0 + w])
                        avs = [(j, (j - jlo) * 128,
                                (lambda bi=bi, kt=kt:
                                 vbuf[bi][:, kt * 144:kt * 144 + 129]))
                               for j in range(jlo, jlo + w // 128)]
                        jl.append(dict(s_mms=[(0, w, lf, rf)], width=w,
                                       bias=crep4[:, hh:hh + 1], avs=avs,
                                       tri=False, bkey=("c", hh)))
                        off += w
                for dd in range(8):
                    for half in range(2):
                        smms = []
                        avs = []
                        for i in range(4):
                            j = half * 4 + i
                            kk = 8 * j + dd
                            lf = (lambda bi=bi, kk=kk:
                                  kbuf[bi][:, kk * 128:(kk + 1) * 128])
                            q0 = qoff + j * 128
                            rf = (lambda q0=q0: self.q_sb[:, q0:q0 + 128])
                            smms.append((i * 128, 128, lf, rf))
                            avs.append((j, i * 128,
                                        (lambda bi=bi, kk=kk:
                                         vbuf[bi][:, kk * 144:kk * 144 + 129])))
                        jl.append(dict(
                            s_mms=smms, width=512,
                            bias=dbadj4[:, hh * JT + dd:hh * JT + dd + 1],
                            avs=avs, tri=False, bkey=("d", hh, dd)))
                for half in range(2):
                    smms = []
                    avs = []
                    for i in range(4):
                        j = half * 4 + i
                        k0 = gi * TOWN + j * 128
                        lf = (lambda k0=k0: self.kown[:, k0:k0 + 128])
                        q0 = qoff + j * 128
                        rf = (lambda q0=q0: self.q_sb[:, q0:q0 + 128])
                        smms.append((i * 128, 128, lf, rf))
                        avs.append((j, i * 128,
                                    (lambda q=gi * JT + j:
                                     self.vown[:, q * 129:(q + 1) * 129])))
                    jl.append(dict(s_mms=smms, width=512,
                                   bias=crep4[:, hh:hh + 1], avs=avs,
                                   tri=True, bkey=("t", hh)))
                plist = []
                p = 0
                while p < len(jl):
                    if (p + 1 < len(jl)
                            and jl[p]["bkey"] == jl[p + 1]["bkey"]):
                        plist.append([jl[p], jl[p + 1]])
                        p += 2
                    else:
                        plist.append([jl[p]])
                        p += 1
                pend = None
                for pr in plist:
                    sl = slot
                    slot = (slot + 1) % NS
                    smm = emit_S(pr, sl, cprep)
                    if pend is not None:
                        emit_EA(*pend)
                    pend = (pr, sl, smm)
                emit_EA(*pend)
                all_av = self.c_te.snap()

                for j in range(JT):
                    P.op("dve", lambda e, j=j: e.reciprocal(
                        rcpb[:, j:j + 1], o_tile(j)[:, 128:129]),
                        waits=[all_av, c_dve], inc=[c_dve])
                    P.op("dve", lambda e, j=j: e.tensor_scalar(
                        sst[:, j * 128:(j + 1) * 128], o_tile(j)[:, 0:128],
                        rcpb[:, j:j + 1], None, ALU.mult),
                        waits=[c_dve, sst_free], inc=[c_dve])
                head_eplg = self.c_dve.snap()
                P.op("sp", lambda e, h=h: e.dma_start(
                    out=self.s_hbm.rearrange("(j p) e -> p j e", p=128)
                    [:, :, h * 128:(h + 1) * 128],
                    in_=sst[:, :].rearrange("p (j d) -> p j d", j=JT)),
                    waits=[head_eplg], inc=[c_sp])
                sst_free = self.c_sp.snap()
            group_done[bi] = self.c_te.snap()

        if DEBUG:
            fin = self.c_sp.snap()
            self.dbg_s = self.dram_out("dbg_s", [TOWN, E], F32)
            P.op("sp", lambda e: e.dma_start(out=self.dbg_s, in_=self.s_hbm),
                 waits=[fin], inc=[c_sp])

    def phase3(self, es):
        P = self.P
        c_sp, c_gp, c_te, c_act, c_dve = (self.c_sp, self.c_gp, self.c_te,
                                          self.c_act, self.c_dve)
        s = self.sb
        acc = [s(es, f"acc{i}", [128, JT * 512], F32) for i in (0, 1)]

        with ExitStack() as pa:
            rw1b = [self.sb(pa, f"rw1b{i}", [128, 4 * HID], BF16)
                    for i in (0, 1)]
            rwgb = [self.sb(pa, f"rwgb{i}", [128, 4 * HID], BF16)
                    for i in (0, 1)]
            rw2b = [self.sb(pa, f"rw2b{i}", [128, 8 * LAT], BF16)
                    for i in (0, 1)]
            abT = self.sb(pa, "abT", [128, 8 * TOWN], BF16)
            rbu = [self.sb(pa, f"rb3{i}", [128, 512], BF16) for i in (0, 1)]
            rb2u = [self.sb(pa, f"rb23{i}", [128, 512], BF16) for i in (0, 1)]
            ab_ps = [self.ps(pa, f"ab_ps{i}", [128, 1024]) for i in (0, 1)]
            y_ps = [self.ps(pa, f"y_ps{i}", [128, 512]) for i in (0, 1)]
            unit_free = [None, None]
            rb_free = [None, None]

            xlT_view = self.xlT[:, :].rearrange("p (l n) -> p l n", l=4)
            P.op("dve", lambda e: e.memset(acc[0][:, :], 0.0), inc=[c_dve])
            acc_zero = c_dve.snap()

            w_free = [None, None]
            y_free = [None, None]

            def issue_wloads(ex):
                wi = ex % 2
                P.op("gp", lambda e, ex=ex, wi=wi: e.dma_start(
                    out=rw1b[wi][:, :].rearrange("p (l n) -> p l n", l=4),
                    in_=self.rw1T_in[ex].rearrange("(l p) n -> p l n", p=128)),
                    waits=[w_free[wi]], inc=[c_gp])
                P.op("gp", lambda e, ex=ex, wi=wi: e.dma_start(
                    out=rwgb[wi][:, :].rearrange("p (l n) -> p l n", l=4),
                    in_=self.rwgT_in[ex].rearrange("(l p) n -> p l n", p=128)),
                    waits=[c_gp], inc=[c_gp])
                P.op("gp", lambda e, ex=ex, wi=wi: e.dma_start(
                    out=rw2b[wi][:, :].rearrange("p (hh n) -> p hh n", hh=8),
                    in_=self.rw2T_in[ex].rearrange("(hh p) n -> p hh n", p=128)),
                    waits=[c_gp], inc=[c_gp])
                return self.c_gp.snap()

            wld = issue_wloads(0)
            for ex in range(NEXP_RUN):
                wi = ex % 2
                if DEBUG and ex == 0:
                    self.dbg_rwge = self.dram_out("dbg_rwge", [128, 4 * HID], BF16)
                    P.op("sp", lambda e: e.dma_start(out=self.dbg_rwge,
                                                     in_=rwgb[0][:, :]),
                         waits=[wld], inc=[c_sp])

                ddw = ex > 0
                for hch in range(8):
                    for tc in range(2):
                        pu = (hch * 2 + tc) % 2
                        for lch in range(4):
                            P.op("te", lambda e, wi=wi, hch=hch, tc=tc,
                                 lch=lch, pu=pu: e.matmul(
                                     ab_ps[pu][:, 0:512],
                                     rw1b[wi][:, lch * HID + hch * 128:
                                              lch * HID + (hch + 1) * 128],
                                     xlT_view[:, lch, tc * 512:(tc + 1) * 512],
                                     start=(lch == 0), stop=(lch == 3)),
                                 waits=[wld, unit_free[pu]], inc=[], dd=ddw)
                            P.op("te", lambda e, wi=wi, hch=hch, tc=tc,
                                 lch=lch, pu=pu: e.matmul(
                                     ab_ps[pu][:, 512:1024],
                                     rwgb[wi][:, lch * HID + hch * 128:
                                              lch * HID + (hch + 1) * 128],
                                     xlT_view[:, lch, tc * 512:(tc + 1) * 512],
                                     start=(lch == 0), stop=(lch == 3)),
                                 waits=[wld, unit_free[pu]],
                                 inc=([c_te] if lch == 3 else []), dd=ddw)
                        abmm = self.c_te.snap()
                        P.op("act", lambda e, pu=pu: e.activation(
                            rbu[pu][:, :], ab_ps[pu][:, 512:1024], AF.Relu),
                            waits=[abmm, rb_free[pu]], inc=[c_act])
                        P.op("dve", lambda e, pu=pu: e.tensor_tensor(
                            rb2u[pu][:, :], rbu[pu][:, :], rbu[pu][:, :],
                            ALU.mult),
                            waits=[c_act], inc=[c_dve])
                        P.op("dve", lambda e, hch=hch, tc=tc, pu=pu:
                             e.tensor_tensor(
                                 abT[:, hch * TOWN + tc * 512:
                                     hch * TOWN + (tc + 1) * 512],
                                 ab_ps[pu][:, 0:512], rb2u[pu][:, :],
                                 ALU.mult),
                             waits=[c_dve], inc=[c_dve])
                        unit_free[pu] = self.c_dve.snap()
                        rb_free[pu] = unit_free[pu]
                ab_done = self.c_dve.snap()

                wld = (issue_wloads(ex + 1) if ex + 1 < NEXP_RUN else None)

                prev_acc = acc[ex % 2]
                next_acc = acc[(ex + 1) % 2]
                for t in range(JT):
                    yi = t % 2
                    for hch in range(8):
                        P.op("te", lambda e, wi=wi, t=t, hch=hch, yi=yi:
                             e.matmul(
                                 y_ps[yi][:, :],
                                 abT[:, hch * TOWN + t * 128:
                                     hch * TOWN + (t + 1) * 128],
                                 rw2b[wi][:, hch * LAT:(hch + 1) * LAT],
                                 start=(hch == 0), stop=(hch == 7)),
                             waits=([ab_done, y_free[yi]]
                                    + ([wld] if t == 0 and hch == 0 else [])),
                             inc=([c_te] if hch == 7 else []),
                             dd=(t >= 2))
                    ymm = self.c_te.snap()
                    P.op("dve", lambda e, ex=ex, t=t, yi=yi,
                         pa_=prev_acc, na=next_acc: e.scalar_tensor_tensor(
                             na[:, t * 512:(t + 1) * 512],
                             y_ps[yi][:, :],
                             self.wdense[:, t * NEXP + ex:t * NEXP + ex + 1],
                             pa_[:, t * 512:(t + 1) * 512],
                             ALU.mult, ALU.add),
                         waits=[ymm, acc_zero], inc=[c_dve])
                    y_free[yi] = self.c_dve.snap()
                w_free[wi] = self.c_dve.snap()
            acc_done = self.c_dve.snap()
            acc_fin = acc[NEXP_RUN % 2]
            if DEBUG:
                self.dbg_acc = self.dram_out("dbg_acc", [128, JT * 512], F32)
                P.op("sp", lambda e: e.dma_start(out=self.dbg_acc,
                                                 in_=acc_fin[:, :]),
                     waits=[acc_done], inc=[c_sp])
                self.dbg_ab = self.dram_out("dbg_ab", [128, 8 * TOWN], BF16)
                P.op("sp", lambda e: e.dma_start(out=self.dbg_ab,
                                                 in_=abT[:, :]),
                     waits=[acc_done], inc=[c_sp])
                self.dbg_xl2 = self.dram_out("dbg_xl2", [128, 4 * TOWN], BF16)
                P.op("sp", lambda e: e.dma_start(out=self.dbg_xl2,
                                                 in_=self.xlT[:, :]),
                     waits=[acc_done], inc=[c_sp])


        with ExitStack() as pb:
            sw2b = self.sb(pb, "sw2b", [128, 8 * E], BF16)
            wupb = self.sb(pb, "wupb", [128, 4 * E], BF16)
            accb = self.sb(pb, "accb", [128, JT * 512], BF16)
            accT = self.sb(pb, "accT", [128, 4 * TOWN], BF16)
            gbuf = self.sb(pb, "gbuf", [128, E], F32)
            stile = self.sb(pb, "stile", [128, E], F32)
            xtile = self.sb(pb, "xtile", [128, E], F32)
            outst = self.sb(pb, "outst", [128, E], F32)
            o4_ps = self.ps(pb, "o4_ps", [128, 2048])
            tp2_ps = self.ps(pb, "tp2_ps", [128, 512], BF16)

            P.op("gp", lambda e: e.dma_start(
                out=sw2b[:, :].rearrange("p (hh n) -> p hh n", hh=8),
                in_=self.sw2T_in.rearrange("(hh p) n -> p hh n", p=128)),
                waits=[acc_done], inc=[c_gp])
            P.op("gp", lambda e: e.dma_start(
                out=wupb[:, :].rearrange("p (l n) -> p l n", l=4),
                in_=self.wupT_in.rearrange("(l p) n -> p l n", p=128)),
                waits=[acc_done], inc=[c_gp])
            wld2 = self.c_gp.snap()

            P.op("dve", lambda e: e.tensor_copy(accb[:, :], acc_fin[:, :]),
                 waits=[acc_done], inc=[c_dve])
            accb_done = self.c_dve.snap()
            tp_free = None
            for t in range(JT):
                for lch in range(4):
                    P.op("te", lambda e, t=t, lch=lch: e.transpose(
                        tp2_ps[:, lch * 128:(lch + 1) * 128],
                        accb[:, t * 512 + lch * 128: t * 512 + (lch + 1) * 128],
                        self.identb[:, :]),
                        waits=[accb_done, tp_free],
                        inc=([c_te] if lch == 3 else []))
                tmm = self.c_te.snap()
                P.op("dve", lambda e, t=t: e.tensor_copy(
                    accT[:, :].rearrange("p (l n) -> p l n", l=4)
                    [:, :, t * 128:(t + 1) * 128],
                    tp2_ps[:, :].rearrange("p (l n) -> p l n", l=4)),
                    waits=[tmm], inc=[c_dve])
                tp_free = self.c_dve.snap()
            accT_done = self.c_dve.snap()

            o4_free = None
            st_free = acc_done
            for t in range(JT):
                P.op("sp", lambda e, t=t: e.dma_start(
                    out=stile[:, :],
                    in_=self.s_hbm[t * 128:(t + 1) * 128, :]),
                    waits=[st_free], inc=[c_sp])
                P.op("sp", lambda e, t=t: e.dma_start(
                    out=xtile[:, :],
                    in_=self.x_own[t * 128:(t + 1) * 128, :]),
                    waits=[st_free], inc=[c_sp])
                sx_in = self.c_sp.snap()
                for nch in range(4):
                    for hch in range(8):
                        P.op("te", lambda e, t=t, nch=nch, hch=hch: e.matmul(
                            o4_ps[:, nch * 512:(nch + 1) * 512],
                            self.hhT[:, hch * TOWN + t * 128:
                                     hch * TOWN + (t + 1) * 128],
                            sw2b[:, hch * E + nch * 512:
                                 hch * E + (nch + 1) * 512],
                            start=(hch == 0), stop=False),
                            waits=[wld2, o4_free], inc=[])
                    for lch in range(4):
                        P.op("te", lambda e, t=t, nch=nch, lch=lch: e.matmul(
                            o4_ps[:, nch * 512:(nch + 1) * 512],
                            accT[:, lch * TOWN + t * 128:
                                 lch * TOWN + (t + 1) * 128],
                            wupb[:, lch * E + nch * 512:
                                 lch * E + (nch + 1) * 512],
                            start=False, stop=(lch == 3)),
                            waits=[accT_done, wld2, o4_free],
                            inc=([c_te] if lch == 3 and nch == 3 else []))
                omm = self.c_te.snap()
                P.op("act", lambda e: e.activation(
                    gbuf[:, :], o4_ps[:, :], AF.Silu),
                    waits=[omm], inc=[c_act])
                o4_free = self.c_act.snap()
                if DEBUG and t == 0:
                    self.dbg_g = self.dram_out("dbg_g", [128, E], F32)
                    P.op("sp", lambda e: e.dma_start(out=self.dbg_g,
                                                     in_=gbuf[:, :]),
                         waits=[o4_free], inc=[c_sp])
                P.op("dve", lambda e: e.tensor_tensor(
                    outst[:, :], stile[:, :], gbuf[:, :], ALU.mult),
                    waits=[c_act, sx_in, st_free], inc=[c_dve])
                P.op("dve", lambda e: e.tensor_tensor(
                    outst[:, :], outst[:, :], xtile[:, :], ALU.add),
                    waits=[c_dve], inc=[c_dve])
                comb = self.c_dve.snap()
                P.op("sp", lambda e, t=t: e.dma_start(
                    out=self.out[t * 128:(t + 1) * 128, :], in_=outst[:, :]),
                    waits=[comb], inc=[c_sp])
                st_free = self.c_sp.snap()

    def finale(self):
        self.P.op("sp", None, waits=[self.c_sp.snap()])



def nc_sem_stack(nc, ctrs):
    class _S:
        def __enter__(self):
            self.es = ExitStack()
            for c in ctrs:
                c.sem = self.es.enter_context(nc.semaphore(c.name))
            return self

        def __exit__(self, *a):
            self.es.close()
            return False
    return _S()


def kernel(**inputs):
    maps = _stage_inputs(inputs)
    b = Builder()
    nc = b.build()
    res = run_bass_kernel_spmd(nc, maps, core_ids=list(range(NCORES)))
    if DEBUG:
        return res
    return _assemble(res.results)
